# version 05
# baseline (speedup 1.0000x reference)
import numpy as np
import ml_dtypes
from contextlib import ExitStack

import concourse.bass as bass
import concourse.mybir as mybir
from concourse.bass_utils import run_bass_kernel_spmd

F32 = mybir.dt.float32
BF16 = mybir.dt.bfloat16
AF = mybir.ActivationFunctionType
ALU = mybir.AluOpType
AX = mybir.AxisListType

D = 1024
S = 2048
NT = S // 128
NC = D // 128
NEG = -1.0e30
BFD = ml_dtypes.bfloat16

ENGINES = ("tensor", "vector", "scalar", "gpsimd", "sync")
ENG_SEM = {"tensor": "PE", "vector": "DVE", "scalar": "ACT",
           "gpsimd": "POOL", "sync": "SP"}
DMA_SEM = {"gpsimd": "QP", "sync": "QS", "scalar": "QA", "vector": "QV"}


class Prog:

    def __init__(self, nc):
        self.nc = nc
        self.items = {e: [] for e in ENGINES}
        self.count = {}
        for sname in list(ENG_SEM.values()) + list(DMA_SEM.values()):
            self.count[sname] = 0
        self.waited = {e: {} for e in ENGINES}
        self.last_w = {}
        self.readers = {}

    def _wait(self, eng, tok):
        if tok is None:
            return
        semname, val = tok
        if eng == "tensor" and semname == "PE":
            return
        if self.waited[eng].get(semname, 0) >= val:
            return
        self.waited[eng][semname] = val
        self.items[eng].append(("wait", semname, val))

    def emit(self, eng, fn, reads=(), writes=(), dma=False):
        for k in reads:
            self._wait(eng, self.last_w.get(k))
        for k in writes:
            self._wait(eng, self.last_w.get(k))
            for t in self.readers.get(k, ()):
                self._wait(eng, t)
        if dma:
            assert writes, "dma emit needs a destination key"
            semname = "q__" + writes[0]
            self.count.setdefault(semname, 0)
        else:
            semname = ENG_SEM[eng]
        inc = 16 if dma else 1
        self.count[semname] += inc
        tok = (semname, self.count[semname])
        self.items[eng].append(("inst", fn, semname, inc))
        for k in writes:
            self.last_w[k] = tok
            self.readers[k] = []
        for k in reads:
            self.readers.setdefault(k, []).append(tok)
        return tok

    def finalize(self):
        nc = self.nc
        final_waits = [(s, v) for s, v in self.count.items() if v > 0]
        with ExitStack() as stack:
            sems = {s: stack.enter_context(nc.semaphore(s))
                    for s in self.count}
            block = stack.enter_context(nc.Block())

            def make_body(eng):
                def body(seq):
                    for it in self.items[eng]:
                        if it[0] == "wait":
                            seq.wait_ge(sems[it[1]], it[2])
                        else:
                            _, fn, semname, inc = it
                            fn(seq, nc).then_inc(sems[semname], inc)
                    if eng == "sync":
                        for sname, v in final_waits:
                            seq.wait_ge(sems[sname], v)
                return body

            for eng in ENGINES:
                getattr(block, eng)(make_body(eng))


def psb(lo, hi):
    return [f"ps:{i}" for i in range(lo // 512, (hi + 511) // 512)]


class Bufs:
    def __init__(self, nc, stack, n_masks):
        def sb(name, shape, dt):
            return stack.enter_context(nc.sbuf_tensor(name, shape, dt))
        self.x_res = sb("x_res", [128, NT, D], F32)
        self.bigA = sb("bigA", [128, 16384], BF16)
        self.bigB = sb("bigB", [128, 16384], BF16)
        self.bigC = sb("bigC", [128, 16384], BF16)
        self.wres = sb("wres", [128, 8, 1024], BF16)
        self.wt0 = sb("wt0", [128, 2048], BF16)
        self.wt1 = sb("wt1", [128, 2048], BF16)
        self.scr32 = sb("scr32", [128, 512], F32)
        self.scrb = sb("scrb", [128, 1024], BF16)
        self.qcol = sb("qcol", [128, 8, 512], BF16)
        self.biasrow = sb("biasrow", [128, 1024], BF16)
        self.lmask = sb("lmask", [128, 64], F32)
        self.gates = sb("gates", [128, NT, 32], F32)
        self.gwb = sb("gwb", [128, 32], BF16)
        self.pt = sb("pt", [128, 2, 512], BF16)
        self.nmpad = sb("nmpad", [128, 8], BF16)
        self.rowbuf = sb("rowbuf", [128, 1024], BF16)
        self.ident = sb("ident", [128, 128], BF16)
        self.ones_bf = sb("ones_bf", [128, 128], BF16)
        self.ones32 = sb("ones32", [128, 128], F32)
        self.stat = sb("stat", [128, 64], F32)
        self.small = sb("small", [128, 40], F32)
        self.bias_fm = sb("bias_fm", [128, 16], F32)
        self.masks = sb("masks", [128, max(n_masks, 1), 128], BF16)
        self.ps = stack.enter_context(nc.psum_tensor("ps", [128, 4096], F32))

    def wslot(self, i):
        t = self.wt0 if i % 2 == 0 else self.wt1
        lo = (i // 2 % 2) * 1024
        return t[:, lo:lo + 1024], f"wts{i % 4}"


def bigA_fm(b, c, t0, t1):
    return b.bigA[:, c * 2048 + t0: c * 2048 + t1]


def load_consts(p, b, dram, n_masks):
    e = p.emit
    e("sync", lambda s, nc: s.dma_start(out=b.ident[:], in_=dram["c_ident"][:]),
      writes=["ident"], dma=True)
    if n_masks:
        e("gpsimd", lambda s, nc: s.dma_start(
            out=b.masks[:, 0:n_masks, :],
            in_=dram["c_masks"][:]), writes=["masks"], dma=True)
    e("sync", lambda s, nc: s.dma_start(out=b.lmask[:], in_=dram["c_lmask"][:]),
      writes=["lmask"], dma=True)
    e("vector", lambda s, nc: s.memset(b.ones_bf[:], 1.0), writes=["ones_bf"])
    e("vector", lambda s, nc: s.memset(b.ones32[:], 1.0), writes=["ones32"])


def load_x(p, b, x_dram):
    for t in range(NT):
        p.emit("sync", lambda s, nc, t=t: s.dma_start(
            out=b.x_res[:, t, :], in_=x_dram[t * 128:(t + 1) * 128, :]),
            writes=["x_res"], dma=True)


def store_x(p, b, out_dram):
    for t in range(NT):
        p.emit("sync", lambda s, nc, t=t: s.dma_start(
            out=out_dram[t * 128:(t + 1) * 128, :], in_=b.x_res[:, t, :]),
            reads=["x_res"], writes=["x_out"], dma=True)


def emit_ln_xhatT(p, b):
    e = p.emit
    st = b.stat
    for t in range(NT):
        e("vector", lambda s, nc, t=t: s.bn_stats(st[:, 0:6],
                                                  b.x_res[:, t, 0:512]),
          reads=["x_res"], writes=["stat"])
        e("vector", lambda s, nc, t=t: s.bn_stats(st[:, 6:12],
                                                  b.x_res[:, t, 512:1024]),
          reads=["x_res"], writes=["stat"])
        e("vector", lambda s, nc: s.bn_aggr(
            st[:, 12:14], st[:, 0:12].rearrange("p (a b) -> p a b", a=2)),
          reads=["stat"], writes=["stat"])
        e("vector", lambda s, nc: s.tensor_scalar(
            st[:, 16:17], st[:, 13:14], 1e-5, None, op0=ALU.add),
          reads=["stat"], writes=["stat"])
        e("scalar", lambda s, nc: s.activation(st[:, 14:15], st[:, 16:17],
                                               AF.Sqrt),
          reads=["stat"], writes=["stat"])
        e("vector", lambda s, nc: s.reciprocal(st[:, 15:16], st[:, 14:15]),
          reads=["stat"], writes=["stat"])
        e("vector", lambda s, nc, t=t: s.tensor_scalar(
            b.scrb[:, 0:1024], b.x_res[:, t, :], st[:, 12:13], st[:, 15:16],
            op0=ALU.subtract, op1=ALU.mult),
          reads=["x_res", "stat"], writes=["scrb"])
        for c in range(NC):
            lo = 2048 + (c % 2) * 512
            e("tensor", lambda s, nc, c=c, lo=lo: s.matmul(
                b.ps[:, lo:lo + 128], b.scrb[:, c * 128:(c + 1) * 128],
                b.ident[:], start=True, stop=True),
              reads=["scrb", "ident"], writes=psb(lo, lo + 128))
            e("scalar", lambda s, nc, c=c, t=t, lo=lo: s.copy(
                bigA_fm(b, c, t * 128, (t + 1) * 128), b.ps[:, lo:lo + 128]),
              reads=psb(lo, lo + 128), writes=["bigA"])


def load_wres(p, b, w_dram, ncol=8, width=1024):
    for dc in range(ncol):
        p.emit("gpsimd", lambda s, nc, dc=dc: s.dma_start(
            out=b.wres[:, dc, 0:width],
            in_=w_dram[dc * 128:(dc + 1) * 128, 0:width]),
            writes=["wres"], dma=True)



class AttnCfg:
    def __init__(self, name, dh, wq, wk, wv, wo, allowed, scale,
                 bq_fm=None, bk_fm=None, bv_row=None, bo_row=None,
                 use_gates=False):
        self.name = name
        self.dh = dh
        self.n_heads = D // dh
        self.wq, self.wk, self.wv, self.wo = wq, wk, wv, wo
        self.allowed = allowed
        self.scale = scale
        self.bq_fm, self.bk_fm = bq_fm, bk_fm
        self.bv_row, self.bo_row = bv_row, bo_row
        self.use_gates = use_gates


def band_blocks(cfg, qt):
    ii = qt * 128 + np.arange(128)[:, None]
    out = []
    for kt in range(NT):
        jj = kt * 128 + np.arange(128)[None, :]
        a = cfg.allowed(ii, jj)
        if not a.any():
            continue
        if a.all():
            out.append((kt, None))
        else:
            out.append((kt, np.where(a, 0.0, NEG).astype(np.float32)))
    return out


def collect_masks(cfgs):
    tiles = []
    index = {}

    def get_id(arr):
        key = arr.tobytes()
        if key not in index:
            index[key] = len(tiles)
            tiles.append(arr)
        return index[key]

    ids = {}
    for cfg in cfgs:
        for qt in range(NT):
            for kt, m in band_blocks(cfg, qt):
                if m is not None:
                    ids[(cfg.name, qt, kt, "tm")] = get_id(m)
                    ids[(cfg.name, qt, kt, "st")] = get_id(
                        np.ascontiguousarray(m.T))
    return tiles, ids


def emit_proj_fm(p, b, out_buf_key, out_view, bias_sl=None):
    e = p.emit
    for oc in range(8):
        for tg in range(4):
            lo = (tg % 2) * 512
            ks = psb(lo, lo + 512)
            for dc in range(NC):
                e("tensor", lambda s, nc, oc=oc, tg=tg, dc=dc, lo=lo: s.matmul(
                    b.ps[:, lo:lo + 512],
                    b.wres[:, dc, oc * 128:(oc + 1) * 128],
                    bigA_fm(b, dc, tg * 512, (tg + 1) * 512),
                    start=(dc == 0), stop=(dc == NC - 1)),
                  reads=["wres", "bigA"], writes=ks)
            if bias_sl is not None:
                e("vector", lambda s, nc, oc=oc, tg=tg, lo=lo: s.tensor_scalar(
                    out_view(oc, tg), b.ps[:, lo:lo + 512],
                    b.bias_fm[:, bias_sl + oc: bias_sl + oc + 1], None,
                    op0=ALU.add),
                  reads=ks + ["bias_fm"], writes=[out_buf_key])
            else:
                e("vector", lambda s, nc, oc=oc, tg=tg, lo=lo: s.tensor_copy(
                    out_view(oc, tg), b.ps[:, lo:lo + 512]),
                  reads=ks, writes=[out_buf_key])


def emit_proj_tm_v(p, b, cfg, dram):
    e = p.emit
    load_wres(p, b, dram[cfg.wv])
    if cfg.bv_row:
        e("gpsimd", lambda s, nc: s.dma_start(
            out=b.biasrow[0:1, 0:D], in_=dram[cfg.bv_row][:]),
          writes=["biasrow"], dma=True)
    for t in range(NT):
        for og in range(2):
            lo = (og % 2) * 512
            ks = psb(lo, lo + 512)
            first = True
            for dc in range(NC):
                for ob in range(4):
                    oc = og * 4 + ob
                    e("tensor", lambda s, nc, t=t, dc=dc, ob=ob, oc=oc, lo=lo,
                        first=first: s.matmul(
                        b.ps[:, lo + ob * 128: lo + (ob + 1) * 128],
                        bigA_fm(b, dc, t * 128, (t + 1) * 128),
                        b.wres[:, dc, oc * 128:(oc + 1) * 128],
                        start=first, stop=False),
                      reads=["bigA", "wres"], writes=ks)
                    first = False
            if cfg.bv_row:
                for ob in range(4):
                    oc = og * 4 + ob
                    e("tensor", lambda s, nc, ob=ob, oc=oc, lo=lo: s.matmul(
                        b.ps[:, lo + ob * 128: lo + (ob + 1) * 128],
                        b.ones_bf[0:1, 0:128],
                        b.biasrow[0:1, oc * 128:(oc + 1) * 128],
                        start=False, stop=(ob == 3)),
                      reads=["ones_bf", "biasrow"], writes=ks)
            e("vector", lambda s, nc, t=t, og=og, lo=lo: s.tensor_copy(
                b.bigC[:, t * 1024 + og * 512: t * 1024 + (og + 1) * 512],
                b.ps[:, lo:lo + 512]),
              reads=ks, writes=["bigC"])


def emit_qcol(p, b, t0, tw, bias_sl=None, scale=1.0):
    e = p.emit
    for oc in range(NC):
        lo = 2048 + (oc % 2) * 512
        ks = psb(lo, lo + tw)
        for dc in range(NC):
            e("tensor", lambda s, nc, oc=oc, dc=dc, lo=lo: s.matmul(
                b.ps[:, lo:lo + tw],
                b.wres[:, dc, oc * 128:(oc + 1) * 128],
                bigA_fm(b, dc, t0, t0 + tw),
                start=(dc == 0), stop=(dc == NC - 1)),
              reads=["wres", "bigA"], writes=ks)
        e("vector", lambda s, nc, oc=oc, lo=lo, bias_sl=bias_sl: s.tensor_scalar(
            b.qcol[:, oc, 0:tw], b.ps[:, lo:lo + tw],
            (b.bias_fm[:, bias_sl + oc:bias_sl + oc + 1]
             if bias_sl is not None else 0.0),
            float(scale), op0=ALU.add, op1=ALU.mult),
          reads=ks + (["bias_fm"] if bias_sl is not None else []),
          writes=["qcol"])


def emit_attention_g4(p, b, cfg, dram, mask_ids):
    e = p.emit
    G = 4

    load_wres(p, b, dram[cfg.wk])
    if cfg.bk_fm:
        e("sync", lambda s, nc: s.dma_start(out=b.bias_fm[:, 8:16],
                                            in_=dram[cfg.bk_fm][:]),
          writes=["bias_fm"], dma=True)
    emit_proj_fm(p, b, "bigB",
                 lambda oc, tg: b.bigB[:, oc * 2048 + tg * 512:
                                       oc * 2048 + (tg + 1) * 512],
                 bias_sl=(8 if cfg.bk_fm else None))
    emit_proj_tm_v(p, b, cfg, dram)
    load_wres(p, b, dram[cfg.wq])
    if cfg.bq_fm:
        e("sync", lambda s, nc: s.dma_start(out=b.bias_fm[:, 0:8],
                                            in_=dram[cfg.bq_fm][:]),
          writes=["bias_fm"], dma=True)

    for g in range(NT // G):
        emit_qcol(p, b, g * 512, 512, bias_sl=(0 if cfg.bq_fm else None),
                  scale=cfg.scale)
        bands = [dict(band_blocks(cfg, g * G + qi)) for qi in range(G)]
        ublocks = sorted(set().union(*[set(d.keys()) for d in bands]))
        for h in range(cfg.n_heads):
            par = h % 2
            hb, hp = 64 * par, h // 2
            sb0 = 16 + par * 24
            kst = f"stat{par}"
            knm = f"nmpad{par}"
            krb = f"rowbuf{par}"
            kpd = "scrb"
            for qi in range(G):
                qt = g * G + qi
                blocks = sorted(bands[qi].keys())
                nbl = len(blocks)
                W = nbl * 128
                base = (qi % 2) * 1024 if W <= 1024 else 0
                i = 0
                while i < nbl:
                    j = i
                    while (j + 1 < nbl and blocks[j + 1] == blocks[j] + 1
                           and (j + 1) // 4 == i // 4):
                        j += 1
                    kt0, width, pos = blocks[i], (j - i + 1) * 128, \
                        base + i * 128
                    e("tensor", lambda s, nc, hb=hb, hp=hp, qi=qi, kt0=kt0,
                        width=width, pos=pos: s.matmul(
                        b.ps[:, pos:pos + width],
                        b.qcol[hb:hb + 64, hp, qi * 128:(qi + 1) * 128],
                        b.bigB[hb:hb + 64, hp * 2048 + kt0 * 128:
                               hp * 2048 + kt0 * 128 + width],
                        start=True, stop=True),
                      reads=["qcol", "bigB"], writes=psb(pos, pos + width))
                    i = j + 1
                for bi, kt in enumerate(blocks):
                    m = bands[qi][kt]
                    if m is None:
                        continue
                    mid = mask_ids[(cfg.name, qt, kt, "tm")]
                    pos = base + bi * 128
                    e("vector", lambda s, nc, mid=mid, pos=pos: s.tensor_tensor(
                        b.ps[:, pos:pos + 128], b.ps[:, pos:pos + 128],
                        b.masks[:, mid, :], op=ALU.add),
                      reads=psb(pos, pos + 128) + ["masks"],
                      writes=psb(pos, pos + 128))
                e("vector", lambda s, nc, W=W, base=base, col=par * 4 + qi:
                    s.tensor_reduce(
                    b.nmpad[:, col:col + 1], b.ps[:, base:base + W], axis=AX.X,
                    op=ALU.max, negate=True),
                  reads=psb(base, base + W), writes=[knm])
                e("vector", lambda s, nc, col=par * 4 + qi, qi=qi, sb0=sb0:
                    s.tensor_copy(b.stat[:, sb0 + qi:sb0 + qi + 1],
                                  b.nmpad[:, col:col + 1]),
                  reads=[knm], writes=[kst])
                nch = (W + 511) // 512
                for c in range(nch):
                    w0, w1 = base + c * 512, base + min(W, (c + 1) * 512)
                    e("scalar", lambda s, nc, w0=w0, w1=w1, c=c, qi=qi,
                        sb0=sb0, par=par: s.activation(
                        b.scrb[:, par * 512:par * 512 + (w1 - w0)],
                        b.ps[:, w0:w1], AF.Exp,
                        bias=b.stat[:, sb0 + qi:sb0 + qi + 1], scale=1.0,
                        accum_out=b.stat[:, sb0 + 4 + qi * 4 + c:
                                         sb0 + 5 + qi * 4 + c]),
                      reads=psb(w0, w1) + [kst], writes=[kpd, kst])
                if nch > 1:
                    e("vector", lambda s, nc, qi=qi, nch=nch, sb0=sb0:
                        s.tensor_reduce(
                        b.stat[:, sb0 + qi:sb0 + qi + 1],
                        b.stat[:, sb0 + 4 + qi * 4:sb0 + 4 + qi * 4 + nch],
                        axis=AX.X, op=ALU.add), reads=[kst], writes=[kst])
                    den_sl = sb0 + qi
                else:
                    den_sl = sb0 + 4 + qi * 4
                e("vector", lambda s, nc, qi=qi, den_sl=den_sl, sb0=sb0:
                    s.reciprocal(
                    b.stat[:, sb0 + 20 + qi:sb0 + 21 + qi],
                    b.stat[:, den_sl:den_sl + 1]),
                  reads=[kst], writes=[kst])
            pvlo = (6 + par) * 512
            for qi in range(G):
                e("tensor", lambda s, nc, qi=qi, col=par * 4 + qi, pvlo=pvlo:
                    s.matmul(b.ps[0:1, pvlo + qi * 128:pvlo + (qi + 1) * 128],
                             b.nmpad[:, col:col + 1],
                             b.ident[:], start=True, stop=True),
                  reads=[knm, "ident"], writes=psb(pvlo, pvlo + 512))
            e("vector", lambda s, nc, pvlo=pvlo, par=par: s.tensor_copy(
                b.rowbuf[0:1, par * 512:(par + 1) * 512],
                b.ps[0:1, pvlo:pvlo + 512]),
              reads=psb(pvlo, pvlo + 512), writes=[krb])
            pv_list = [(kt, qi) for kt in ublocks for qi in range(G)
                       if kt in bands[qi]]
            n_pv = len(pv_list)
            pvk = psb(pvlo, pvlo + 512)
            ipv = 0
            for bi, kt in enumerate(ublocks):
                lo_st = 2048 + (bi % 2) * 512
                ksS = psb(lo_st, lo_st + 512)
                e("tensor", lambda s, nc, hb=hb, hp=hp, kt=kt, lo_st=lo_st:
                    s.matmul(b.ps[:, lo_st:lo_st + 512],
                             b.bigB[hb:hb + 64, hp * 2048 + kt * 128:
                                    hp * 2048 + (kt + 1) * 128],
                             b.qcol[hb:hb + 64, hp, 0:512],
                             start=True, stop=False),
                  reads=["bigB", "qcol"], writes=ksS)
                e("tensor", lambda s, nc, lo_st=lo_st, par=par: s.matmul(
                    b.ps[:, lo_st:lo_st + 512], b.ones_bf[0:1, 0:128],
                    b.rowbuf[0:1, par * 512:(par + 1) * 512],
                    start=False, stop=True),
                  reads=["ones_bf", krb], writes=ksS)
                qi = 0
                while qi < G:
                    if kt in bands[qi]:
                        m = bands[qi][kt]
                        if m is not None:
                            mid = mask_ids[(cfg.name, g * G + qi, kt, "st")]
                            e("vector", lambda s, nc, mid=mid, qi=qi,
                                lo_st=lo_st: s.tensor_tensor(
                                b.ps[:, lo_st + qi * 128:lo_st + (qi + 1) * 128],
                                b.ps[:, lo_st + qi * 128:lo_st + (qi + 1) * 128],
                                b.masks[:, mid, :], op=ALU.add),
                              reads=ksS + ["masks"], writes=ksS)
                        qi += 1
                    else:
                        q1 = qi
                        while q1 < G and kt not in bands[q1]:
                            q1 += 1
                        e("vector", lambda s, nc, qi=qi, q1=q1, lo_st=lo_st:
                            s.memset(b.ps[:, lo_st + qi * 128:
                                          lo_st + q1 * 128], NEG),
                          reads=ksS, writes=ksS)
                        qi = q1
                e("scalar", lambda s, nc, bi=bi, lo_st=lo_st: s.activation(
                    b.pt[:, bi % 2, :], b.ps[:, lo_st:lo_st + 512], AF.Exp),
                  reads=ksS, writes=[f"pt{bi % 2}"])
                for qi in range(G):
                    if kt not in bands[qi]:
                        continue
                    e("tensor", lambda s, nc, bi=bi, kt=kt, qi=qi, h=h,
                        pvlo=pvlo, st=(ipv == 0), sp=(ipv == n_pv - 1):
                        s.matmul(
                        b.ps[:, pvlo + qi * 64:pvlo + (qi + 1) * 64],
                        b.pt[:, bi % 2, qi * 128:(qi + 1) * 128],
                        b.bigC[:, kt * 1024 + h * 64:kt * 1024 + (h + 1) * 64],
                        start=st, stop=sp, skip_group_check=True),
                      reads=[f"pt{bi % 2}", "bigC"], writes=pvk)
                    ipv += 1
            for qi in range(G):
                qt = g * G + qi
                strip = (2 * qi + h // 8) * 2048 + g * 512 + (h % 8) * 64
                if cfg.use_gates:
                    e("vector", lambda s, nc, qi=qi, qt=qt, strip=strip,
                        pvlo=pvlo, sb0=sb0, h=h: s.tensor_scalar(
                        b.bigA[:, strip:strip + 64],
                        b.ps[:, pvlo + qi * 64:pvlo + (qi + 1) * 64],
                        b.stat[:, sb0 + 20 + qi:sb0 + 21 + qi],
                        b.gates[:, qt, 2 * h:2 * h + 1],
                        op0=ALU.mult, op1=ALU.mult),
                      reads=pvk + [kst, "gates"], writes=["bigA"])
                else:
                    e("vector", lambda s, nc, qi=qi, strip=strip,
                        pvlo=pvlo, sb0=sb0: s.tensor_scalar(
                        b.bigA[:, strip:strip + 64],
                        b.ps[:, pvlo + qi * 64:pvlo + (qi + 1) * 64],
                        b.stat[:, sb0 + 20 + qi:sb0 + 21 + qi], None,
                        op0=ALU.mult),
                      reads=pvk + [kst], writes=["bigA"])
        for qi in range(G):
            for c in range(NC):
                src = (2 * qi + c // 4) * 2048 + g * 512 + (c % 4) * 128
                bank = qi * 2 + c // 4
                lo = bank * 512 + (c % 4) * 128
                e("tensor", lambda s, nc, src=src, lo=lo: s.matmul(
                    b.ps[:, lo:lo + 128], b.bigA[:, src:src + 128],
                    b.ident[:], start=True, stop=True),
                  reads=["bigA", "ident"], writes=psb(lo, lo + 128))
        for bank in range(8):
            qi, chalf = bank // 2, bank % 2
            e("vector", lambda s, nc, qi=qi, chalf=chalf, bank=bank, g=g:
                s.tensor_copy(
                b.bigA[:, 0:16384].rearrange("p (c t) -> p c t", c=8)
                [:, chalf * 4:(chalf + 1) * 4,
                 (g * G + qi) * 128:(g * G + qi + 1) * 128],
                b.ps[:, bank * 512:(bank + 1) * 512]
                .rearrange("p (c t) -> p c t", c=4)),
              reads=psb(bank * 512, (bank + 1) * 512), writes=["bigA"])

    load_wres(p, b, dram[cfg.wo])
    if cfg.bo_row:
        e("gpsimd", lambda s, nc: s.dma_start(
            out=b.biasrow[0:1, 0:D], in_=dram[cfg.bo_row][:]),
          writes=["biasrow"], dma=True)
    for qt in range(NT):
        for og in range(2):
            lo = 3072 + og * 512
            ks = psb(lo, lo + 512)
            for dc in range(NC):
                e("tensor", lambda s, nc, qt=qt, og=og, dc=dc, lo=lo: s.matmul(
                    b.ps[:, lo:lo + 512],
                    bigA_fm(b, dc, qt * 128, (qt + 1) * 128),
                    b.wres[:, dc, og * 512:(og + 1) * 512],
                    start=(dc == 0),
                    stop=(dc == NC - 1 and not cfg.bo_row)),
                  reads=["bigA", "wres"], writes=ks)
            if cfg.bo_row:
                e("tensor", lambda s, nc, og=og, lo=lo: s.matmul(
                    b.ps[:, lo:lo + 512], b.ones_bf[0:1, 0:128],
                    b.biasrow[0:1, og * 512:(og + 1) * 512],
                    start=False, stop=True),
                  reads=["ones_bf", "biasrow"], writes=ks)
            e("vector", lambda s, nc, qt=qt, og=og, lo=lo: s.tensor_tensor(
                b.x_res[:, qt, og * 512:(og + 1) * 512], b.ps[:, lo:lo + 512],
                b.x_res[:, qt, og * 512:(og + 1) * 512], op=ALU.add),
              reads=ks + ["x_res"], writes=["x_res"])


def st_runs(blocks):
    runs = []
    i = 0
    nbl = len(blocks)
    while i < nbl:
        j = i
        while (j + 1 < nbl and blocks[j + 1][0] == blocks[j][0] + 1
               and (j + 1) // 4 == i // 4):
            j += 1
        runs.append((i, j))
        i = j + 1
    return runs


def emit_attention(p, b, cfg, dram, mask_ids):
    if cfg.dh == 64:
        emit_attention_g4(p, b, cfg, dram, mask_ids)
    else:
        emit_attention_g1(p, b, cfg, dram, mask_ids)


def emit_attention_g1(p, b, cfg, dram, mask_ids):
    e = p.emit
    dh = cfg.dh

    load_wres(p, b, dram[cfg.wk])
    if cfg.bk_fm:
        e("sync", lambda s, nc: s.dma_start(out=b.bias_fm[:, 8:16],
                                            in_=dram[cfg.bk_fm][:]),
          writes=["bias_fm"], dma=True)
    emit_proj_fm(p, b, "bigB",
                 lambda oc, tg: b.bigB[:, oc * 2048 + tg * 512:
                                       oc * 2048 + (tg + 1) * 512],
                 bias_sl=(8 if cfg.bk_fm else None))
    emit_proj_tm_v(p, b, cfg, dram)

    load_wres(p, b, dram[cfg.wq])
    if cfg.bq_fm:
        e("sync", lambda s, nc: s.dma_start(out=b.bias_fm[:, 0:8],
                                            in_=dram[cfg.bq_fm][:]),
          writes=["bias_fm"], dma=True)

    for qt in range(NT):
        emit_qcol(p, b, qt * 128, 128, bias_sl=(0 if cfg.bq_fm else None),
                  scale=cfg.scale)
        blocks = band_blocks(cfg, qt)
        nbl = len(blocks)
        W = nbl * 128
        for h in range(cfg.n_heads):
            hb = 64 * (h % 2) if dh == 64 else 0
            hp = h // 2
            for (i0, i1) in st_runs(blocks):
                kt0 = blocks[i0][0]
                width = (i1 - i0 + 1) * 128
                pos = i0 * 128
                ks = psb(pos, pos + width)
                if dh == 64:
                    e("tensor", lambda s, nc, hb=hb, hp=hp, kt0=kt0,
                        width=width, pos=pos: s.matmul(
                        b.ps[:, pos:pos + width],
                        b.qcol[hb:hb + 64, hp, 0:128],
                        b.bigB[hb:hb + 64, hp * 2048 + kt0 * 128:
                               hp * 2048 + kt0 * 128 + width],
                        start=True, stop=True),
                      reads=["qcol", "bigB"], writes=ks)
                else:
                    for dc in range(NC):
                        e("tensor", lambda s, nc, dc=dc, kt0=kt0,
                            width=width, pos=pos: s.matmul(
                            b.ps[:, pos:pos + width],
                            b.qcol[:, dc, 0:128],
                            b.bigB[:, dc * 2048 + kt0 * 128:
                                   dc * 2048 + kt0 * 128 + width],
                            start=(dc == 0), stop=(dc == NC - 1)),
                          reads=["qcol", "bigB"], writes=ks)
            for bi, (kt, m) in enumerate(blocks):
                if m is None:
                    continue
                mid = mask_ids[(cfg.name, qt, kt, "tm")]
                pos = bi * 128
                e("vector", lambda s, nc, mid=mid, pos=pos: s.tensor_tensor(
                    b.ps[:, pos:pos + 128], b.ps[:, pos:pos + 128],
                    b.masks[:, mid, :], op=ALU.add),
                  reads=psb(pos, pos + 128) + ["masks"],
                  writes=psb(pos, pos + 128))
            e("vector", lambda s, nc, W=W, hb=hb: s.tensor_reduce(
                b.nmpad[:, hb:hb + 1], b.ps[:, 0:W], axis=AX.X,
                op=ALU.max, negate=True),
              reads=psb(0, W), writes=["nmpad"])
            e("vector", lambda s, nc, hb=hb: s.tensor_copy(
                b.stat[:, 20:21], b.nmpad[:, hb:hb + 1]),
              reads=["nmpad"], writes=["stat"])
            nch = (W + 1023) // 1024
            for c in range(nch):
                w0, w1 = c * 1024, min(W, (c + 1) * 1024)
                e("scalar", lambda s, nc, w0=w0, w1=w1, c=c: s.activation(
                    b.scrb[:, 0:w1 - w0], b.ps[:, w0:w1], AF.Exp,
                    bias=b.stat[:, 20:21], scale=1.0,
                    accum_out=b.stat[:, 24 + c:25 + c]),
                  reads=psb(w0, w1) + ["stat"], writes=["scrb", "stat"])
            if nch > 1:
                e("vector", lambda s, nc, nch=nch: s.tensor_reduce(
                    b.stat[:, 28:29], b.stat[:, 24:24 + nch], axis=AX.X,
                    op=ALU.add), reads=["stat"], writes=["stat"])
            den_sl = 28 if nch > 1 else 24
            e("vector", lambda s, nc, den_sl=den_sl: s.reciprocal(
                b.stat[:, 29:30], b.stat[:, den_sl:den_sl + 1]),
              reads=["stat"], writes=["stat"])
            lo_nm = 2560
            e("tensor", lambda s, nc, lo_nm=lo_nm: s.matmul(
                b.ps[0:8, lo_nm:lo_nm + 128], b.nmpad[:], b.ident[:],
                start=True, stop=True),
              reads=["nmpad", "ident"], writes=psb(lo_nm, lo_nm + 128))
            e("vector", lambda s, nc, hb=hb, lo_nm=lo_nm: s.tensor_copy(
                b.rowbuf[hb:hb + 1, 0:128], b.ps[hb:hb + 1, lo_nm:lo_nm + 128]),
              reads=psb(lo_nm, lo_nm + 128), writes=["rowbuf"])
            for bi, (kt, m) in enumerate(blocks):
                lo_st = 2048 + (bi % 2) * 512
                ksS = psb(lo_st, lo_st + 128)
                if dh == 64:
                    e("tensor", lambda s, nc, hb=hb, hp=hp, kt=kt, lo_st=lo_st:
                        s.matmul(b.ps[:, lo_st:lo_st + 128],
                                 b.bigB[hb:hb + 64,
                                        hp * 2048 + kt * 128:
                                        hp * 2048 + (kt + 1) * 128],
                                 b.qcol[hb:hb + 64, hp, 0:128],
                                 start=True, stop=False),
                      reads=["bigB", "qcol"], writes=ksS)
                else:
                    for dc in range(NC):
                        e("tensor", lambda s, nc, dc=dc, kt=kt, lo_st=lo_st:
                            s.matmul(b.ps[:, lo_st:lo_st + 128],
                                     b.bigB[:, dc * 2048 + kt * 128:
                                            dc * 2048 + (kt + 1) * 128],
                                     b.qcol[:, dc, 0:128],
                                     start=(dc == 0), stop=False),
                          reads=["bigB", "qcol"], writes=ksS)
                e("tensor", lambda s, nc, hb=hb, lo_st=lo_st: s.matmul(
                    b.ps[:, lo_st:lo_st + 128],
                    b.ones_bf[hb:hb + 1, 0:128],
                    b.rowbuf[hb:hb + 1, 0:128],
                    start=False, stop=True),
                  reads=["ones_bf", "rowbuf"], writes=ksS)
                if m is not None:
                    mid = mask_ids[(cfg.name, qt, kt, "st")]
                    e("vector", lambda s, nc, mid=mid, lo_st=lo_st:
                        s.tensor_tensor(b.ps[:, lo_st:lo_st + 128],
                                        b.ps[:, lo_st:lo_st + 128],
                                        b.masks[:, mid, :], op=ALU.add),
                      reads=ksS + ["masks"], writes=ksS)
                e("scalar", lambda s, nc, bi=bi, lo_st=lo_st: s.activation(
                    b.pt[:, bi % 2, 0:128], b.ps[:, lo_st:lo_st + 128], AF.Exp),
                  reads=ksS, writes=[f"pt{bi % 2}"])
                if dh == 64:
                    e("tensor", lambda s, nc, bi=bi, kt=kt, h=h, nbl=nbl:
                        s.matmul(b.ps[:, 3072:3136],
                                 b.pt[:, bi % 2, 0:128],
                                 b.bigC[:, kt * 1024 + h * 64:
                                        kt * 1024 + (h + 1) * 64],
                                 start=(bi == 0), stop=(bi == nbl - 1)),
                      reads=[f"pt{bi % 2}", "bigC"], writes=psb(3072, 3136))
                else:
                    for og in range(2):
                        e("tensor", lambda s, nc, bi=bi, kt=kt, og=og, nbl=nbl:
                            s.matmul(b.ps[:, 3072 + og * 512:
                                          3072 + og * 512 + 512],
                                     b.pt[:, bi % 2, 0:128],
                                     b.bigC[:, kt * 1024 + og * 512:
                                            kt * 1024 + (og + 1) * 512],
                                     start=(bi == 0), stop=(bi == nbl - 1)),
                          reads=[f"pt{bi % 2}", "bigC"],
                          writes=psb(3072 + og * 512, 3072 + og * 512 + 512))
            if dh == 64:
                if cfg.use_gates:
                    e("vector", lambda s, nc, h=h, qt=qt: s.tensor_scalar(
                        b.scrb[:, h * 64:(h + 1) * 64], b.ps[:, 3072:3136],
                        b.stat[:, 29:30], b.gates[:, qt, 2 * h:2 * h + 1],
                        op0=ALU.mult, op1=ALU.mult),
                      reads=psb(3072, 3136) + ["stat", "gates"],
                      writes=["scrb"])
                else:
                    e("vector", lambda s, nc, h=h: s.tensor_scalar(
                        b.scrb[:, h * 64:(h + 1) * 64], b.ps[:, 3072:3136],
                        b.stat[:, 29:30], None, op0=ALU.mult),
                      reads=psb(3072, 3136) + ["stat"], writes=["scrb"])
            else:
                for og in range(2):
                    e("vector", lambda s, nc, og=og: s.tensor_scalar(
                        b.scrb[:, og * 512:(og + 1) * 512],
                        b.ps[:, 3072 + og * 512:3072 + og * 512 + 512],
                        b.stat[:, 29:30], None, op0=ALU.mult),
                      reads=psb(3072 + og * 512, 3072 + og * 512 + 512)
                      + ["stat"], writes=["scrb"])
        for c in range(NC):
            lo = 2048 + (c % 2) * 512
            e("tensor", lambda s, nc, c=c, lo=lo: s.matmul(
                b.ps[:, lo:lo + 128], b.scrb[:, c * 128:(c + 1) * 128],
                b.ident[:], start=True, stop=True),
              reads=["scrb", "ident"], writes=psb(lo, lo + 128))
            e("scalar", lambda s, nc, c=c, qt=qt, lo=lo: s.copy(
                bigA_fm(b, c, qt * 128, (qt + 1) * 128), b.ps[:, lo:lo + 128]),
              reads=psb(lo, lo + 128), writes=["bigA"])

    load_wres(p, b, dram[cfg.wo])
    if cfg.bo_row:
        e("gpsimd", lambda s, nc: s.dma_start(
            out=b.biasrow[0:1, 0:D], in_=dram[cfg.bo_row][:]),
          writes=["biasrow"], dma=True)
    for qt in range(NT):
        for og in range(2):
            lo = 3072 + og * 512
            ks = psb(lo, lo + 512)
            for dc in range(NC):
                e("tensor", lambda s, nc, qt=qt, og=og, dc=dc, lo=lo: s.matmul(
                    b.ps[:, lo:lo + 512],
                    bigA_fm(b, dc, qt * 128, (qt + 1) * 128),
                    b.wres[:, dc, og * 512:(og + 1) * 512],
                    start=(dc == 0),
                    stop=(dc == NC - 1 and not cfg.bo_row)),
                  reads=["bigA", "wres"], writes=ks)
            if cfg.bo_row:
                e("tensor", lambda s, nc, og=og, lo=lo: s.matmul(
                    b.ps[:, lo:lo + 512], b.ones_bf[0:1, 0:128],
                    b.biasrow[0:1, og * 512:(og + 1) * 512],
                    start=False, stop=True),
                  reads=["ones_bf", "biasrow"], writes=ks)
            e("vector", lambda s, nc, qt=qt, og=og, lo=lo: s.tensor_tensor(
                b.x_res[:, qt, og * 512:(og + 1) * 512], b.ps[:, lo:lo + 512],
                b.x_res[:, qt, og * 512:(og + 1) * 512], op=ALU.add),
              reads=ks + ["x_res"], writes=["x_res"])



def emit_gamma(p, b, w_dram, nrc, ncc, thr_sl):
    e = p.emit
    assert nrc * ncc == 64
    gview = b.gates[:, 0:16, 0:32].rearrange("p a c -> p (a c)")
    nch = 0
    for rc in range(nrc):
        for cc in range(ncc):
            st32, k32 = (b.scr32, "scr32") if nch % 2 == 0 else (gview,
                                                                 "gates")
            e("sync", lambda s, nc, rc=rc, cc=cc, st32=st32: s.dma_start(
                out=st32[:, 0:512], in_=w_dram[rc * 128:(rc + 1) * 128,
                                               cc * 512:(cc + 1) * 512]),
              writes=[k32], dma=True)
            e("vector", lambda s, nc, nch=nch, st32=st32: s.tensor_reduce(
                b.stat[:, nch:nch + 1], st32[:, 0:512], axis=AX.X,
                op=ALU.add, apply_absolute_value=True),
              reads=[k32], writes=["stat"])
            nch += 1
    e("vector", lambda s, nc: s.tensor_reduce(
        b.small[:, 34:35], b.stat[:, 0:64], axis=AX.X, op=ALU.add),
      reads=["stat"], writes=["small"])
    e("tensor", lambda s, nc: s.matmul(b.ps[0:1, 2048:2049], b.small[:, 34:35],
                                       b.ones32[0:128, 0:1],
                                       start=True, stop=True),
      reads=["small", "ones32"], writes=psb(2048, 2049))
    n_el = nrc * 128 * ncc * 512
    e("scalar", lambda s, nc, n_el=n_el: s.activation(
        b.small[0:1, 35:36], b.ps[0:1, 2048:2049], AF.Copy,
        bias=0.5e-6, scale=float(0.5 / n_el)),
      reads=psb(2048, 2049), writes=["small"])
    e("tensor", lambda s, nc: s.matmul(b.ps[:, 2048:2049],
                                       b.ones32[0:1, 0:128],
                                       b.small[0:1, 35:36],
                                       start=True, stop=True),
      reads=["ones32", "small"], writes=psb(2048, 2049))
    e("vector", lambda s, nc, thr_sl=thr_sl: s.tensor_copy(
        b.small[:, thr_sl:thr_sl + 1], b.ps[:, 2048:2049]),
      reads=psb(2048, 2176), writes=["small"])
    e("vector", lambda s, nc, thr_sl=thr_sl: s.tensor_scalar(
        b.small[:, thr_sl + 1:thr_sl + 2], b.small[:, thr_sl:thr_sl + 1],
        -1.0, None, op0=ALU.mult),
      reads=["small"], writes=["small"])


def emit_l0f_prep(p, b, dram, scr):
    e = p.emit
    emit_gamma(p, b, dram["L0f_wfcT"], 8, 8, 36)
    emit_gamma(p, b, dram["L0f_wprT"], 32, 2, 38)
    e("sync", lambda s, nc: s.dma_start(out=b.small[:, 0:32],
                                        in_=dram["L0f_lnw"][:]),
      writes=["small"], dma=True)
    e("gpsimd", lambda s, nc: s.dma_start(out=b.gwb[:, 0:32],
                                          in_=dram["L0f_lnb_fm"][:]),
      writes=["gwb"], dma=True)
    gview = b.gates[:, 0:16, 0:32].rearrange("p a c -> p (a c)")
    for rc in range(8):
        for cc in range(8):
            par = (rc * 8 + cc) % 2
            st32, k32 = (b.scr32, "scr32") if par == 0 else (gview, "gates")
            sa, ka = b.wslot(2 * par)
            sb_, kb = b.wslot(2 * par + 1)
            e("sync", lambda s, nc, rc=rc, cc=cc, st32=st32: s.dma_start(
                out=st32[:, 0:512],
                in_=dram["L0f_wfcT"][rc * 128:(rc + 1) * 128,
                                     cc * 512:(cc + 1) * 512]),
              writes=[k32], dma=True)
            e("vector", lambda s, nc, st32=st32, sa=sa: s.tensor_scalar(
                sa[:, 0:512], st32[:, 0:512], b.small[:, 36:37], None,
                op0=ALU.is_gt), reads=[k32, "small"], writes=[ka])
            e("vector", lambda s, nc, st32=st32, sa=sa: s.tensor_scalar(
                sa[:, 512:1024], st32[:, 0:512], b.small[:, 37:38], None,
                op0=ALU.is_lt), reads=[k32, "small"], writes=[ka])
            e("vector", lambda s, nc, sa=sa, sb_=sb_: s.tensor_tensor(
                sb_[:, 0:512], sa[:, 0:512], sa[:, 512:1024],
                op=ALU.subtract), reads=[ka], writes=[kb])
            e("sync", lambda s, nc, rc=rc, cc=cc, sb_=sb_: s.dma_start(
                out=scr["fcq"][rc * 128:(rc + 1) * 128,
                               cc * 512:(cc + 1) * 512],
                in_=sb_[:, 0:512]),
              reads=[kb], writes=["fcq"], dma=True)
    for rc in range(32):
        par = rc % 2
        sa, ka = b.wslot(2 * par)
        sb_, kb = b.wslot(2 * par + 1)
        for hf in range(2):
            st32, k32 = (b.scr32, "scr32") if hf == 0 else (gview, "gates")
            e("sync", lambda s, nc, rc=rc, hf=hf, st32=st32: s.dma_start(
                out=st32[:, 0:512],
                in_=dram["L0f_wprT"][rc * 128:(rc + 1) * 128,
                                     hf * 512:(hf + 1) * 512]),
              writes=[k32], dma=True)
            e("vector", lambda s, nc, st32=st32, sa=sa: s.tensor_scalar(
                sa[:, 0:512], st32[:, 0:512], b.small[:, 38:39], None,
                op0=ALU.is_gt), reads=[k32, "small"], writes=[ka])
            e("vector", lambda s, nc, st32=st32, sa=sa: s.tensor_scalar(
                sa[:, 512:1024], st32[:, 0:512], b.small[:, 39:40], None,
                op0=ALU.is_lt), reads=[k32, "small"], writes=[ka])
            e("vector", lambda s, nc, hf=hf, sa=sa, sb_=sb_: s.tensor_tensor(
                sb_[:, hf * 512:(hf + 1) * 512], sa[:, 0:512],
                sa[:, 512:1024],
                op=ALU.subtract), reads=[ka], writes=[kb])
        for half in range(2):
            e("tensor", lambda s, nc, rc=rc, half=half, sb_=sb_: s.matmul(
                b.ps[0:1, 3072 + half * 512: 3072 + (half + 1) * 512],
                b.gwb[:, rc:rc + 1],
                sb_[:, half * 512:(half + 1) * 512],
                start=(rc == 0), stop=(rc == 31)),
              reads=["gwb", kb],
              writes=psb(3072 + half * 512, 3072 + (half + 1) * 512))
        e("vector", lambda s, nc, rc=rc, sa=sa, sb_=sb_: s.tensor_scalar(
            sa[:, 0:1024], sb_[:, 0:1024], b.small[:, rc:rc + 1],
            None, op0=ALU.mult),
          reads=[kb, "small"], writes=[ka])
        e("sync", lambda s, nc, rc=rc, sa=sa: s.dma_start(
            out=scr["prq"][rc * 128:(rc + 1) * 128, :], in_=sa[:, 0:1024]),
          reads=[ka], writes=["prq"], dma=True)
    e("vector", lambda s, nc: s.tensor_copy(b.scrb[0:1, 0:512],
                                            b.ps[0:1, 3072:3584]),
      reads=psb(3072, 3584), writes=["scrb"])
    e("vector", lambda s, nc: s.tensor_copy(b.scrb[0:1, 512:1024],
                                            b.ps[0:1, 3584:4096]),
      reads=psb(3584, 4096), writes=["scrb"])
    e("sync", lambda s, nc: s.dma_start(out=scr["l0f_vec"][0:1, :],
                                        in_=b.scrb[0:1, 0:1024]),
      reads=["scrb"], writes=["l0f_vec"], dma=True)


def emit_l0_ffn(p, b, scr):
    e = p.emit
    st = b.stat
    e("sync", lambda s, nc: s.dma_start(out=b.biasrow[0:1, 0:1024],
                                        in_=scr["l0f_vec"][0:1, :]),
      reads=["l0f_vec"], writes=["biasrow"], dma=True)
    for tg in range(4):
        for hg in range(8):
            for dc in range(NC):
                e("sync", lambda s, nc, dc=dc, hg=hg: s.dma_start(
                    out=b.wres[:, dc, 0:512],
                    in_=scr["fcq"][dc * 128:(dc + 1) * 128,
                                   hg * 512:(hg + 1) * 512]),
                  reads=["fcq"], writes=["wres"], dma=True)
            for ti in range(4):
                t = tg * 4 + ti
                lo = (ti % 2) * 512
                ks = psb(lo, lo + 512)
                for dc in range(NC):
                    e("tensor", lambda s, nc, dc=dc, t=t, lo=lo: s.matmul(
                        b.ps[:, lo:lo + 512],
                        bigA_fm(b, dc, t * 128, (t + 1) * 128),
                        b.wres[:, dc, 0:512],
                        start=(dc == 0), stop=(dc == NC - 1)),
                      reads=["bigA", "wres"], writes=ks)
                e("scalar", lambda s, nc, ti=ti, hg=hg, lo=lo: s.activation(
                    b.bigB[:, ti * 4096 + hg * 512: ti * 4096 + (hg + 1) * 512],
                    b.ps[:, lo:lo + 512], AF.Gelu),
                  reads=ks, writes=["bigB"])
        for ti in range(4):
            for i8 in range(8):
                e("vector", lambda s, nc, ti=ti, i8=i8: s.bn_stats(
                    st[:, 6 * i8:6 * i8 + 6],
                    b.bigB[:, ti * 4096 + i8 * 512: ti * 4096 + (i8 + 1) * 512]),
                  reads=["bigB"], writes=["stat"])
            e("vector", lambda s, nc: s.bn_aggr(
                st[:, 48:50], st[:, 0:48].rearrange("p (a b) -> p a b", a=8)),
              reads=["stat"], writes=["stat"])
            e("vector", lambda s, nc: s.tensor_scalar(
                st[:, 50:51], st[:, 49:50], 1e-5, None, op0=ALU.add),
              reads=["stat"], writes=["stat"])
            e("scalar", lambda s, nc: s.activation(st[:, 51:52], st[:, 50:51],
                                                   AF.Sqrt),
              reads=["stat"], writes=["stat"])
            e("vector", lambda s, nc: s.reciprocal(st[:, 52:53], st[:, 51:52]),
              reads=["stat"], writes=["stat"])
            for cc in range(4):
                e("vector", lambda s, nc, ti=ti, cc=cc: s.tensor_scalar(
                    b.scrb[:, 0:1024],
                    b.bigB[:, ti * 4096 + cc * 1024: ti * 4096 + (cc + 1) * 1024],
                    st[:, 48:49], st[:, 52:53],
                    op0=ALU.subtract, op1=ALU.mult),
                  reads=["bigB", "stat"], writes=["scrb"])
                for ci in range(8):
                    hc = cc * 8 + ci
                    lo = 2048 + (ci % 2) * 512
                    e("tensor", lambda s, nc, ci=ci, lo=lo: s.matmul(
                        b.ps[:, lo:lo + 128],
                        b.scrb[:, ci * 128:(ci + 1) * 128], b.ident[:],
                        start=True, stop=True),
                      reads=["scrb", "ident"], writes=psb(lo, lo + 128))
                    e("scalar", lambda s, nc, hc=hc, ti=ti, lo=lo: s.copy(
                        b.bigC[:, hc * 512 + ti * 128: hc * 512 + (ti + 1) * 128],
                        b.ps[:, lo:lo + 128]),
                      reads=psb(lo, lo + 128), writes=["bigC"])
        for dg in range(2):
            for hc in range(32):
                wslot, wkey = b.wslot(dg * 32 + hc)
                e("sync", lambda s, nc, hc=hc, dg=dg, wslot=wslot: s.dma_start(
                    out=wslot[:, 0:512],
                    in_=scr["prq"][hc * 128:(hc + 1) * 128,
                                   dg * 512:(dg + 1) * 512]),
                  reads=["prq"], writes=[wkey], dma=True)
                for ti in range(4):
                    bank = dg * 4 + ti
                    e("tensor", lambda s, nc, hc=hc, ti=ti, bank=bank,
                        wslot=wslot, wkey=wkey: s.matmul(
                        b.ps[:, bank * 512:(bank + 1) * 512],
                        b.bigC[:, hc * 512 + ti * 128:
                               hc * 512 + (ti + 1) * 128],
                        wslot[:, 0:512],
                        start=(hc == 0), stop=False),
                      reads=["bigC", wkey],
                      writes=psb(bank * 512, (bank + 1) * 512))
            for ti in range(4):
                bank = dg * 4 + ti
                e("tensor", lambda s, nc, dg=dg, bank=bank: s.matmul(
                    b.ps[:, bank * 512:(bank + 1) * 512],
                    b.ones_bf[0:1, 0:128],
                    b.biasrow[0:1, dg * 512:(dg + 1) * 512],
                    start=False, stop=True),
                  reads=["ones_bf", "biasrow"],
                  writes=psb(bank * 512, (bank + 1) * 512))
                t = tg * 4 + ti
                e("vector", lambda s, nc, t=t, dg=dg, bank=bank: s.tensor_tensor(
                    b.x_res[:, t, dg * 512:(dg + 1) * 512],
                    b.ps[:, bank * 512:(bank + 1) * 512],
                    b.x_res[:, t, dg * 512:(dg + 1) * 512], op=ALU.add),
                  reads=psb(bank * 512, (bank + 1) * 512) + ["x_res"],
                  writes=["x_res"])



def emit_l1f_prep(p, b, dram, scr):
    e = p.emit
    jobs = [("L1f_w1T", "w1q", 8, 4096), ("L1f_w2T", "w2q", 8, 4096),
            ("L1f_h1T", "h1q", 32, 4096), ("L1f_h2T", "h2q", 32, 4096),
            ("L1f_w3T", "w3q", 32, 1024)]
    gview = b.gates[:, 0:16, 0:32].rearrange("p a c -> p (a c)")
    n = 0
    for src, dst, nrc, width in jobs:
        for rc in range(nrc):
            for cc in range(width // 512):
                st32, k32 = (b.scr32, "scr32") if n % 2 == 0 else (gview,
                                                                   "gates")
                slot, key = b.wslot(n)
                n += 1
                e("sync", lambda s, nc, src=src, rc=rc, cc=cc,
                    st32=st32: s.dma_start(
                    out=st32[:, 0:512],
                    in_=dram[src][rc * 128:(rc + 1) * 128,
                                  cc * 512:cc * 512 + 512]),
                  writes=[k32], dma=True)
                e("gpsimd", lambda s, nc, st32=st32, slot=slot: s.tensor_copy(
                    slot[:, 0:512], st32[:, 0:512]),
                  reads=[k32], writes=[key])
                e("sync", lambda s, nc, dst=dst, rc=rc, cc=cc,
                    slot=slot: s.dma_start(
                    out=scr[dst][rc * 128:(rc + 1) * 128,
                                 cc * 512:cc * 512 + 512],
                    in_=slot[:, 0:512]),
                  reads=[key], writes=[dst], dma=True)


def emit_l1_ffn(p, b, dram, scr):
    e = p.emit
    for i, k in enumerate(["L1f_s1", "L1f_s2", "L1f_sh1", "L1f_sh2"]):
        e("sync", lambda s, nc, i=i, k=k: s.dma_start(
            out=b.scr32[:, 32 * i:32 * (i + 1)], in_=dram[k][:]),
          writes=["scr32"], dma=True)
    e("sync", lambda s, nc: s.dma_start(out=b.small[:, 0:8],
                                        in_=dram["L1f_s3fm"][:]),
      writes=["small"], dma=True)

    def mm_block(scrname, in_key, in_view, out_cb, s_off, n_ic, tg,
                 nog=4, width_o=4096):
        for ocg in range(nog):
            for ic in range(n_ic):
                slot, key = b.wslot(ocg * n_ic + ic)
                e("sync", lambda s, nc, scrname=scrname, ic=ic, ocg=ocg,
                    slot=slot: s.dma_start(
                    out=slot[:, 0:1024],
                    in_=scr[scrname][ic * 128:(ic + 1) * 128,
                                     ocg * 1024:(ocg + 1) * 1024]),
                  reads=[scrname], writes=[key], dma=True)
                for ob in range(8):
                    e("tensor", lambda s, nc, ic=ic, ob=ob, slot=slot,
                        key=key: s.matmul(
                        b.ps[:, ob * 512:(ob + 1) * 512],
                        slot[:, ob * 128:(ob + 1) * 128],
                        in_view(ic),
                        start=(ic == 0), stop=(ic == n_ic - 1)),
                      reads=[key, in_key],
                      writes=psb(ob * 512, (ob + 1) * 512))
            for ob in range(8):
                out_cb(ocg * 8 + ob, ob)

    for tg in range(4):
        def evac_silu(oc, bank):
            e("scalar", lambda s, nc, oc=oc, bank=bank: s.activation(
                b.bigB[:, oc * 512:(oc + 1) * 512],
                b.ps[:, bank * 512:(bank + 1) * 512], AF.Silu,
                scale=b.scr32[:, oc:oc + 1]),
              reads=psb(bank * 512, (bank + 1) * 512) + ["scr32"],
              writes=["bigB"])
        mm_block("w1q", "bigA",
                 lambda ic, tg=tg: bigA_fm(b, ic, tg * 512, (tg + 1) * 512),
                 evac_silu, 0, NC, tg)
        def evac_h(oc, bank):
            e("vector", lambda s, nc, oc=oc, bank=bank: s.scalar_tensor_tensor(
                b.bigC[:, oc * 512:(oc + 1) * 512],
                b.ps[:, bank * 512:(bank + 1) * 512],
                b.scr32[:, 32 + oc:33 + oc],
                b.bigB[:, oc * 512:(oc + 1) * 512],
                op0=ALU.mult, op1=ALU.mult),
              reads=psb(bank * 512, (bank + 1) * 512) + ["scr32", "bigB"],
              writes=["bigC"])
        mm_block("w2q", "bigA",
                 lambda ic, tg=tg: bigA_fm(b, ic, tg * 512, (tg + 1) * 512),
                 evac_h, 32, NC, tg)
        def evac_g1(oc, bank):
            e("scalar", lambda s, nc, oc=oc, bank=bank: s.activation(
                b.bigB[:, oc * 512:(oc + 1) * 512],
                b.ps[:, bank * 512:(bank + 1) * 512], AF.Gelu,
                scale=b.scr32[:, 64 + oc:65 + oc]),
              reads=psb(bank * 512, (bank + 1) * 512) + ["scr32"],
              writes=["bigB"])
        mm_block("h1q", "bigC",
                 lambda ic: b.bigC[:, ic * 512:(ic + 1) * 512],
                 evac_g1, 64, 32, tg)
        def evac_g2(oc, bank):
            e("scalar", lambda s, nc, oc=oc, bank=bank: s.activation(
                b.bigC[:, oc * 512:(oc + 1) * 512],
                b.ps[:, bank * 512:(bank + 1) * 512], AF.Gelu,
                scale=b.scr32[:, 96 + oc:97 + oc]),
              reads=psb(bank * 512, (bank + 1) * 512) + ["scr32"],
              writes=["bigC"])
        mm_block("h2q", "bigB",
                 lambda ic: b.bigB[:, ic * 512:(ic + 1) * 512],
                 evac_g2, 96, 32, tg)
        def evac_w3(dcb, bank):
            e("vector", lambda s, nc, dcb=dcb, bank=bank: s.tensor_scalar(
                b.scrb[:, 0:512], b.ps[:, bank * 512:(bank + 1) * 512],
                b.small[:, dcb:dcb + 1], None, op0=ALU.mult),
              reads=psb(bank * 512, (bank + 1) * 512) + ["small"],
              writes=["scrb"])
            for ti in range(4):
                t = tg * 4 + ti
                lo = bank * 512
                e("tensor", lambda s, nc, ti=ti, lo=lo: s.matmul(
                    b.ps[:, lo:lo + 128],
                    b.scrb[:, ti * 128:(ti + 1) * 128], b.ident[:],
                    start=True, stop=True),
                  reads=["scrb", "ident"], writes=psb(lo, lo + 128))
                e("vector", lambda s, nc, t=t, dcb=dcb, lo=lo: s.tensor_tensor(
                    b.x_res[:, t, dcb * 128:(dcb + 1) * 128],
                    b.ps[:, lo:lo + 128],
                    b.x_res[:, t, dcb * 128:(dcb + 1) * 128], op=ALU.add),
                  reads=psb(lo, lo + 128) + ["x_res"], writes=["x_res"])
        mm_block("w3q", "bigC",
                 lambda ic: b.bigC[:, ic * 512:(ic + 1) * 512],
                 evac_w3, 0, 32, tg, nog=1, width_o=1024)



def emit_l1b(p, b, dram, mask_ids):
    e = p.emit
    emit_ln_xhatT(p, b)
    load_wres(p, b, dram["L1b_wq"])
    e("gpsimd", lambda s, nc: s.dma_start(out=b.gwb[:, 0:32],
                                          in_=dram["L1b_gw"][:]),
      writes=["gwb"], dma=True)
    for qt in range(NT):
        emit_qcol(p, b, qt * 128, 128, bias_sl=None, scale=1.0)
        for h in range(16):
            hb, hp = 64 * (h % 2), h // 2
            for gi in range(2):
                e("tensor", lambda s, nc, h=h, hb=hb, hp=hp, gi=gi: s.matmul(
                    b.ps[:, 2048 + h * 2 + gi: 2048 + h * 2 + gi + 1],
                    b.qcol[hb:hb + 64, hp, 0:128],
                    b.gwb[hb:hb + 64, 16 * gi + h:16 * gi + h + 1],
                    start=True, stop=True),
                  reads=["qcol", "gwb"], writes=psb(2048, 2560))
        e("scalar", lambda s, nc, qt=qt: s.activation(
            b.gates[:, qt, :], b.ps[:, 2048:2080], AF.Sigmoid),
          reads=psb(2048, 2560), writes=["gates"])
    load_wres(p, b, dram["L1b_wgq"], width=512)
    for oc in range(4):
        for tg in range(4):
            lo = (tg % 2) * 512
            ks = psb(lo, lo + 512)
            for dc in range(NC):
                e("tensor", lambda s, nc, oc=oc, tg=tg, dc=dc, lo=lo: s.matmul(
                    b.ps[:, lo:lo + 512],
                    b.wres[:, dc, oc * 128:(oc + 1) * 128],
                    bigA_fm(b, dc, tg * 512, (tg + 1) * 512),
                    start=(dc == 0), stop=(dc == NC - 1)),
                  reads=["wres", "bigA"], writes=ks)
            e("vector", lambda s, nc, oc=oc, tg=tg, lo=lo: s.tensor_scalar(
                b.bigB[:, oc * 2048 + tg * 512: oc * 2048 + (tg + 1) * 512],
                b.ps[:, lo:lo + 512], 0.125, None, op0=ALU.mult),
              reads=psb(lo, lo + 512), writes=["bigB"])
    for which, wname, dst0 in (("gk", "L1b_wgk", 0), ("gv", "L1b_wgv", 256)):
        load_wres(p, b, dram[wname], width=512)
        for oc in range(4):
            for tg in range(4):
                lo = (tg % 2) * 512
                ks = psb(lo, lo + 512)
                for dc in range(NC):
                    e("tensor", lambda s, nc, oc=oc, tg=tg, dc=dc, lo=lo:
                        s.matmul(b.ps[:, lo:lo + 512],
                                 b.wres[:, dc, oc * 128:(oc + 1) * 128],
                                 bigA_fm(b, dc, tg * 512, (tg + 1) * 512),
                                 start=(dc == 0), stop=(dc == NC - 1)),
                      reads=["wres", "bigA"], writes=ks)
                e("vector", lambda s, nc, oc=oc, tg=tg, lo=lo: s.tensor_copy(
                    b.bigC[:, oc * 2048 + tg * 512:
                           oc * 2048 + (tg + 1) * 512],
                    b.ps[:, lo:lo + 512]),
                  reads=psb(lo, lo + 512), writes=["bigC"])
        for oc in range(4):
            e("vector", lambda s, nc, oc=oc, dst0=dst0: s.tensor_reduce(
                b.scr32[:, dst0 + oc * 64: dst0 + (oc + 1) * 64],
                b.bigC[:, oc * 2048:(oc + 1) * 2048]
                .rearrange("p (n w) -> p n w", w=32),
                axis=AX.X, op=ALU.add),
              reads=["bigC"], writes=["scr32"])
        e("vector", lambda s, nc, dst0=dst0: s.tensor_scalar(
            b.scrb[:, dst0:dst0 + 256], b.scr32[:, dst0:dst0 + 256],
            1.0 / 32.0, None, op0=ALU.mult),
          reads=["scr32"], writes=["scrb"])
    for lh in range(8):
        lhb, lhp = 64 * (lh % 2), lh // 2
        e("tensor", lambda s, nc, lhb=lhb, lhp=lhp: s.matmul(
            b.ps[0:64, 2560:2624],
            b.scrb[lhb:lhb + 64, 256 + lhp * 64: 256 + (lhp + 1) * 64],
            b.ident[lhb:lhb + 64, lhb:lhb + 64],
            start=True, stop=True),
          reads=["scrb", "ident"], writes=psb(2560, 2688))
        e("vector", lambda s, nc: s.tensor_copy(b.pt[0:64, 1, 0:64],
                                                b.ps[0:64, 2560:2624]),
          reads=psb(2560, 2688), writes=["pt1"])
        for stile in range(NT):
            e("tensor", lambda s, nc, lhb=lhb, lhp=lhp, stile=stile: s.matmul(
                b.ps[:, 2048:2112],
                b.bigB[lhb:lhb + 64, lhp * 2048 + stile * 128:
                       lhp * 2048 + (stile + 1) * 128],
                b.scrb[lhb:lhb + 64, lhp * 64:(lhp + 1) * 64],
                start=True, stop=True),
              reads=["bigB", "scrb"], writes=psb(2048, 2112))
            if 4 * stile < 64:
                e("vector", lambda s, nc, stile=stile: s.tensor_tensor(
                    b.ps[:, 2048 + 4 * stile:2112],
                    b.ps[:, 2048 + 4 * stile:2112],
                    b.lmask[:, 0:64 - 4 * stile], op=ALU.add),
                  reads=psb(2048, 2112) + ["lmask"], writes=psb(2048, 2112))
            e("vector", lambda s, nc: s.tensor_reduce(
                b.stat[:, 20:21], b.ps[:, 2048:2112], axis=AX.X,
                op=ALU.max, negate=True),
              reads=psb(2048, 2112), writes=["stat"])
            e("scalar", lambda s, nc: s.activation(
                b.scrb[:, 512:576], b.ps[:, 2048:2112], AF.Exp,
                bias=b.stat[:, 20:21], scale=1.0,
                accum_out=b.stat[:, 24:25]),
              reads=psb(2048, 2112) + ["stat"], writes=["scrb", "stat"])
            e("vector", lambda s, nc: s.reciprocal(b.stat[:, 29:30],
                                                   b.stat[:, 24:25]),
              reads=["stat"], writes=["stat"])
            e("tensor", lambda s, nc: s.matmul(
                b.ps[0:64, 2560:2688], b.scrb[:, 512:576], b.ident[:],
                start=True, stop=True),
              reads=["scrb", "ident"], writes=psb(2560, 2688))
            e("vector", lambda s, nc: s.tensor_copy(b.pt[0:64, 0, 0:128],
                                                    b.ps[0:64, 2560:2688]),
              reads=psb(2560, 2688), writes=["pt0"])
            e("tensor", lambda s, nc: s.matmul(
                b.ps[:, 3072:3136], b.pt[0:64, 0, 0:128], b.pt[0:64, 1, 0:64],
                start=True, stop=True),
              reads=["pt0", "pt1"], writes=psb(3072, 3136))
            e("vector", lambda s, nc, lh=lh, stile=stile: s.tensor_scalar(
                b.bigB[:, 8192 + stile * 512 + lh * 64:
                       8192 + stile * 512 + (lh + 1) * 64],
                b.ps[:, 3072:3136], b.stat[:, 29:30], None, op0=ALU.mult),
              reads=psb(3072, 3136) + ["stat"], writes=["bigB"])
    load_wres(p, b, dram["L1b_rout"], ncol=4)
    for stile in range(NT):
        for lc in range(4):
            lo = 2048 + (lc % 2) * 512
            e("tensor", lambda s, nc, stile=stile, lc=lc, lo=lo: s.matmul(
                b.ps[:, lo:lo + 128],
                b.bigB[:, 8192 + stile * 512 + lc * 128:
                       8192 + stile * 512 + (lc + 1) * 128],
                b.ident[:], start=True, stop=True),
              reads=["bigB", "ident"], writes=psb(lo, lo + 128))
            e("scalar", lambda s, nc, lc=lc, lo=lo: s.copy(
                b.qcol[:, lc, 0:128], b.ps[:, lo:lo + 128]),
              reads=psb(lo, lo + 128), writes=["qcol"])
        for og in range(2):
            lo = 3072 + og * 512
            ks = psb(lo, lo + 512)
            for lc in range(4):
                e("tensor", lambda s, nc, lc=lc, og=og, lo=lo: s.matmul(
                    b.ps[:, lo:lo + 512], b.qcol[:, lc, 0:128],
                    b.wres[:, lc, og * 512:(og + 1) * 512],
                    start=(lc == 0), stop=(lc == 3)),
                  reads=["qcol", "wres"], writes=ks)
            for hh in range(8):
                h = og * 8 + hh
                e("vector", lambda s, nc, stile=stile, h=h, hh=hh, lo=lo:
                    s.tensor_scalar(
                    b.bigC[:, stile * 1024 + h * 64: stile * 1024 + (h + 1) * 64],
                    b.ps[:, lo + hh * 64: lo + (hh + 1) * 64],
                    b.gates[:, stile, 2 * h + 1:2 * h + 2], None, op0=ALU.mult),
                  reads=ks + ["gates"], writes=["bigC"])
    load_wres(p, b, dram["L1b_wout"])
    for qt in range(NT):
        for c in range(NC):
            lo = 2048 + (c % 2) * 512
            e("tensor", lambda s, nc, qt=qt, c=c, lo=lo: s.matmul(
                b.ps[:, lo:lo + 128],
                b.bigC[:, qt * 1024 + c * 128: qt * 1024 + (c + 1) * 128],
                b.ident[:], start=True, stop=True),
              reads=["bigC", "ident"], writes=psb(lo, lo + 128))
            e("scalar", lambda s, nc, c=c, lo=lo: s.copy(
                b.qcol[:, c, 0:128], b.ps[:, lo:lo + 128]),
              reads=psb(lo, lo + 128), writes=["qcol"])
        for og in range(2):
            lo = 3072 + og * 512
            ks = psb(lo, lo + 512)
            for dc in range(NC):
                e("tensor", lambda s, nc, dc=dc, og=og, lo=lo: s.matmul(
                    b.ps[:, lo:lo + 512], b.qcol[:, dc, 0:128],
                    b.wres[:, dc, og * 512:(og + 1) * 512],
                    start=(dc == 0), stop=(dc == NC - 1)),
                  reads=["qcol", "wres"], writes=ks)
            e("vector", lambda s, nc, qt=qt, og=og, lo=lo: s.tensor_tensor(
                b.x_res[:, qt, og * 512:(og + 1) * 512], b.ps[:, lo:lo + 512],
                b.x_res[:, qt, og * 512:(og + 1) * 512], op=ALU.add),
              reads=ks + ["x_res"], writes=["x_res"])
    emit_attention(p, b, CFG_L1B, dram, mask_ids)



def _caus(ii, jj):
    return jj <= ii


def _longf(ii, jj):
    return (np.abs(ii - jj) <= 256) | (ii == 0) | (jj == 0)


def _local(ii, jj):
    d = ii - jj
    return (d >= 0) & (d < 256)


CFG_L0A = AttnCfg("l0a", 64, "L0a_wqT", "L0a_wkT", "L0a_wvT", "L0a_woT",
                  _caus, 0.125, bq_fm="L0a_bq", bk_fm="L0a_bk",
                  bv_row="L0a_bv", bo_row="L0a_bo")
CFG_L0B = AttnCfg("l0b", 1024, "L0b_wq", "L0b_wk", "L0b_wv", "L0b_wp",
                  _caus, 1.0 / 32.0, bq_fm="L0b_bq", bk_fm="L0b_bk",
                  bv_row="L0b_bv", bo_row="L0b_bp")
CFG_L1A = AttnCfg("l1a", 64, "L1a_wq", "L1a_wk", "L1a_wv", "L1a_wo",
                  _longf, 0.125, bq_fm="L1a_bq", bk_fm="L1a_bk",
                  bv_row="L1a_bv", bo_row="L1a_bo")
CFG_L1B = AttnCfg("l1b", 64, "L1b_wq", "L1b_wk", "L1b_wv", "L1b_wout",
                  _local, 0.125, use_gates=True)

ALL_CFGS = [CFG_L0A, CFG_L0B, CFG_L1A, CFG_L1B]
MASK_TILES, MASK_IDS = collect_masks(ALL_CFGS)


def fm_pack(v):
    return np.ascontiguousarray(v.reshape(8, 128).T).astype(np.float32)


def host_arrays(inputs):
    a = {}
    t = lambda w: np.ascontiguousarray(np.asarray(w).T)
    asf = lambda w: np.ascontiguousarray(np.asarray(w), dtype=np.float32)
    a["c_ident"] = np.eye(128, dtype=BFD)
    a["c_masks"] = (np.stack(MASK_TILES, axis=1).reshape(128, -1)
                    if MASK_TILES else np.zeros((128, 128), np.float32))
    ss = np.arange(128)[:, None]
    mm = np.arange(64)[None, :]
    a["c_lmask"] = np.where(mm <= ss // 32, 0.0, NEG).astype(np.float32)
    a["L0a_wqT"] = t(inputs["L0_attn_a_Wq"])
    a["L0a_wkT"] = t(inputs["L0_attn_a_Wk"])
    a["L0a_wvT"] = t(inputs["L0_attn_a_Wv"])
    a["L0a_woT"] = t(inputs["L0_attn_a_Wo"])
    a["L0a_bq"] = fm_pack(np.asarray(inputs["L0_attn_a_bq"]))
    a["L0a_bk"] = fm_pack(np.asarray(inputs["L0_attn_a_bk"]))
    a["L0a_bv"] = asf(inputs["L0_attn_a_bv"]).reshape(1, D)
    a["L0a_bo"] = asf(inputs["L0_attn_a_bo"]).reshape(1, D)
    wat = np.asarray(inputs["L0_attn_b_w_attn"])
    bat = np.asarray(inputs["L0_attn_b_b_attn"])
    a["L0b_wq"] = np.ascontiguousarray(wat[:, 0:1024])
    a["L0b_wk"] = np.ascontiguousarray(wat[:, 1024:2048])
    a["L0b_wv"] = np.ascontiguousarray(wat[:, 2048:3072])
    a["L0b_bq"] = fm_pack(bat[0:1024])
    a["L0b_bk"] = fm_pack(bat[1024:2048])
    a["L0b_bv"] = asf(bat[2048:3072]).reshape(1, D)
    a["L0b_wp"] = asf(inputs["L0_attn_b_w_proj"])
    a["L0b_bp"] = asf(inputs["L0_attn_b_b_proj"]).reshape(1, D)
    a["L1a_wq"] = asf(inputs["L1_attn_a_Wq"])
    a["L1a_wk"] = asf(inputs["L1_attn_a_Wk"])
    a["L1a_wv"] = asf(inputs["L1_attn_a_Wv"])
    a["L1a_wo"] = asf(inputs["L1_attn_a_Wo"])
    a["L1a_bq"] = fm_pack(np.asarray(inputs["L1_attn_a_bq"]))
    a["L1a_bk"] = fm_pack(np.asarray(inputs["L1_attn_a_bk"]))
    a["L1a_bv"] = asf(inputs["L1_attn_a_bv"]).reshape(1, D)
    a["L1a_bo"] = asf(inputs["L1_attn_a_bo"]).reshape(1, D)
    a["L1b_wq"] = asf(inputs["L1_attn_b_wq"])
    a["L1b_wk"] = asf(inputs["L1_attn_b_wk"])
    a["L1b_wv"] = asf(inputs["L1_attn_b_wv"])
    a["L1b_wgq"] = asf(inputs["L1_attn_b_wgq"])
    a["L1b_wgk"] = asf(inputs["L1_attn_b_wgk"])
    a["L1b_wgv"] = asf(inputs["L1_attn_b_wgv"])
    a["L1b_rout"] = asf(inputs["L1_attn_b_r_out_w"])
    a["L1b_wout"] = asf(inputs["L1_attn_b_w_out"])
    gl = np.asarray(inputs["L1_attn_b_gate_local_w"])
    gr = np.asarray(inputs["L1_attn_b_gate_remote_w"])
    gw = np.zeros((128, 32), np.float32)
    for h in range(16):
        hb = 64 * (h % 2)
        gw[hb:hb + 64, h] = gl[h]
        gw[hb:hb + 64, 16 + h] = gr[h]
    a["L1b_gw"] = gw
    a["L0f_wfcT"] = t(inputs["L0_ffn_W_fc"])
    a["L0f_wprT"] = t(inputs["L0_ffn_W_proj"])
    a["L0f_lnw"] = np.ascontiguousarray(
        np.asarray(inputs["L0_ffn_ln_w"]).reshape(32, 128).T).astype(np.float32)
    a["L0f_lnb_fm"] = np.ascontiguousarray(
        np.asarray(inputs["L0_ffn_ln_b"]).reshape(32, 128).T).astype(np.float32)
    a["L1f_w1T"] = t(inputs["L1_ffn_w1_w"])
    a["L1f_w2T"] = t(inputs["L1_ffn_w2_w"])
    a["L1f_h1T"] = t(inputs["L1_ffn_h1_w"])
    a["L1f_h2T"] = t(inputs["L1_ffn_h2_w"])
    a["L1f_w3T"] = t(inputs["L1_ffn_w3_w"])
    a["L1f_s1"] = np.ascontiguousarray(
        np.asarray(inputs["L1_ffn_w1_s"]).reshape(32, 128).T).astype(np.float32)
    a["L1f_s2"] = np.ascontiguousarray(
        np.asarray(inputs["L1_ffn_w2_s"]).reshape(32, 128).T).astype(np.float32)
    a["L1f_sh1"] = np.ascontiguousarray(
        np.asarray(inputs["L1_ffn_h1_s"]).reshape(32, 128).T).astype(np.float32)
    a["L1f_sh2"] = np.ascontiguousarray(
        np.asarray(inputs["L1_ffn_h2_s"]).reshape(32, 128).T).astype(np.float32)
    a["L1f_s3fm"] = np.ascontiguousarray(
        np.asarray(inputs["L1_ffn_w3_s"]).reshape(8, 128).T).astype(np.float32)
    return a


DRAM_SPECS = None


def dram_specs(arrays):
    return {k: (list(v.shape),
                BF16 if v.dtype == BFD else F32)
            for k, v in arrays.items()}



SUBLAYERS = ["l0a", "l0f1", "l0b", "l0f2", "l1a", "l1f1", "l1b", "l1f2"]


def build_program(subs, arrays, dumps=()):
    nc = bass.Bass("TRN2", target_bir_lowering=False)
    dram = {}
    for k, (shape, dt) in dram_specs(arrays).items():
        dram[k] = nc.dram_tensor(k, shape, dt, kind="ExternalInput").ap()
    x_in = nc.dram_tensor("x_in", [S, D], F32, kind="ExternalInput").ap()
    x_out = nc.dram_tensor("x_out", [S, D], F32, kind="ExternalOutput").ap()
    scr = {}
    if any(s in subs for s in ("l0f1", "l0f2")):
        scr["fcq"] = nc.dram_tensor("scr_fcq", [1024, 4096], BF16,
                                    kind="Internal").ap()
        scr["prq"] = nc.dram_tensor("scr_prq", [4096, 1024], BF16,
                                    kind="Internal").ap()
        scr["l0f_vec"] = nc.dram_tensor("scr_l0f_vec", [1, 1024], BF16,
                                        kind="Internal").ap()
    if any(s in subs for s in ("l1f1", "l1f2")):
        scr["w1q"] = nc.dram_tensor("scr_w1q", [1024, 4096], BF16,
                                    kind="Internal").ap()
        scr["w2q"] = nc.dram_tensor("scr_w2q", [1024, 4096], BF16,
                                    kind="Internal").ap()
        scr["h1q"] = nc.dram_tensor("scr_h1q", [4096, 4096], BF16,
                                    kind="Internal").ap()
        scr["h2q"] = nc.dram_tensor("scr_h2q", [4096, 4096], BF16,
                                    kind="Internal").ap()
        scr["w3q"] = nc.dram_tensor("scr_w3q", [4096, 1024], BF16,
                                    kind="Internal").ap()
    dump_drams = {}
    with ExitStack() as stack:
        b = Bufs(nc, stack, len(MASK_TILES))
        p = Prog(nc)
        load_consts(p, b, dram, len(MASK_TILES))
        load_x(p, b, x_in)
        did_l0f_prep = did_l1f_prep = False
        if any(s in subs for s in ("l1f1", "l1f2")):
            emit_l1f_prep(p, b, dram, scr)
            did_l1f_prep = True
        for sub in subs:
            if sub == "l0a":
                emit_ln_xhatT(p, b)
                emit_attention(p, b, CFG_L0A, dram, MASK_IDS)
            elif sub == "l0b":
                emit_ln_xhatT(p, b)
                emit_attention(p, b, CFG_L0B, dram, MASK_IDS)
            elif sub == "l1a":
                emit_ln_xhatT(p, b)
                emit_attention(p, b, CFG_L1A, dram, MASK_IDS)
            elif sub in ("l0f1", "l0f2"):
                if not did_l0f_prep:
                    emit_l0f_prep(p, b, dram, scr)
                    did_l0f_prep = True
                emit_ln_xhatT(p, b)
                emit_l0_ffn(p, b, scr)
            elif sub in ("l1f1", "l1f2"):
                if not did_l1f_prep:
                    emit_l1f_prep(p, b, dram, scr)
                    did_l1f_prep = True
                emit_ln_xhatT(p, b)
                emit_l1_ffn(p, b, dram, scr)
            elif sub == "l1b":
                emit_l1b(p, b, dram, MASK_IDS)
        for name in dumps:
            t = getattr(b, name)
            shp = [t.shape[i] for i in range(len(t.shape))]
            dd = nc.dram_tensor("dump_" + name, shp,
                                t.dtype, kind="ExternalOutput").ap()
            dump_drams[name] = dd
            p.emit("sync", lambda s, nc, t=t, dd=dd: s.dma_start(
                out=dd[:], in_=t[:]), reads=[name],
                writes=["dump_" + name], dma=True)
        store_x(p, b, x_out)
        p.finalize()
    return nc


def run_subs(x_full, inputs, subs, dumps=(), cores=8):
    arrays = host_arrays(inputs)
    nc = build_program(subs, arrays, dumps)
    in_maps = []
    for c in range(cores):
        m = dict(arrays)
        m["x_in"] = np.ascontiguousarray(x_full[c], dtype=np.float32)
        in_maps.append(m)
    res = run_bass_kernel_spmd(nc, in_maps, core_ids=list(range(cores)))
    out = np.stack([r["x_out"] for r in res.results], axis=0)
    dump_vals = [{k: r["dump_" + k] for k in dumps} for r in res.results]
    return out, dump_vals


def kernel(**inputs):
    x = np.asarray(inputs["L0_attn_a_x"], dtype=np.float32)
    out, _ = run_subs(x, inputs, SUBLAYERS)
    return out


# version 08
# speedup vs baseline: 1.0861x; 1.0861x over previous
import numpy as np
import ml_dtypes
from contextlib import ExitStack

import concourse.bass as bass
import concourse.mybir as mybir
from concourse.bass_utils import run_bass_kernel_spmd

F32 = mybir.dt.float32
BF16 = mybir.dt.bfloat16
AF = mybir.ActivationFunctionType
ALU = mybir.AluOpType
AX = mybir.AxisListType

D = 1024
S = 2048
NT = S // 128
NC = D // 128
NEG = -1.0e30
BFD = ml_dtypes.bfloat16

ENGINES = ("tensor", "vector", "scalar", "gpsimd", "sync")
ENG_SEM = {"tensor": "PE", "vector": "DVE", "scalar": "ACT",
           "gpsimd": "POOL", "sync": "SP"}
DMA_SEM = {"gpsimd": "QP", "sync": "QS", "scalar": "QA", "vector": "QV"}


class Prog:

    def __init__(self, nc):
        self.nc = nc
        self.items = {e: [] for e in ENGINES}
        self.count = {}
        for sname in list(ENG_SEM.values()) + list(DMA_SEM.values()):
            self.count[sname] = 0
        self.waited = {e: {} for e in ENGINES}
        self.last_w = {}
        self.readers = {}

    def _wait(self, eng, tok):
        if tok is None:
            return
        semname, val = tok
        if eng == "tensor" and semname == "PE":
            return
        if self.waited[eng].get(semname, 0) >= val:
            return
        self.waited[eng][semname] = val
        self.items[eng].append(("wait", semname, val))

    def emit(self, eng, fn, reads=(), writes=(), dma=False):
        for k in reads:
            self._wait(eng, self.last_w.get(k))
        for k in writes:
            self._wait(eng, self.last_w.get(k))
            for t in self.readers.get(k, ()):
                self._wait(eng, t)
        if dma:
            assert writes, "dma emit needs a destination key"
            semname = "q__" + writes[0]
            self.count.setdefault(semname, 0)
        else:
            semname = ENG_SEM[eng]
        inc = 16 if dma else 1
        self.count[semname] += inc
        tok = (semname, self.count[semname])
        self.items[eng].append(("inst", fn, semname, inc))
        for k in writes:
            self.last_w[k] = tok
            self.readers[k] = []
        for k in reads:
            self.readers.setdefault(k, []).append(tok)
        return tok

    def finalize(self):
        nc = self.nc
        final_waits = [(s, v) for s, v in self.count.items() if v > 0]
        with ExitStack() as stack:
            sems = {s: stack.enter_context(nc.semaphore(s))
                    for s in self.count}
            block = stack.enter_context(nc.Block())

            def make_body(eng):
                def body(seq):
                    for it in self.items[eng]:
                        if it[0] == "wait":
                            seq.wait_ge(sems[it[1]], it[2])
                        else:
                            _, fn, semname, inc = it
                            fn(seq, nc).then_inc(sems[semname], inc)
                    if eng == "sync":
                        for sname, v in final_waits:
                            seq.wait_ge(sems[sname], v)
                return body

            for eng in ENGINES:
                getattr(block, eng)(make_body(eng))


def psb(lo, hi):
    return [f"ps:{i}" for i in range(lo // 512, (hi + 511) // 512)]


class Bufs:
    def __init__(self, nc, stack, n_masks):
        def sb(name, shape, dt):
            return stack.enter_context(nc.sbuf_tensor(name, shape, dt))
        self.x_res = sb("x_res", [128, NT, D], F32)
        self.bigA = sb("bigA", [128, 16384], BF16)
        self.bigB = sb("bigB", [128, 16384], BF16)
        self.bigC = sb("bigC", [128, 16384], BF16)
        self.wres = sb("wres", [128, 8, 1024], BF16)
        self.wt0 = sb("wt0", [128, 2048], BF16)
        self.wt1 = sb("wt1", [128, 2048], BF16)
        self.scr32 = sb("scr32", [128, 512], F32)
        self.scrb = sb("scrb", [128, 1024], BF16)
        self.qcol = sb("qcol", [128, 8, 512], BF16)
        self.biasrow = sb("biasrow", [128, 1024], BF16)
        self.lmask = sb("lmask", [128, 64], F32)
        self.gates = sb("gates", [128, NT, 32], F32)
        self.gwb = sb("gwb", [128, 32], BF16)
        self.pt = sb("pt", [128, 2, 512], BF16)
        self.nmpad = sb("nmpad", [128, 8], BF16)
        self.rowbuf = sb("rowbuf", [128, 1024], BF16)
        self.ident = sb("ident", [128, 128], BF16)
        self.ones_bf = sb("ones_bf", [128, 128], BF16)
        self.ones32 = sb("ones32", [128, 128], F32)
        self.stat = sb("stat", [128, 64], F32)
        self.small = sb("small", [128, 40], F32)
        self.bias_fm = sb("bias_fm", [128, 16], F32)
        self.masks = sb("masks", [128, max(n_masks, 1), 128], BF16)
        self.ps = stack.enter_context(nc.psum_tensor("ps", [128, 4096], F32))

    def wslot(self, i):
        t = self.wt0 if i % 2 == 0 else self.wt1
        lo = (i // 2 % 2) * 1024
        return t[:, lo:lo + 1024], f"wts{i % 4}"


def bigA_fm(b, c, t0, t1):
    return b.bigA[:, c * 2048 + t0: c * 2048 + t1]


def load_consts(p, b, dram, n_masks):
    e = p.emit
    e("sync", lambda s, nc: s.dma_start(out=b.ident[:], in_=dram["c_ident"][:]),
      writes=["ident"], dma=True)
    if n_masks:
        e("gpsimd", lambda s, nc: s.dma_start(
            out=b.masks[:, 0:n_masks, :],
            in_=dram["c_masks"][:]), writes=["masks"], dma=True)
    e("sync", lambda s, nc: s.dma_start(out=b.lmask[:], in_=dram["c_lmask"][:]),
      writes=["lmask"], dma=True)
    e("vector", lambda s, nc: s.memset(b.ones_bf[:], 1.0), writes=["ones_bf"])
    e("vector", lambda s, nc: s.memset(b.ones32[:], 1.0), writes=["ones32"])


def load_x(p, b, x_dram):
    for t in range(NT):
        p.emit("sync", lambda s, nc, t=t: s.dma_start(
            out=b.x_res[:, t, :], in_=x_dram[t * 128:(t + 1) * 128, :]),
            writes=["x_res"], dma=True)


def store_x(p, b, out_dram):
    for t in range(NT):
        p.emit("sync", lambda s, nc, t=t: s.dma_start(
            out=out_dram[t * 128:(t + 1) * 128, :], in_=b.x_res[:, t, :]),
            reads=["x_res"], writes=["x_out"], dma=True)


def emit_ln_xhatT_old(p, b):
    e = p.emit
    st = b.stat
    for t in range(NT):
        e("vector", lambda s, nc, t=t: s.bn_stats(st[:, 0:6],
                                                  b.x_res[:, t, 0:512]),
          reads=["x_res"], writes=["stat"])
        e("vector", lambda s, nc, t=t: s.bn_stats(st[:, 6:12],
                                                  b.x_res[:, t, 512:1024]),
          reads=["x_res"], writes=["stat"])
        e("vector", lambda s, nc: s.bn_aggr(
            st[:, 12:14], st[:, 0:12].rearrange("p (a b) -> p a b", a=2)),
          reads=["stat"], writes=["stat"])
        e("vector", lambda s, nc: s.tensor_scalar(
            st[:, 16:17], st[:, 13:14], 1e-5, None, op0=ALU.add),
          reads=["stat"], writes=["stat"])
        e("scalar", lambda s, nc: s.activation(st[:, 14:15], st[:, 16:17],
                                               AF.Sqrt),
          reads=["stat"], writes=["stat"])
        e("vector", lambda s, nc: s.reciprocal(st[:, 15:16], st[:, 14:15]),
          reads=["stat"], writes=["stat"])
        e("vector", lambda s, nc, t=t: s.tensor_scalar(
            b.scrb[:, 0:1024], b.x_res[:, t, :], st[:, 12:13], st[:, 15:16],
            op0=ALU.subtract, op1=ALU.mult),
          reads=["x_res", "stat"], writes=["scrb"])
        for c in range(NC):
            lo = 2048 + (c % 2) * 512
            e("tensor", lambda s, nc, c=c, lo=lo: s.matmul(
                b.ps[:, lo:lo + 128], b.scrb[:, c * 128:(c + 1) * 128],
                b.ident[:], start=True, stop=True),
              reads=["scrb", "ident"], writes=psb(lo, lo + 128))
            e("scalar", lambda s, nc, c=c, t=t, lo=lo: s.copy(
                bigA_fm(b, c, t * 128, (t + 1) * 128), b.ps[:, lo:lo + 128]),
              reads=psb(lo, lo + 128), writes=["bigA"])


def emit_ln_xhatT(p, b):
    import os as _os
    if _os.environ.get("K_OLDLN"):
        return emit_ln_xhatT_old(p, b)
    e = p.emit
    st = b.stat
    for t in range(NT):
        par = t % 2
        sb0 = par * 20
        kst = f"stat{par}"
        slot, kslot = b.wslot(par)
        e("vector", lambda s, nc, t=t, sb0=sb0: s.bn_stats(
            st[:, sb0:sb0 + 6], b.x_res[:, t, 0:512]),
          reads=["x_res"], writes=[kst])
        e("vector", lambda s, nc, t=t, sb0=sb0: s.bn_stats(
            st[:, sb0 + 6:sb0 + 12], b.x_res[:, t, 512:1024]),
          reads=["x_res"], writes=[kst])
        e("vector", lambda s, nc, sb0=sb0: s.bn_aggr(
            st[:, sb0 + 12:sb0 + 14],
            st[:, sb0:sb0 + 12].rearrange("p (a b) -> p a b", a=2)),
          reads=[kst], writes=[kst])
        e("vector", lambda s, nc, sb0=sb0: s.tensor_scalar(
            st[:, sb0 + 16:sb0 + 17], st[:, sb0 + 13:sb0 + 14], 1e-5, None,
            op0=ALU.add),
          reads=[kst], writes=[kst])
        e("scalar", lambda s, nc, sb0=sb0: s.activation(
            st[:, sb0 + 14:sb0 + 15], st[:, sb0 + 16:sb0 + 17], AF.Sqrt),
          reads=[kst], writes=[kst])
        e("vector", lambda s, nc, sb0=sb0: s.reciprocal(
            st[:, sb0 + 15:sb0 + 16], st[:, sb0 + 14:sb0 + 15]),
          reads=[kst], writes=[kst])
        e("vector", lambda s, nc, t=t, sb0=sb0, slot=slot: s.tensor_scalar(
            slot[:, 0:1024], b.x_res[:, t, :], st[:, sb0 + 12:sb0 + 13],
            st[:, sb0 + 15:sb0 + 16],
            op0=ALU.subtract, op1=ALU.mult),
          reads=["x_res", kst], writes=[kslot])
        for c in range(NC):
            lo = 2048 + par * 1024 + (c // 4) * 512 + (c % 4) * 128
            e("tensor", lambda s, nc, c=c, lo=lo, slot=slot: s.matmul(
                b.ps[:, lo:lo + 128], slot[:, c * 128:(c + 1) * 128],
                b.ident[:], start=True, stop=True),
              reads=[kslot, "ident"], writes=psb(lo, lo + 128))
        for half in range(2):
            lo = 2048 + par * 1024 + half * 512
            e("vector", lambda s, nc, t=t, half=half, lo=lo: s.tensor_copy(
                b.bigA[:, 0:16384].rearrange("p (c x) -> p c x", c=8)
                [:, half * 4:(half + 1) * 4, t * 128:(t + 1) * 128],
                b.ps[:, lo:lo + 512].rearrange("p (c x) -> p c x", c=4)),
              reads=psb(lo, lo + 512), writes=["bigA"])


def load_wres(p, b, w_dram, ncol=8, width=1024):
    for dc in range(ncol):
        p.emit("gpsimd", lambda s, nc, dc=dc: s.dma_start(
            out=b.wres[:, dc, 0:width],
            in_=w_dram[dc * 128:(dc + 1) * 128, 0:width]),
            writes=["wres"], dma=True)



class AttnCfg:
    def __init__(self, name, dh, wq, wk, wv, wo, allowed, scale,
                 bq_fm=None, bk_fm=None, bv_row=None, bo_row=None,
                 use_gates=False):
        self.name = name
        self.dh = dh
        self.n_heads = D // dh
        self.wq, self.wk, self.wv, self.wo = wq, wk, wv, wo
        self.allowed = allowed
        self.scale = scale
        self.bq_fm, self.bk_fm = bq_fm, bk_fm
        self.bv_row, self.bo_row = bv_row, bo_row
        self.use_gates = use_gates


def band_blocks(cfg, qt):
    ii = qt * 128 + np.arange(128)[:, None]
    out = []
    for kt in range(NT):
        jj = kt * 128 + np.arange(128)[None, :]
        a = cfg.allowed(ii, jj)
        if not a.any():
            continue
        if a.all():
            out.append((kt, None))
        else:
            out.append((kt, np.where(a, 0.0, NEG).astype(np.float32)))
    return out


def collect_masks(cfgs):
    tiles = []
    index = {}

    def get_id(arr):
        key = arr.tobytes()
        if key not in index:
            index[key] = len(tiles)
            tiles.append(arr)
        return index[key]

    ids = {}
    for cfg in cfgs:
        for qt in range(NT):
            for kt, m in band_blocks(cfg, qt):
                if m is not None:
                    ids[(cfg.name, qt, kt, "tm")] = get_id(m)
                    ids[(cfg.name, qt, kt, "st")] = get_id(
                        np.ascontiguousarray(m.T))
    return tiles, ids


def emit_proj_fm(p, b, out_buf_key, out_view, bias_sl=None):
    e = p.emit
    for oc in range(8):
        for tg in range(4):
            lo = (tg % 2) * 512
            ks = psb(lo, lo + 512)
            for dc in range(NC):
                e("tensor", lambda s, nc, oc=oc, tg=tg, dc=dc, lo=lo: s.matmul(
                    b.ps[:, lo:lo + 512],
                    b.wres[:, dc, oc * 128:(oc + 1) * 128],
                    bigA_fm(b, dc, tg * 512, (tg + 1) * 512),
                    start=(dc == 0), stop=(dc == NC - 1)),
                  reads=["wres", "bigA"], writes=ks)
            if bias_sl is not None:
                e("vector", lambda s, nc, oc=oc, tg=tg, lo=lo: s.tensor_scalar(
                    out_view(oc, tg), b.ps[:, lo:lo + 512],
                    b.bias_fm[:, bias_sl + oc: bias_sl + oc + 1], None,
                    op0=ALU.add),
                  reads=ks + ["bias_fm"], writes=[out_buf_key])
            else:
                e("vector", lambda s, nc, oc=oc, tg=tg, lo=lo: s.tensor_copy(
                    out_view(oc, tg), b.ps[:, lo:lo + 512]),
                  reads=ks, writes=[out_buf_key])


def emit_proj_tm_v(p, b, cfg, dram):
    e = p.emit
    load_wres(p, b, dram[cfg.wv])
    if cfg.bv_row:
        e("gpsimd", lambda s, nc: s.dma_start(
            out=b.biasrow[0:1, 0:D], in_=dram[cfg.bv_row][:]),
          writes=["biasrow"], dma=True)
    for t in range(NT):
        for og in range(2):
            lo = (og % 2) * 512
            ks = psb(lo, lo + 512)
            first = True
            for dc in range(NC):
                for ob in range(4):
                    oc = og * 4 + ob
                    e("tensor", lambda s, nc, t=t, dc=dc, ob=ob, oc=oc, lo=lo,
                        first=first: s.matmul(
                        b.ps[:, lo + ob * 128: lo + (ob + 1) * 128],
                        bigA_fm(b, dc, t * 128, (t + 1) * 128),
                        b.wres[:, dc, oc * 128:(oc + 1) * 128],
                        start=first, stop=False),
                      reads=["bigA", "wres"], writes=ks)
                    first = False
            if cfg.bv_row:
                for ob in range(4):
                    oc = og * 4 + ob
                    e("tensor", lambda s, nc, ob=ob, oc=oc, lo=lo: s.matmul(
                        b.ps[:, lo + ob * 128: lo + (ob + 1) * 128],
                        b.ones_bf[0:1, 0:128],
                        b.biasrow[0:1, oc * 128:(oc + 1) * 128],
                        start=False, stop=(ob == 3)),
                      reads=["ones_bf", "biasrow"], writes=ks)
            e("vector", lambda s, nc, t=t, og=og, lo=lo: s.tensor_copy(
                b.bigC[:, t * 1024 + og * 512: t * 1024 + (og + 1) * 512],
                b.ps[:, lo:lo + 512]),
              reads=ks, writes=["bigC"])


def emit_qcol(p, b, t0, tw, bias_sl=None, scale=1.0):
    e = p.emit
    for oc in range(NC):
        lo = 2048 + (oc % 2) * 512
        ks = psb(lo, lo + tw)
        for dc in range(NC):
            e("tensor", lambda s, nc, oc=oc, dc=dc, lo=lo: s.matmul(
                b.ps[:, lo:lo + tw],
                b.wres[:, dc, oc * 128:(oc + 1) * 128],
                bigA_fm(b, dc, t0, t0 + tw),
                start=(dc == 0), stop=(dc == NC - 1)),
              reads=["wres", "bigA"], writes=ks)
        e("vector", lambda s, nc, oc=oc, lo=lo, bias_sl=bias_sl: s.tensor_scalar(
            b.qcol[:, oc, 0:tw], b.ps[:, lo:lo + tw],
            (b.bias_fm[:, bias_sl + oc:bias_sl + oc + 1]
             if bias_sl is not None else 0.0),
            float(scale), op0=ALU.add, op1=ALU.mult),
          reads=ks + (["bias_fm"] if bias_sl is not None else []),
          writes=["qcol"])


def emit_attention_g4(p, b, cfg, dram, mask_ids):
    e = p.emit
    G = 4

    load_wres(p, b, dram[cfg.wk])
    if cfg.bk_fm:
        e("sync", lambda s, nc: s.dma_start(out=b.bias_fm[:, 8:16],
                                            in_=dram[cfg.bk_fm][:]),
          writes=["bias_fm"], dma=True)
    emit_proj_fm(p, b, "bigB",
                 lambda oc, tg: b.bigB[:, oc * 2048 + tg * 512:
                                       oc * 2048 + (tg + 1) * 512],
                 bias_sl=(8 if cfg.bk_fm else None))
    emit_proj_tm_v(p, b, cfg, dram)
    load_wres(p, b, dram[cfg.wq])
    if cfg.bq_fm:
        e("sync", lambda s, nc: s.dma_start(out=b.bias_fm[:, 0:8],
                                            in_=dram[cfg.bq_fm][:]),
          writes=["bias_fm"], dma=True)

    for g in range(NT // G):
        emit_qcol(p, b, g * 512, 512, bias_sl=(0 if cfg.bq_fm else None),
                  scale=cfg.scale)
        bands = [dict(band_blocks(cfg, g * G + qi)) for qi in range(G)]
        ublocks = sorted(set().union(*[set(d.keys()) for d in bands]))
        for h in range(cfg.n_heads):
            par = h % 2
            hb, hp = 64 * par, h // 2
            sb0 = 16 + par * 24
            kst = f"stat{par}"
            knm = f"nmpad{par}"
            krb = f"rowbuf{par}"
            kpd = "scrb"
            for qi in range(G):
                qt = g * G + qi
                blocks = sorted(bands[qi].keys())
                nbl = len(blocks)
                W = nbl * 128
                base = (qi % 2) * 1024 if W <= 1024 else 0
                i = 0
                while i < nbl:
                    j = i
                    while (j + 1 < nbl and blocks[j + 1] == blocks[j] + 1
                           and (j + 1) // 4 == i // 4):
                        j += 1
                    kt0, width, pos = blocks[i], (j - i + 1) * 128, \
                        base + i * 128
                    e("tensor", lambda s, nc, hb=hb, hp=hp, qi=qi, kt0=kt0,
                        width=width, pos=pos: s.matmul(
                        b.ps[:, pos:pos + width],
                        b.qcol[hb:hb + 64, hp, qi * 128:(qi + 1) * 128],
                        b.bigB[hb:hb + 64, hp * 2048 + kt0 * 128:
                               hp * 2048 + kt0 * 128 + width],
                        start=True, stop=True),
                      reads=["qcol", "bigB"], writes=psb(pos, pos + width))
                    i = j + 1
                for bi, kt in enumerate(blocks):
                    m = bands[qi][kt]
                    if m is None:
                        continue
                    mid = mask_ids[(cfg.name, qt, kt, "tm")]
                    pos = base + bi * 128
                    e("vector", lambda s, nc, mid=mid, pos=pos: s.tensor_tensor(
                        b.ps[:, pos:pos + 128], b.ps[:, pos:pos + 128],
                        b.masks[:, mid, :], op=ALU.add),
                      reads=psb(pos, pos + 128) + ["masks"],
                      writes=psb(pos, pos + 128))
                e("vector", lambda s, nc, W=W, base=base, col=par * 4 + qi:
                    s.tensor_reduce(
                    b.nmpad[:, col:col + 1], b.ps[:, base:base + W], axis=AX.X,
                    op=ALU.max, negate=True),
                  reads=psb(base, base + W), writes=[knm])
                e("vector", lambda s, nc, col=par * 4 + qi, qi=qi, sb0=sb0:
                    s.tensor_copy(b.stat[:, sb0 + qi:sb0 + qi + 1],
                                  b.nmpad[:, col:col + 1]),
                  reads=[knm], writes=[kst])
                nch = (W + 511) // 512
                for c in range(nch):
                    w0, w1 = base + c * 512, base + min(W, (c + 1) * 512)
                    e("scalar", lambda s, nc, w0=w0, w1=w1, c=c, qi=qi,
                        sb0=sb0, par=par: s.activation(
                        b.scrb[:, par * 512:par * 512 + (w1 - w0)],
                        b.ps[:, w0:w1], AF.Exp,
                        bias=b.stat[:, sb0 + qi:sb0 + qi + 1], scale=1.0,
                        accum_out=b.stat[:, sb0 + 4 + qi * 4 + c:
                                         sb0 + 5 + qi * 4 + c]),
                      reads=psb(w0, w1) + [kst], writes=[kpd, kst])
                if nch > 1:
                    e("vector", lambda s, nc, qi=qi, nch=nch, sb0=sb0:
                        s.tensor_reduce(
                        b.stat[:, sb0 + qi:sb0 + qi + 1],
                        b.stat[:, sb0 + 4 + qi * 4:sb0 + 4 + qi * 4 + nch],
                        axis=AX.X, op=ALU.add), reads=[kst], writes=[kst])
                    den_sl = sb0 + qi
                else:
                    den_sl = sb0 + 4 + qi * 4
                e("vector", lambda s, nc, qi=qi, den_sl=den_sl, sb0=sb0:
                    s.reciprocal(
                    b.stat[:, sb0 + 20 + qi:sb0 + 21 + qi],
                    b.stat[:, den_sl:den_sl + 1]),
                  reads=[kst], writes=[kst])
            pvlo = (6 + par) * 512
            for qi in range(G):
                e("tensor", lambda s, nc, qi=qi, col=par * 4 + qi, pvlo=pvlo:
                    s.matmul(b.ps[0:1, pvlo + qi * 128:pvlo + (qi + 1) * 128],
                             b.nmpad[:, col:col + 1],
                             b.ident[:], start=True, stop=True),
                  reads=[knm, "ident"], writes=psb(pvlo, pvlo + 512))
            e("vector", lambda s, nc, pvlo=pvlo, par=par: s.tensor_copy(
                b.rowbuf[0:1, par * 512:(par + 1) * 512],
                b.ps[0:1, pvlo:pvlo + 512]),
              reads=psb(pvlo, pvlo + 512), writes=[krb])
            pv_list = [(kt, qi) for kt in ublocks for qi in range(G)
                       if kt in bands[qi]]
            n_pv = len(pv_list)
            pvk = psb(pvlo, pvlo + 512)
            nbl_u = len(ublocks)
            ipv = 0

            def emit_st(bi, kt):
                lo_st = 2048 + (bi % 2) * 512
                ksS = psb(lo_st, lo_st + 512)
                e("tensor", lambda s, nc, hb=hb, hp=hp, kt=kt, lo_st=lo_st:
                    s.matmul(b.ps[:, lo_st:lo_st + 512],
                             b.bigB[hb:hb + 64, hp * 2048 + kt * 128:
                                    hp * 2048 + (kt + 1) * 128],
                             b.qcol[hb:hb + 64, hp, 0:512],
                             start=True, stop=False),
                  reads=["bigB", "qcol"], writes=ksS)
                e("tensor", lambda s, nc, lo_st=lo_st, par=par: s.matmul(
                    b.ps[:, lo_st:lo_st + 512], b.ones_bf[0:1, 0:128],
                    b.rowbuf[0:1, par * 512:(par + 1) * 512],
                    start=False, stop=True),
                  reads=["ones_bf", krb], writes=ksS)
                qi = 0
                while qi < G:
                    if kt in bands[qi]:
                        m = bands[qi][kt]
                        if m is not None:
                            mid = mask_ids[(cfg.name, g * G + qi, kt, "st")]
                            e("vector", lambda s, nc, mid=mid, qi=qi,
                                lo_st=lo_st: s.tensor_tensor(
                                b.ps[:, lo_st + qi * 128:lo_st + (qi + 1) * 128],
                                b.ps[:, lo_st + qi * 128:lo_st + (qi + 1) * 128],
                                b.masks[:, mid, :], op=ALU.add),
                              reads=ksS + ["masks"], writes=ksS)
                        qi += 1
                    else:
                        q1 = qi
                        while q1 < G and kt not in bands[q1]:
                            q1 += 1
                        e("vector", lambda s, nc, qi=qi, q1=q1, lo_st=lo_st:
                            s.memset(b.ps[:, lo_st + qi * 128:
                                          lo_st + q1 * 128], NEG),
                          reads=ksS, writes=ksS)
                        qi = q1
                e("scalar", lambda s, nc, bi=bi, lo_st=lo_st: s.activation(
                    b.pt[:, bi % 2, :], b.ps[:, lo_st:lo_st + 512], AF.Exp),
                  reads=ksS, writes=[f"pt{bi % 2}"])

            def emit_pv(bi, kt):
                nonlocal ipv
                for qi in range(G):
                    if kt not in bands[qi]:
                        continue
                    e("tensor", lambda s, nc, bi=bi, kt=kt, qi=qi, h=h,
                        pvlo=pvlo, st=(ipv == 0), sp=(ipv == n_pv - 1):
                        s.matmul(
                        b.ps[:, pvlo + qi * 64:pvlo + (qi + 1) * 64],
                        b.pt[:, bi % 2, qi * 128:(qi + 1) * 128],
                        b.bigC[:, kt * 1024 + h * 64:kt * 1024 + (h + 1) * 64],
                        start=st, stop=sp, skip_group_check=True),
                      reads=[f"pt{bi % 2}", "bigC"], writes=pvk)
                    ipv += 1

            import os as _os
            if _os.environ.get("K_NOPIPE"):
                for bi in range(nbl_u):
                    emit_st(bi, ublocks[bi])
                    emit_pv(bi, ublocks[bi])
            else:
                for bi in range(nbl_u + 1):
                    if bi < nbl_u:
                        emit_st(bi, ublocks[bi])
                    if bi >= 1:
                        emit_pv(bi - 1, ublocks[bi - 1])
            for qi in range(G):
                qt = g * G + qi
                strip = (2 * qi + h // 8) * 2048 + g * 512 + (h % 8) * 64
                if cfg.use_gates:
                    e("vector", lambda s, nc, qi=qi, qt=qt, strip=strip,
                        pvlo=pvlo, sb0=sb0, h=h: s.tensor_scalar(
                        b.bigA[:, strip:strip + 64],
                        b.ps[:, pvlo + qi * 64:pvlo + (qi + 1) * 64],
                        b.stat[:, sb0 + 20 + qi:sb0 + 21 + qi],
                        b.gates[:, qt, 2 * h:2 * h + 1],
                        op0=ALU.mult, op1=ALU.mult),
                      reads=pvk + [kst, "gates"], writes=["bigA"])
                else:
                    e("vector", lambda s, nc, qi=qi, strip=strip,
                        pvlo=pvlo, sb0=sb0: s.tensor_scalar(
                        b.bigA[:, strip:strip + 64],
                        b.ps[:, pvlo + qi * 64:pvlo + (qi + 1) * 64],
                        b.stat[:, sb0 + 20 + qi:sb0 + 21 + qi], None,
                        op0=ALU.mult),
                      reads=pvk + [kst], writes=["bigA"])
        for qi in range(G):
            for c in range(NC):
                src = (2 * qi + c // 4) * 2048 + g * 512 + (c % 4) * 128
                bank = qi * 2 + c // 4
                lo = bank * 512 + (c % 4) * 128
                e("tensor", lambda s, nc, src=src, lo=lo: s.matmul(
                    b.ps[:, lo:lo + 128], b.bigA[:, src:src + 128],
                    b.ident[:], start=True, stop=True),
                  reads=["bigA", "ident"], writes=psb(lo, lo + 128))
        for bank in range(8):
            qi, chalf = bank // 2, bank % 2
            e("vector", lambda s, nc, qi=qi, chalf=chalf, bank=bank, g=g:
                s.tensor_copy(
                b.bigA[:, 0:16384].rearrange("p (c t) -> p c t", c=8)
                [:, chalf * 4:(chalf + 1) * 4,
                 (g * G + qi) * 128:(g * G + qi + 1) * 128],
                b.ps[:, bank * 512:(bank + 1) * 512]
                .rearrange("p (c t) -> p c t", c=4)),
              reads=psb(bank * 512, (bank + 1) * 512), writes=["bigA"])

    load_wres(p, b, dram[cfg.wo])
    if cfg.bo_row:
        e("gpsimd", lambda s, nc: s.dma_start(
            out=b.biasrow[0:1, 0:D], in_=dram[cfg.bo_row][:]),
          writes=["biasrow"], dma=True)
    for qt in range(NT):
        for og in range(2):
            lo = 3072 + og * 512
            ks = psb(lo, lo + 512)
            for dc in range(NC):
                e("tensor", lambda s, nc, qt=qt, og=og, dc=dc, lo=lo: s.matmul(
                    b.ps[:, lo:lo + 512],
                    bigA_fm(b, dc, qt * 128, (qt + 1) * 128),
                    b.wres[:, dc, og * 512:(og + 1) * 512],
                    start=(dc == 0),
                    stop=(dc == NC - 1 and not cfg.bo_row)),
                  reads=["bigA", "wres"], writes=ks)
            if cfg.bo_row:
                e("tensor", lambda s, nc, og=og, lo=lo: s.matmul(
                    b.ps[:, lo:lo + 512], b.ones_bf[0:1, 0:128],
                    b.biasrow[0:1, og * 512:(og + 1) * 512],
                    start=False, stop=True),
                  reads=["ones_bf", "biasrow"], writes=ks)
            e("vector", lambda s, nc, qt=qt, og=og, lo=lo: s.tensor_tensor(
                b.x_res[:, qt, og * 512:(og + 1) * 512], b.ps[:, lo:lo + 512],
                b.x_res[:, qt, og * 512:(og + 1) * 512], op=ALU.add),
              reads=ks + ["x_res"], writes=["x_res"])


def st_runs(blocks):
    runs = []
    i = 0
    nbl = len(blocks)
    while i < nbl:
        j = i
        while (j + 1 < nbl and blocks[j + 1][0] == blocks[j][0] + 1
               and (j + 1) // 4 == i // 4):
            j += 1
        runs.append((i, j))
        i = j + 1
    return runs


def emit_attention(p, b, cfg, dram, mask_ids):
    if cfg.dh == 64:
        emit_attention_g4(p, b, cfg, dram, mask_ids)
    else:
        emit_attention_g1(p, b, cfg, dram, mask_ids)


def emit_attention_g1(p, b, cfg, dram, mask_ids):
    e = p.emit
    dh = cfg.dh

    load_wres(p, b, dram[cfg.wk])
    if cfg.bk_fm:
        e("sync", lambda s, nc: s.dma_start(out=b.bias_fm[:, 8:16],
                                            in_=dram[cfg.bk_fm][:]),
          writes=["bias_fm"], dma=True)
    emit_proj_fm(p, b, "bigB",
                 lambda oc, tg: b.bigB[:, oc * 2048 + tg * 512:
                                       oc * 2048 + (tg + 1) * 512],
                 bias_sl=(8 if cfg.bk_fm else None))
    emit_proj_tm_v(p, b, cfg, dram)

    load_wres(p, b, dram[cfg.wq])
    if cfg.bq_fm:
        e("sync", lambda s, nc: s.dma_start(out=b.bias_fm[:, 0:8],
                                            in_=dram[cfg.bq_fm][:]),
          writes=["bias_fm"], dma=True)

    for qt in range(NT):
        emit_qcol(p, b, qt * 128, 128, bias_sl=(0 if cfg.bq_fm else None),
                  scale=cfg.scale)
        blocks = band_blocks(cfg, qt)
        nbl = len(blocks)
        W = nbl * 128
        for h in range(cfg.n_heads):
            hb = 64 * (h % 2) if dh == 64 else 0
            hp = h // 2
            for (i0, i1) in st_runs(blocks):
                kt0 = blocks[i0][0]
                width = (i1 - i0 + 1) * 128
                pos = i0 * 128
                ks = psb(pos, pos + width)
                if dh == 64:
                    e("tensor", lambda s, nc, hb=hb, hp=hp, kt0=kt0,
                        width=width, pos=pos: s.matmul(
                        b.ps[:, pos:pos + width],
                        b.qcol[hb:hb + 64, hp, 0:128],
                        b.bigB[hb:hb + 64, hp * 2048 + kt0 * 128:
                               hp * 2048 + kt0 * 128 + width],
                        start=True, stop=True),
                      reads=["qcol", "bigB"], writes=ks)
                else:
                    for dc in range(NC):
                        e("tensor", lambda s, nc, dc=dc, kt0=kt0,
                            width=width, pos=pos: s.matmul(
                            b.ps[:, pos:pos + width],
                            b.qcol[:, dc, 0:128],
                            b.bigB[:, dc * 2048 + kt0 * 128:
                                   dc * 2048 + kt0 * 128 + width],
                            start=(dc == 0), stop=(dc == NC - 1)),
                          reads=["qcol", "bigB"], writes=ks)
            for bi, (kt, m) in enumerate(blocks):
                if m is None:
                    continue
                mid = mask_ids[(cfg.name, qt, kt, "tm")]
                pos = bi * 128
                e("vector", lambda s, nc, mid=mid, pos=pos: s.tensor_tensor(
                    b.ps[:, pos:pos + 128], b.ps[:, pos:pos + 128],
                    b.masks[:, mid, :], op=ALU.add),
                  reads=psb(pos, pos + 128) + ["masks"],
                  writes=psb(pos, pos + 128))
            e("vector", lambda s, nc, W=W, hb=hb: s.tensor_reduce(
                b.nmpad[:, hb:hb + 1], b.ps[:, 0:W], axis=AX.X,
                op=ALU.max, negate=True),
              reads=psb(0, W), writes=["nmpad"])
            e("vector", lambda s, nc, hb=hb: s.tensor_copy(
                b.stat[:, 20:21], b.nmpad[:, hb:hb + 1]),
              reads=["nmpad"], writes=["stat"])
            nch = (W + 1023) // 1024
            for c in range(nch):
                w0, w1 = c * 1024, min(W, (c + 1) * 1024)
                e("scalar", lambda s, nc, w0=w0, w1=w1, c=c: s.activation(
                    b.scrb[:, 0:w1 - w0], b.ps[:, w0:w1], AF.Exp,
                    bias=b.stat[:, 20:21], scale=1.0,
                    accum_out=b.stat[:, 24 + c:25 + c]),
                  reads=psb(w0, w1) + ["stat"], writes=["scrb", "stat"])
            if nch > 1:
                e("vector", lambda s, nc, nch=nch: s.tensor_reduce(
                    b.stat[:, 28:29], b.stat[:, 24:24 + nch], axis=AX.X,
                    op=ALU.add), reads=["stat"], writes=["stat"])
            den_sl = 28 if nch > 1 else 24
            e("vector", lambda s, nc, den_sl=den_sl: s.reciprocal(
                b.stat[:, 29:30], b.stat[:, den_sl:den_sl + 1]),
              reads=["stat"], writes=["stat"])
            lo_nm = 2560
            e("tensor", lambda s, nc, lo_nm=lo_nm: s.matmul(
                b.ps[0:8, lo_nm:lo_nm + 128], b.nmpad[:], b.ident[:],
                start=True, stop=True),
              reads=["nmpad", "ident"], writes=psb(lo_nm, lo_nm + 128))
            e("vector", lambda s, nc, hb=hb, lo_nm=lo_nm: s.tensor_copy(
                b.rowbuf[hb:hb + 1, 0:128], b.ps[hb:hb + 1, lo_nm:lo_nm + 128]),
              reads=psb(lo_nm, lo_nm + 128), writes=["rowbuf"])
            for bi, (kt, m) in enumerate(blocks):
                lo_st = 2048 + (bi % 2) * 512
                ksS = psb(lo_st, lo_st + 128)
                if dh == 64:
                    e("tensor", lambda s, nc, hb=hb, hp=hp, kt=kt, lo_st=lo_st:
                        s.matmul(b.ps[:, lo_st:lo_st + 128],
                                 b.bigB[hb:hb + 64,
                                        hp * 2048 + kt * 128:
                                        hp * 2048 + (kt + 1) * 128],
                                 b.qcol[hb:hb + 64, hp, 0:128],
                                 start=True, stop=False),
                      reads=["bigB", "qcol"], writes=ksS)
                else:
                    for dc in range(NC):
                        e("tensor", lambda s, nc, dc=dc, kt=kt, lo_st=lo_st:
                            s.matmul(b.ps[:, lo_st:lo_st + 128],
                                     b.bigB[:, dc * 2048 + kt * 128:
                                            dc * 2048 + (kt + 1) * 128],
                                     b.qcol[:, dc, 0:128],
                                     start=(dc == 0), stop=False),
                          reads=["bigB", "qcol"], writes=ksS)
                e("tensor", lambda s, nc, hb=hb, lo_st=lo_st: s.matmul(
                    b.ps[:, lo_st:lo_st + 128],
                    b.ones_bf[hb:hb + 1, 0:128],
                    b.rowbuf[hb:hb + 1, 0:128],
                    start=False, stop=True),
                  reads=["ones_bf", "rowbuf"], writes=ksS)
                if m is not None:
                    mid = mask_ids[(cfg.name, qt, kt, "st")]
                    e("vector", lambda s, nc, mid=mid, lo_st=lo_st:
                        s.tensor_tensor(b.ps[:, lo_st:lo_st + 128],
                                        b.ps[:, lo_st:lo_st + 128],
                                        b.masks[:, mid, :], op=ALU.add),
                      reads=ksS + ["masks"], writes=ksS)
                e("scalar", lambda s, nc, bi=bi, lo_st=lo_st: s.activation(
                    b.pt[:, bi % 2, 0:128], b.ps[:, lo_st:lo_st + 128], AF.Exp),
                  reads=ksS, writes=[f"pt{bi % 2}"])
                if dh == 64:
                    e("tensor", lambda s, nc, bi=bi, kt=kt, h=h, nbl=nbl:
                        s.matmul(b.ps[:, 3072:3136],
                                 b.pt[:, bi % 2, 0:128],
                                 b.bigC[:, kt * 1024 + h * 64:
                                        kt * 1024 + (h + 1) * 64],
                                 start=(bi == 0), stop=(bi == nbl - 1)),
                      reads=[f"pt{bi % 2}", "bigC"], writes=psb(3072, 3136))
                else:
                    for og in range(2):
                        e("tensor", lambda s, nc, bi=bi, kt=kt, og=og, nbl=nbl:
                            s.matmul(b.ps[:, 3072 + og * 512:
                                          3072 + og * 512 + 512],
                                     b.pt[:, bi % 2, 0:128],
                                     b.bigC[:, kt * 1024 + og * 512:
                                            kt * 1024 + (og + 1) * 512],
                                     start=(bi == 0), stop=(bi == nbl - 1)),
                          reads=[f"pt{bi % 2}", "bigC"],
                          writes=psb(3072 + og * 512, 3072 + og * 512 + 512))
            if dh == 64:
                if cfg.use_gates:
                    e("vector", lambda s, nc, h=h, qt=qt: s.tensor_scalar(
                        b.scrb[:, h * 64:(h + 1) * 64], b.ps[:, 3072:3136],
                        b.stat[:, 29:30], b.gates[:, qt, 2 * h:2 * h + 1],
                        op0=ALU.mult, op1=ALU.mult),
                      reads=psb(3072, 3136) + ["stat", "gates"],
                      writes=["scrb"])
                else:
                    e("vector", lambda s, nc, h=h: s.tensor_scalar(
                        b.scrb[:, h * 64:(h + 1) * 64], b.ps[:, 3072:3136],
                        b.stat[:, 29:30], None, op0=ALU.mult),
                      reads=psb(3072, 3136) + ["stat"], writes=["scrb"])
            else:
                for og in range(2):
                    e("vector", lambda s, nc, og=og: s.tensor_scalar(
                        b.scrb[:, og * 512:(og + 1) * 512],
                        b.ps[:, 3072 + og * 512:3072 + og * 512 + 512],
                        b.stat[:, 29:30], None, op0=ALU.mult),
                      reads=psb(3072 + og * 512, 3072 + og * 512 + 512)
                      + ["stat"], writes=["scrb"])
        for c in range(NC):
            lo = 2048 + (c % 2) * 512
            e("tensor", lambda s, nc, c=c, lo=lo: s.matmul(
                b.ps[:, lo:lo + 128], b.scrb[:, c * 128:(c + 1) * 128],
                b.ident[:], start=True, stop=True),
              reads=["scrb", "ident"], writes=psb(lo, lo + 128))
            e("scalar", lambda s, nc, c=c, qt=qt, lo=lo: s.copy(
                bigA_fm(b, c, qt * 128, (qt + 1) * 128), b.ps[:, lo:lo + 128]),
              reads=psb(lo, lo + 128), writes=["bigA"])

    load_wres(p, b, dram[cfg.wo])
    if cfg.bo_row:
        e("gpsimd", lambda s, nc: s.dma_start(
            out=b.biasrow[0:1, 0:D], in_=dram[cfg.bo_row][:]),
          writes=["biasrow"], dma=True)
    for qt in range(NT):
        for og in range(2):
            lo = 3072 + og * 512
            ks = psb(lo, lo + 512)
            for dc in range(NC):
                e("tensor", lambda s, nc, qt=qt, og=og, dc=dc, lo=lo: s.matmul(
                    b.ps[:, lo:lo + 512],
                    bigA_fm(b, dc, qt * 128, (qt + 1) * 128),
                    b.wres[:, dc, og * 512:(og + 1) * 512],
                    start=(dc == 0),
                    stop=(dc == NC - 1 and not cfg.bo_row)),
                  reads=["bigA", "wres"], writes=ks)
            if cfg.bo_row:
                e("tensor", lambda s, nc, og=og, lo=lo: s.matmul(
                    b.ps[:, lo:lo + 512], b.ones_bf[0:1, 0:128],
                    b.biasrow[0:1, og * 512:(og + 1) * 512],
                    start=False, stop=True),
                  reads=["ones_bf", "biasrow"], writes=ks)
            e("vector", lambda s, nc, qt=qt, og=og, lo=lo: s.tensor_tensor(
                b.x_res[:, qt, og * 512:(og + 1) * 512], b.ps[:, lo:lo + 512],
                b.x_res[:, qt, og * 512:(og + 1) * 512], op=ALU.add),
              reads=ks + ["x_res"], writes=["x_res"])



def emit_gamma(p, b, w_dram, nrc, ncc, thr_sl):
    e = p.emit
    assert nrc * ncc == 64
    gview = b.gates[:, 0:16, 0:32].rearrange("p a c -> p (a c)")
    nch = 0
    for rc in range(nrc):
        for cc in range(ncc):
            st32, k32 = (b.scr32, "scr32") if nch % 2 == 0 else (gview,
                                                                 "gates")
            e("sync", lambda s, nc, rc=rc, cc=cc, st32=st32: s.dma_start(
                out=st32[:, 0:512], in_=w_dram[rc * 128:(rc + 1) * 128,
                                               cc * 512:(cc + 1) * 512]),
              writes=[k32], dma=True)
            e("vector", lambda s, nc, nch=nch, st32=st32: s.tensor_reduce(
                b.stat[:, nch:nch + 1], st32[:, 0:512], axis=AX.X,
                op=ALU.add, apply_absolute_value=True),
              reads=[k32], writes=["stat"])
            nch += 1
    e("vector", lambda s, nc: s.tensor_reduce(
        b.small[:, 34:35], b.stat[:, 0:64], axis=AX.X, op=ALU.add),
      reads=["stat"], writes=["small"])
    e("tensor", lambda s, nc: s.matmul(b.ps[0:1, 2048:2049], b.small[:, 34:35],
                                       b.ones32[0:128, 0:1],
                                       start=True, stop=True),
      reads=["small", "ones32"], writes=psb(2048, 2049))
    n_el = nrc * 128 * ncc * 512
    e("scalar", lambda s, nc, n_el=n_el: s.activation(
        b.small[0:1, 35:36], b.ps[0:1, 2048:2049], AF.Copy,
        bias=0.5e-6, scale=float(0.5 / n_el)),
      reads=psb(2048, 2049), writes=["small"])
    e("tensor", lambda s, nc: s.matmul(b.ps[:, 2048:2049],
                                       b.ones32[0:1, 0:128],
                                       b.small[0:1, 35:36],
                                       start=True, stop=True),
      reads=["ones32", "small"], writes=psb(2048, 2049))
    e("vector", lambda s, nc, thr_sl=thr_sl: s.tensor_copy(
        b.small[:, thr_sl:thr_sl + 1], b.ps[:, 2048:2049]),
      reads=psb(2048, 2176), writes=["small"])
    e("vector", lambda s, nc, thr_sl=thr_sl: s.tensor_scalar(
        b.small[:, thr_sl + 1:thr_sl + 2], b.small[:, thr_sl:thr_sl + 1],
        -1.0, None, op0=ALU.mult),
      reads=["small"], writes=["small"])


def emit_l0f_prep(p, b, dram, scr):
    e = p.emit
    emit_gamma(p, b, dram["L0f_wfcT"], 8, 8, 36)
    emit_gamma(p, b, dram["L0f_wprT"], 32, 2, 38)
    e("sync", lambda s, nc: s.dma_start(out=b.small[:, 0:32],
                                        in_=dram["L0f_lnw"][:]),
      writes=["small"], dma=True)
    e("gpsimd", lambda s, nc: s.dma_start(out=b.gwb[:, 0:32],
                                          in_=dram["L0f_lnb_fm"][:]),
      writes=["gwb"], dma=True)
    gview = b.gates[:, 0:16, 0:32].rearrange("p a c -> p (a c)")
    for rc in range(8):
        for cc in range(8):
            par = (rc * 8 + cc) % 2
            st32, k32 = (b.scr32, "scr32") if par == 0 else (gview, "gates")
            sa, ka = b.wslot(2 * par)
            sb_, kb = b.wslot(2 * par + 1)
            e("sync", lambda s, nc, rc=rc, cc=cc, st32=st32: s.dma_start(
                out=st32[:, 0:512],
                in_=dram["L0f_wfcT"][rc * 128:(rc + 1) * 128,
                                     cc * 512:(cc + 1) * 512]),
              writes=[k32], dma=True)
            e("vector", lambda s, nc, st32=st32, sa=sa: s.tensor_scalar(
                sa[:, 0:512], st32[:, 0:512], b.small[:, 36:37], None,
                op0=ALU.is_gt), reads=[k32, "small"], writes=[ka])
            e("vector", lambda s, nc, st32=st32, sa=sa: s.tensor_scalar(
                sa[:, 512:1024], st32[:, 0:512], b.small[:, 37:38], None,
                op0=ALU.is_lt), reads=[k32, "small"], writes=[ka])
            e("vector", lambda s, nc, sa=sa, sb_=sb_: s.tensor_tensor(
                sb_[:, 0:512], sa[:, 0:512], sa[:, 512:1024],
                op=ALU.subtract), reads=[ka], writes=[kb])
            e("sync", lambda s, nc, rc=rc, cc=cc, sb_=sb_: s.dma_start(
                out=scr["fcq"][rc * 128:(rc + 1) * 128,
                               cc * 512:(cc + 1) * 512],
                in_=sb_[:, 0:512]),
              reads=[kb], writes=["fcq"], dma=True)
    for rc in range(32):
        par = rc % 2
        sa, ka = b.wslot(2 * par)
        sb_, kb = b.wslot(2 * par + 1)
        for hf in range(2):
            st32, k32 = (b.scr32, "scr32") if hf == 0 else (gview, "gates")
            e("sync", lambda s, nc, rc=rc, hf=hf, st32=st32: s.dma_start(
                out=st32[:, 0:512],
                in_=dram["L0f_wprT"][rc * 128:(rc + 1) * 128,
                                     hf * 512:(hf + 1) * 512]),
              writes=[k32], dma=True)
            e("vector", lambda s, nc, st32=st32, sa=sa: s.tensor_scalar(
                sa[:, 0:512], st32[:, 0:512], b.small[:, 38:39], None,
                op0=ALU.is_gt), reads=[k32, "small"], writes=[ka])
            e("vector", lambda s, nc, st32=st32, sa=sa: s.tensor_scalar(
                sa[:, 512:1024], st32[:, 0:512], b.small[:, 39:40], None,
                op0=ALU.is_lt), reads=[k32, "small"], writes=[ka])
            e("vector", lambda s, nc, hf=hf, sa=sa, sb_=sb_: s.tensor_tensor(
                sb_[:, hf * 512:(hf + 1) * 512], sa[:, 0:512],
                sa[:, 512:1024],
                op=ALU.subtract), reads=[ka], writes=[kb])
        for half in range(2):
            e("tensor", lambda s, nc, rc=rc, half=half, sb_=sb_: s.matmul(
                b.ps[0:1, 3072 + half * 512: 3072 + (half + 1) * 512],
                b.gwb[:, rc:rc + 1],
                sb_[:, half * 512:(half + 1) * 512],
                start=(rc == 0), stop=(rc == 31)),
              reads=["gwb", kb],
              writes=psb(3072 + half * 512, 3072 + (half + 1) * 512))
        e("vector", lambda s, nc, rc=rc, sa=sa, sb_=sb_: s.tensor_scalar(
            sa[:, 0:1024], sb_[:, 0:1024], b.small[:, rc:rc + 1],
            None, op0=ALU.mult),
          reads=[kb, "small"], writes=[ka])
        e("sync", lambda s, nc, rc=rc, sa=sa: s.dma_start(
            out=scr["prq"][rc * 128:(rc + 1) * 128, :], in_=sa[:, 0:1024]),
          reads=[ka], writes=["prq"], dma=True)
    e("vector", lambda s, nc: s.tensor_copy(b.scrb[0:1, 0:512],
                                            b.ps[0:1, 3072:3584]),
      reads=psb(3072, 3584), writes=["scrb"])
    e("vector", lambda s, nc: s.tensor_copy(b.scrb[0:1, 512:1024],
                                            b.ps[0:1, 3584:4096]),
      reads=psb(3584, 4096), writes=["scrb"])
    e("sync", lambda s, nc: s.dma_start(out=scr["l0f_vec"][0:1, :],
                                        in_=b.scrb[0:1, 0:1024]),
      reads=["scrb"], writes=["l0f_vec"], dma=True)


def emit_l0_ffn(p, b, scr):
    e = p.emit
    st = b.stat
    e("sync", lambda s, nc: s.dma_start(out=b.biasrow[0:1, 0:1024],
                                        in_=scr["l0f_vec"][0:1, :]),
      reads=["l0f_vec"], writes=["biasrow"], dma=True)
    for tg in range(4):
        for hg in range(8):
            for dc in range(NC):
                e("sync", lambda s, nc, dc=dc, hg=hg: s.dma_start(
                    out=b.wres[:, dc, 0:512],
                    in_=scr["fcq"][dc * 128:(dc + 1) * 128,
                                   hg * 512:(hg + 1) * 512]),
                  reads=["fcq"], writes=["wres"], dma=True)
            for ti in range(4):
                t = tg * 4 + ti
                lo = (ti % 2) * 512
                ks = psb(lo, lo + 512)
                for dc in range(NC):
                    e("tensor", lambda s, nc, dc=dc, t=t, lo=lo: s.matmul(
                        b.ps[:, lo:lo + 512],
                        bigA_fm(b, dc, t * 128, (t + 1) * 128),
                        b.wres[:, dc, 0:512],
                        start=(dc == 0), stop=(dc == NC - 1)),
                      reads=["bigA", "wres"], writes=ks)
                e("scalar", lambda s, nc, ti=ti, hg=hg, lo=lo: s.activation(
                    b.bigB[:, ti * 4096 + hg * 512: ti * 4096 + (hg + 1) * 512],
                    b.ps[:, lo:lo + 512], AF.Gelu),
                  reads=ks, writes=["bigB"])
        for ti in range(4):
            for i8 in range(8):
                e("vector", lambda s, nc, ti=ti, i8=i8: s.bn_stats(
                    st[:, 6 * i8:6 * i8 + 6],
                    b.bigB[:, ti * 4096 + i8 * 512: ti * 4096 + (i8 + 1) * 512]),
                  reads=["bigB"], writes=["stat"])
            e("vector", lambda s, nc: s.bn_aggr(
                st[:, 48:50], st[:, 0:48].rearrange("p (a b) -> p a b", a=8)),
              reads=["stat"], writes=["stat"])
            e("vector", lambda s, nc: s.tensor_scalar(
                st[:, 50:51], st[:, 49:50], 1e-5, None, op0=ALU.add),
              reads=["stat"], writes=["stat"])
            e("scalar", lambda s, nc: s.activation(st[:, 51:52], st[:, 50:51],
                                                   AF.Sqrt),
              reads=["stat"], writes=["stat"])
            e("vector", lambda s, nc: s.reciprocal(st[:, 52:53], st[:, 51:52]),
              reads=["stat"], writes=["stat"])
            for cc in range(4):
                e("vector", lambda s, nc, ti=ti, cc=cc: s.tensor_scalar(
                    b.scrb[:, 0:1024],
                    b.bigB[:, ti * 4096 + cc * 1024: ti * 4096 + (cc + 1) * 1024],
                    st[:, 48:49], st[:, 52:53],
                    op0=ALU.subtract, op1=ALU.mult),
                  reads=["bigB", "stat"], writes=["scrb"])
                for ci in range(8):
                    hc = cc * 8 + ci
                    lo = 2048 + (ci % 2) * 512
                    e("tensor", lambda s, nc, ci=ci, lo=lo: s.matmul(
                        b.ps[:, lo:lo + 128],
                        b.scrb[:, ci * 128:(ci + 1) * 128], b.ident[:],
                        start=True, stop=True),
                      reads=["scrb", "ident"], writes=psb(lo, lo + 128))
                    e("scalar", lambda s, nc, hc=hc, ti=ti, lo=lo: s.copy(
                        b.bigC[:, hc * 512 + ti * 128: hc * 512 + (ti + 1) * 128],
                        b.ps[:, lo:lo + 128]),
                      reads=psb(lo, lo + 128), writes=["bigC"])
        for dg in range(2):
            for hc in range(32):
                wslot, wkey = b.wslot(dg * 32 + hc)
                e("sync", lambda s, nc, hc=hc, dg=dg, wslot=wslot: s.dma_start(
                    out=wslot[:, 0:512],
                    in_=scr["prq"][hc * 128:(hc + 1) * 128,
                                   dg * 512:(dg + 1) * 512]),
                  reads=["prq"], writes=[wkey], dma=True)
                for ti in range(4):
                    bank = dg * 4 + ti
                    e("tensor", lambda s, nc, hc=hc, ti=ti, bank=bank,
                        wslot=wslot, wkey=wkey: s.matmul(
                        b.ps[:, bank * 512:(bank + 1) * 512],
                        b.bigC[:, hc * 512 + ti * 128:
                               hc * 512 + (ti + 1) * 128],
                        wslot[:, 0:512],
                        start=(hc == 0), stop=False),
                      reads=["bigC", wkey],
                      writes=psb(bank * 512, (bank + 1) * 512))
            for ti in range(4):
                bank = dg * 4 + ti
                e("tensor", lambda s, nc, dg=dg, bank=bank: s.matmul(
                    b.ps[:, bank * 512:(bank + 1) * 512],
                    b.ones_bf[0:1, 0:128],
                    b.biasrow[0:1, dg * 512:(dg + 1) * 512],
                    start=False, stop=True),
                  reads=["ones_bf", "biasrow"],
                  writes=psb(bank * 512, (bank + 1) * 512))
                t = tg * 4 + ti
                e("vector", lambda s, nc, t=t, dg=dg, bank=bank: s.tensor_tensor(
                    b.x_res[:, t, dg * 512:(dg + 1) * 512],
                    b.ps[:, bank * 512:(bank + 1) * 512],
                    b.x_res[:, t, dg * 512:(dg + 1) * 512], op=ALU.add),
                  reads=psb(bank * 512, (bank + 1) * 512) + ["x_res"],
                  writes=["x_res"])



def emit_l1f_prep(p, b, dram, scr):
    e = p.emit
    jobs = [("L1f_w1T", "w1q", 8, 4096), ("L1f_w2T", "w2q", 8, 4096),
            ("L1f_h1T", "h1q", 32, 4096), ("L1f_h2T", "h2q", 32, 4096),
            ("L1f_w3T", "w3q", 32, 1024)]
    n = 0
    for src, dst, nrc, width in jobs:
        for rc in range(nrc):
            for cc in range(width // 1024):
                slot, key = b.wslot(n)
                n += 1
                e("gpsimd", lambda s, nc, src=src, rc=rc, cc=cc,
                    slot=slot: s.dma_start(
                    out=slot[:, 0:1024],
                    in_=dram[src][rc * 128:(rc + 1) * 128,
                                  cc * 1024:cc * 1024 + 1024]),
                  writes=[key], dma=True)
                e("sync", lambda s, nc, dst=dst, rc=rc, cc=cc,
                    slot=slot: s.dma_start(
                    out=scr[dst][rc * 128:(rc + 1) * 128,
                                 cc * 1024:cc * 1024 + 1024],
                    in_=slot[:, 0:1024]),
                  reads=[key], writes=[dst], dma=True)


def emit_l1_ffn(p, b, dram, scr):
    e = p.emit
    for i, k in enumerate(["L1f_s1", "L1f_s2", "L1f_sh1", "L1f_sh2"]):
        e("sync", lambda s, nc, i=i, k=k: s.dma_start(
            out=b.scr32[:, 32 * i:32 * (i + 1)], in_=dram[k][:]),
          writes=["scr32"], dma=True)
    e("sync", lambda s, nc: s.dma_start(out=b.small[:, 0:8],
                                        in_=dram["L1f_s3fm"][:]),
      writes=["small"], dma=True)

    def mm_block(scrname, in_key, in_view, out_cb, s_off, n_ic, tg,
                 nog=4, width_o=4096):
        for ocg in range(nog):
            for ic in range(n_ic):
                slot, key = b.wslot(ocg * n_ic + ic)
                e("sync", lambda s, nc, scrname=scrname, ic=ic, ocg=ocg,
                    slot=slot: s.dma_start(
                    out=slot[:, 0:1024],
                    in_=scr[scrname][ic * 128:(ic + 1) * 128,
                                     ocg * 1024:(ocg + 1) * 1024]),
                  reads=[scrname], writes=[key], dma=True)
                for ob in range(8):
                    e("tensor", lambda s, nc, ic=ic, ob=ob, slot=slot,
                        key=key: s.matmul(
                        b.ps[:, ob * 512:(ob + 1) * 512],
                        slot[:, ob * 128:(ob + 1) * 128],
                        in_view(ic),
                        start=(ic == 0), stop=(ic == n_ic - 1)),
                      reads=[key, in_key],
                      writes=psb(ob * 512, (ob + 1) * 512))
            for ob in range(8):
                out_cb(ocg * 8 + ob, ob)

    for tg in range(4):
        def evac_silu(oc, bank):
            e("scalar", lambda s, nc, oc=oc, bank=bank: s.activation(
                b.bigB[:, oc * 512:(oc + 1) * 512],
                b.ps[:, bank * 512:(bank + 1) * 512], AF.Silu,
                scale=b.scr32[:, oc:oc + 1]),
              reads=psb(bank * 512, (bank + 1) * 512) + ["scr32"],
              writes=["bigB"])
        mm_block("w1q", "bigA",
                 lambda ic, tg=tg: bigA_fm(b, ic, tg * 512, (tg + 1) * 512),
                 evac_silu, 0, NC, tg)
        def evac_h(oc, bank):
            e("vector", lambda s, nc, oc=oc, bank=bank: s.scalar_tensor_tensor(
                b.bigC[:, oc * 512:(oc + 1) * 512],
                b.ps[:, bank * 512:(bank + 1) * 512],
                b.scr32[:, 32 + oc:33 + oc],
                b.bigB[:, oc * 512:(oc + 1) * 512],
                op0=ALU.mult, op1=ALU.mult),
              reads=psb(bank * 512, (bank + 1) * 512) + ["scr32", "bigB"],
              writes=["bigC"])
        mm_block("w2q", "bigA",
                 lambda ic, tg=tg: bigA_fm(b, ic, tg * 512, (tg + 1) * 512),
                 evac_h, 32, NC, tg)
        def evac_g1(oc, bank):
            e("scalar", lambda s, nc, oc=oc, bank=bank: s.activation(
                b.bigB[:, oc * 512:(oc + 1) * 512],
                b.ps[:, bank * 512:(bank + 1) * 512], AF.Gelu,
                scale=b.scr32[:, 64 + oc:65 + oc]),
              reads=psb(bank * 512, (bank + 1) * 512) + ["scr32"],
              writes=["bigB"])
        mm_block("h1q", "bigC",
                 lambda ic: b.bigC[:, ic * 512:(ic + 1) * 512],
                 evac_g1, 64, 32, tg)
        def evac_g2(oc, bank):
            e("scalar", lambda s, nc, oc=oc, bank=bank: s.activation(
                b.bigC[:, oc * 512:(oc + 1) * 512],
                b.ps[:, bank * 512:(bank + 1) * 512], AF.Gelu,
                scale=b.scr32[:, 96 + oc:97 + oc]),
              reads=psb(bank * 512, (bank + 1) * 512) + ["scr32"],
              writes=["bigC"])
        mm_block("h2q", "bigB",
                 lambda ic: b.bigB[:, ic * 512:(ic + 1) * 512],
                 evac_g2, 96, 32, tg)
        def evac_w3(dcb, bank):
            e("vector", lambda s, nc, dcb=dcb, bank=bank: s.tensor_scalar(
                b.scrb[:, 0:512], b.ps[:, bank * 512:(bank + 1) * 512],
                b.small[:, dcb:dcb + 1], None, op0=ALU.mult),
              reads=psb(bank * 512, (bank + 1) * 512) + ["small"],
              writes=["scrb"])
            for ti in range(4):
                t = tg * 4 + ti
                lo = bank * 512
                e("tensor", lambda s, nc, ti=ti, lo=lo: s.matmul(
                    b.ps[:, lo:lo + 128],
                    b.scrb[:, ti * 128:(ti + 1) * 128], b.ident[:],
                    start=True, stop=True),
                  reads=["scrb", "ident"], writes=psb(lo, lo + 128))
                e("vector", lambda s, nc, t=t, dcb=dcb, lo=lo: s.tensor_tensor(
                    b.x_res[:, t, dcb * 128:(dcb + 1) * 128],
                    b.ps[:, lo:lo + 128],
                    b.x_res[:, t, dcb * 128:(dcb + 1) * 128], op=ALU.add),
                  reads=psb(lo, lo + 128) + ["x_res"], writes=["x_res"])
        mm_block("w3q", "bigC",
                 lambda ic: b.bigC[:, ic * 512:(ic + 1) * 512],
                 evac_w3, 0, 32, tg, nog=1, width_o=1024)



def emit_l1b(p, b, dram, mask_ids):
    e = p.emit
    emit_ln_xhatT(p, b)
    load_wres(p, b, dram["L1b_wq"])
    e("gpsimd", lambda s, nc: s.dma_start(out=b.gwb[:, 0:32],
                                          in_=dram["L1b_gw"][:]),
      writes=["gwb"], dma=True)
    for qt in range(NT):
        emit_qcol(p, b, qt * 128, 128, bias_sl=None, scale=1.0)
        for h in range(16):
            hb, hp = 64 * (h % 2), h // 2
            for gi in range(2):
                e("tensor", lambda s, nc, h=h, hb=hb, hp=hp, gi=gi: s.matmul(
                    b.ps[:, 2048 + h * 2 + gi: 2048 + h * 2 + gi + 1],
                    b.qcol[hb:hb + 64, hp, 0:128],
                    b.gwb[hb:hb + 64, 16 * gi + h:16 * gi + h + 1],
                    start=True, stop=True),
                  reads=["qcol", "gwb"], writes=psb(2048, 2560))
        e("scalar", lambda s, nc, qt=qt: s.activation(
            b.gates[:, qt, :], b.ps[:, 2048:2080], AF.Sigmoid),
          reads=psb(2048, 2560), writes=["gates"])
    load_wres(p, b, dram["L1b_wgq"], width=512)
    for oc in range(4):
        for tg in range(4):
            lo = (tg % 2) * 512
            ks = psb(lo, lo + 512)
            for dc in range(NC):
                e("tensor", lambda s, nc, oc=oc, tg=tg, dc=dc, lo=lo: s.matmul(
                    b.ps[:, lo:lo + 512],
                    b.wres[:, dc, oc * 128:(oc + 1) * 128],
                    bigA_fm(b, dc, tg * 512, (tg + 1) * 512),
                    start=(dc == 0), stop=(dc == NC - 1)),
                  reads=["wres", "bigA"], writes=ks)
            e("vector", lambda s, nc, oc=oc, tg=tg, lo=lo: s.tensor_scalar(
                b.bigB[:, oc * 2048 + tg * 512: oc * 2048 + (tg + 1) * 512],
                b.ps[:, lo:lo + 512], 0.125, None, op0=ALU.mult),
              reads=psb(lo, lo + 512), writes=["bigB"])
    for which, wname, dst0 in (("gk", "L1b_wgk", 0), ("gv", "L1b_wgv", 256)):
        load_wres(p, b, dram[wname], width=512)
        for oc in range(4):
            for tg in range(4):
                lo = (tg % 2) * 512
                ks = psb(lo, lo + 512)
                for dc in range(NC):
                    e("tensor", lambda s, nc, oc=oc, tg=tg, dc=dc, lo=lo:
                        s.matmul(b.ps[:, lo:lo + 512],
                                 b.wres[:, dc, oc * 128:(oc + 1) * 128],
                                 bigA_fm(b, dc, tg * 512, (tg + 1) * 512),
                                 start=(dc == 0), stop=(dc == NC - 1)),
                      reads=["wres", "bigA"], writes=ks)
                e("vector", lambda s, nc, oc=oc, tg=tg, lo=lo: s.tensor_copy(
                    b.bigC[:, oc * 2048 + tg * 512:
                           oc * 2048 + (tg + 1) * 512],
                    b.ps[:, lo:lo + 512]),
                  reads=psb(lo, lo + 512), writes=["bigC"])
        for oc in range(4):
            e("vector", lambda s, nc, oc=oc, dst0=dst0: s.tensor_reduce(
                b.scr32[:, dst0 + oc * 64: dst0 + (oc + 1) * 64],
                b.bigC[:, oc * 2048:(oc + 1) * 2048]
                .rearrange("p (n w) -> p n w", w=32),
                axis=AX.X, op=ALU.add),
              reads=["bigC"], writes=["scr32"])
        e("vector", lambda s, nc, dst0=dst0: s.tensor_scalar(
            b.scrb[:, dst0:dst0 + 256], b.scr32[:, dst0:dst0 + 256],
            1.0 / 32.0, None, op0=ALU.mult),
          reads=["scr32"], writes=["scrb"])
    for lh in range(8):
        lhb, lhp = 64 * (lh % 2), lh // 2
        e("tensor", lambda s, nc, lhb=lhb, lhp=lhp: s.matmul(
            b.ps[0:64, 2560:2624],
            b.scrb[lhb:lhb + 64, 256 + lhp * 64: 256 + (lhp + 1) * 64],
            b.ident[lhb:lhb + 64, lhb:lhb + 64],
            start=True, stop=True),
          reads=["scrb", "ident"], writes=psb(2560, 2688))
        e("vector", lambda s, nc: s.tensor_copy(b.pt[0:64, 1, 0:64],
                                                b.ps[0:64, 2560:2624]),
          reads=psb(2560, 2688), writes=["pt1"])
        for stile in range(NT):
            e("tensor", lambda s, nc, lhb=lhb, lhp=lhp, stile=stile: s.matmul(
                b.ps[:, 2048:2112],
                b.bigB[lhb:lhb + 64, lhp * 2048 + stile * 128:
                       lhp * 2048 + (stile + 1) * 128],
                b.scrb[lhb:lhb + 64, lhp * 64:(lhp + 1) * 64],
                start=True, stop=True),
              reads=["bigB", "scrb"], writes=psb(2048, 2112))
            if 4 * stile < 64:
                e("vector", lambda s, nc, stile=stile: s.tensor_tensor(
                    b.ps[:, 2048 + 4 * stile:2112],
                    b.ps[:, 2048 + 4 * stile:2112],
                    b.lmask[:, 0:64 - 4 * stile], op=ALU.add),
                  reads=psb(2048, 2112) + ["lmask"], writes=psb(2048, 2112))
            e("vector", lambda s, nc: s.tensor_reduce(
                b.stat[:, 20:21], b.ps[:, 2048:2112], axis=AX.X,
                op=ALU.max, negate=True),
              reads=psb(2048, 2112), writes=["stat"])
            e("scalar", lambda s, nc: s.activation(
                b.scrb[:, 512:576], b.ps[:, 2048:2112], AF.Exp,
                bias=b.stat[:, 20:21], scale=1.0,
                accum_out=b.stat[:, 24:25]),
              reads=psb(2048, 2112) + ["stat"], writes=["scrb", "stat"])
            e("vector", lambda s, nc: s.reciprocal(b.stat[:, 29:30],
                                                   b.stat[:, 24:25]),
              reads=["stat"], writes=["stat"])
            e("tensor", lambda s, nc: s.matmul(
                b.ps[0:64, 2560:2688], b.scrb[:, 512:576], b.ident[:],
                start=True, stop=True),
              reads=["scrb", "ident"], writes=psb(2560, 2688))
            e("vector", lambda s, nc: s.tensor_copy(b.pt[0:64, 0, 0:128],
                                                    b.ps[0:64, 2560:2688]),
              reads=psb(2560, 2688), writes=["pt0"])
            e("tensor", lambda s, nc: s.matmul(
                b.ps[:, 3072:3136], b.pt[0:64, 0, 0:128], b.pt[0:64, 1, 0:64],
                start=True, stop=True),
              reads=["pt0", "pt1"], writes=psb(3072, 3136))
            e("vector", lambda s, nc, lh=lh, stile=stile: s.tensor_scalar(
                b.bigB[:, 8192 + stile * 512 + lh * 64:
                       8192 + stile * 512 + (lh + 1) * 64],
                b.ps[:, 3072:3136], b.stat[:, 29:30], None, op0=ALU.mult),
              reads=psb(3072, 3136) + ["stat"], writes=["bigB"])
    load_wres(p, b, dram["L1b_rout"], ncol=4)
    for stile in range(NT):
        for lc in range(4):
            lo = 2048 + (lc % 2) * 512
            e("tensor", lambda s, nc, stile=stile, lc=lc, lo=lo: s.matmul(
                b.ps[:, lo:lo + 128],
                b.bigB[:, 8192 + stile * 512 + lc * 128:
                       8192 + stile * 512 + (lc + 1) * 128],
                b.ident[:], start=True, stop=True),
              reads=["bigB", "ident"], writes=psb(lo, lo + 128))
            e("scalar", lambda s, nc, lc=lc, lo=lo: s.copy(
                b.qcol[:, lc, 0:128], b.ps[:, lo:lo + 128]),
              reads=psb(lo, lo + 128), writes=["qcol"])
        for og in range(2):
            lo = 3072 + og * 512
            ks = psb(lo, lo + 512)
            for lc in range(4):
                e("tensor", lambda s, nc, lc=lc, og=og, lo=lo: s.matmul(
                    b.ps[:, lo:lo + 512], b.qcol[:, lc, 0:128],
                    b.wres[:, lc, og * 512:(og + 1) * 512],
                    start=(lc == 0), stop=(lc == 3)),
                  reads=["qcol", "wres"], writes=ks)
            for hh in range(8):
                h = og * 8 + hh
                e("vector", lambda s, nc, stile=stile, h=h, hh=hh, lo=lo:
                    s.tensor_scalar(
                    b.bigC[:, stile * 1024 + h * 64: stile * 1024 + (h + 1) * 64],
                    b.ps[:, lo + hh * 64: lo + (hh + 1) * 64],
                    b.gates[:, stile, 2 * h + 1:2 * h + 2], None, op0=ALU.mult),
                  reads=ks + ["gates"], writes=["bigC"])
    load_wres(p, b, dram["L1b_wout"])
    for qt in range(NT):
        for c in range(NC):
            lo = 2048 + (c % 2) * 512
            e("tensor", lambda s, nc, qt=qt, c=c, lo=lo: s.matmul(
                b.ps[:, lo:lo + 128],
                b.bigC[:, qt * 1024 + c * 128: qt * 1024 + (c + 1) * 128],
                b.ident[:], start=True, stop=True),
              reads=["bigC", "ident"], writes=psb(lo, lo + 128))
            e("scalar", lambda s, nc, c=c, lo=lo: s.copy(
                b.qcol[:, c, 0:128], b.ps[:, lo:lo + 128]),
              reads=psb(lo, lo + 128), writes=["qcol"])
        for og in range(2):
            lo = 3072 + og * 512
            ks = psb(lo, lo + 512)
            for dc in range(NC):
                e("tensor", lambda s, nc, dc=dc, og=og, lo=lo: s.matmul(
                    b.ps[:, lo:lo + 512], b.qcol[:, dc, 0:128],
                    b.wres[:, dc, og * 512:(og + 1) * 512],
                    start=(dc == 0), stop=(dc == NC - 1)),
                  reads=["qcol", "wres"], writes=ks)
            e("vector", lambda s, nc, qt=qt, og=og, lo=lo: s.tensor_tensor(
                b.x_res[:, qt, og * 512:(og + 1) * 512], b.ps[:, lo:lo + 512],
                b.x_res[:, qt, og * 512:(og + 1) * 512], op=ALU.add),
              reads=ks + ["x_res"], writes=["x_res"])
    emit_attention(p, b, CFG_L1B, dram, mask_ids)



def _caus(ii, jj):
    return jj <= ii


def _longf(ii, jj):
    return (np.abs(ii - jj) <= 256) | (ii == 0) | (jj == 0)


def _local(ii, jj):
    d = ii - jj
    return (d >= 0) & (d < 256)


CFG_L0A = AttnCfg("l0a", 64, "L0a_wqT", "L0a_wkT", "L0a_wvT", "L0a_woT",
                  _caus, 0.125, bq_fm="L0a_bq", bk_fm="L0a_bk",
                  bv_row="L0a_bv", bo_row="L0a_bo")
CFG_L0B = AttnCfg("l0b", 1024, "L0b_wq", "L0b_wk", "L0b_wv", "L0b_wp",
                  _caus, 1.0 / 32.0, bq_fm="L0b_bq", bk_fm="L0b_bk",
                  bv_row="L0b_bv", bo_row="L0b_bp")
CFG_L1A = AttnCfg("l1a", 64, "L1a_wq", "L1a_wk", "L1a_wv", "L1a_wo",
                  _longf, 0.125, bq_fm="L1a_bq", bk_fm="L1a_bk",
                  bv_row="L1a_bv", bo_row="L1a_bo")
CFG_L1B = AttnCfg("l1b", 64, "L1b_wq", "L1b_wk", "L1b_wv", "L1b_wout",
                  _local, 0.125, use_gates=True)

ALL_CFGS = [CFG_L0A, CFG_L0B, CFG_L1A, CFG_L1B]
MASK_TILES, MASK_IDS = collect_masks(ALL_CFGS)


def fm_pack(v):
    return np.ascontiguousarray(v.reshape(8, 128).T).astype(np.float32)


def host_arrays(inputs):
    a = {}
    t = lambda w: np.ascontiguousarray(np.asarray(w).T)
    asf = lambda w: np.ascontiguousarray(np.asarray(w), dtype=np.float32)
    a["c_ident"] = np.eye(128, dtype=BFD)
    a["c_masks"] = (np.stack(MASK_TILES, axis=1).reshape(128, -1)
                    if MASK_TILES else np.zeros((128, 128), np.float32))
    ss = np.arange(128)[:, None]
    mm = np.arange(64)[None, :]
    a["c_lmask"] = np.where(mm <= ss // 32, 0.0, NEG).astype(np.float32)
    a["L0a_wqT"] = t(inputs["L0_attn_a_Wq"])
    a["L0a_wkT"] = t(inputs["L0_attn_a_Wk"])
    a["L0a_wvT"] = t(inputs["L0_attn_a_Wv"])
    a["L0a_woT"] = t(inputs["L0_attn_a_Wo"])
    a["L0a_bq"] = fm_pack(np.asarray(inputs["L0_attn_a_bq"]))
    a["L0a_bk"] = fm_pack(np.asarray(inputs["L0_attn_a_bk"]))
    a["L0a_bv"] = asf(inputs["L0_attn_a_bv"]).reshape(1, D)
    a["L0a_bo"] = asf(inputs["L0_attn_a_bo"]).reshape(1, D)
    wat = np.asarray(inputs["L0_attn_b_w_attn"])
    bat = np.asarray(inputs["L0_attn_b_b_attn"])
    a["L0b_wq"] = np.ascontiguousarray(wat[:, 0:1024])
    a["L0b_wk"] = np.ascontiguousarray(wat[:, 1024:2048])
    a["L0b_wv"] = np.ascontiguousarray(wat[:, 2048:3072])
    a["L0b_bq"] = fm_pack(bat[0:1024])
    a["L0b_bk"] = fm_pack(bat[1024:2048])
    a["L0b_bv"] = asf(bat[2048:3072]).reshape(1, D)
    a["L0b_wp"] = asf(inputs["L0_attn_b_w_proj"])
    a["L0b_bp"] = asf(inputs["L0_attn_b_b_proj"]).reshape(1, D)
    a["L1a_wq"] = asf(inputs["L1_attn_a_Wq"])
    a["L1a_wk"] = asf(inputs["L1_attn_a_Wk"])
    a["L1a_wv"] = asf(inputs["L1_attn_a_Wv"])
    a["L1a_wo"] = asf(inputs["L1_attn_a_Wo"])
    a["L1a_bq"] = fm_pack(np.asarray(inputs["L1_attn_a_bq"]))
    a["L1a_bk"] = fm_pack(np.asarray(inputs["L1_attn_a_bk"]))
    a["L1a_bv"] = asf(inputs["L1_attn_a_bv"]).reshape(1, D)
    a["L1a_bo"] = asf(inputs["L1_attn_a_bo"]).reshape(1, D)
    a["L1b_wq"] = asf(inputs["L1_attn_b_wq"])
    a["L1b_wk"] = asf(inputs["L1_attn_b_wk"])
    a["L1b_wv"] = asf(inputs["L1_attn_b_wv"])
    a["L1b_wgq"] = asf(inputs["L1_attn_b_wgq"])
    a["L1b_wgk"] = asf(inputs["L1_attn_b_wgk"])
    a["L1b_wgv"] = asf(inputs["L1_attn_b_wgv"])
    a["L1b_rout"] = asf(inputs["L1_attn_b_r_out_w"])
    a["L1b_wout"] = asf(inputs["L1_attn_b_w_out"])
    gl = np.asarray(inputs["L1_attn_b_gate_local_w"])
    gr = np.asarray(inputs["L1_attn_b_gate_remote_w"])
    gw = np.zeros((128, 32), np.float32)
    for h in range(16):
        hb = 64 * (h % 2)
        gw[hb:hb + 64, h] = gl[h]
        gw[hb:hb + 64, 16 + h] = gr[h]
    a["L1b_gw"] = gw
    a["L0f_wfcT"] = t(inputs["L0_ffn_W_fc"])
    a["L0f_wprT"] = t(inputs["L0_ffn_W_proj"])
    a["L0f_lnw"] = np.ascontiguousarray(
        np.asarray(inputs["L0_ffn_ln_w"]).reshape(32, 128).T).astype(np.float32)
    a["L0f_lnb_fm"] = np.ascontiguousarray(
        np.asarray(inputs["L0_ffn_ln_b"]).reshape(32, 128).T).astype(np.float32)
    a["L1f_w1T"] = t(inputs["L1_ffn_w1_w"])
    a["L1f_w2T"] = t(inputs["L1_ffn_w2_w"])
    a["L1f_h1T"] = t(inputs["L1_ffn_h1_w"])
    a["L1f_h2T"] = t(inputs["L1_ffn_h2_w"])
    a["L1f_w3T"] = t(inputs["L1_ffn_w3_w"])
    a["L1f_s1"] = np.ascontiguousarray(
        np.asarray(inputs["L1_ffn_w1_s"]).reshape(32, 128).T).astype(np.float32)
    a["L1f_s2"] = np.ascontiguousarray(
        np.asarray(inputs["L1_ffn_w2_s"]).reshape(32, 128).T).astype(np.float32)
    a["L1f_sh1"] = np.ascontiguousarray(
        np.asarray(inputs["L1_ffn_h1_s"]).reshape(32, 128).T).astype(np.float32)
    a["L1f_sh2"] = np.ascontiguousarray(
        np.asarray(inputs["L1_ffn_h2_s"]).reshape(32, 128).T).astype(np.float32)
    a["L1f_s3fm"] = np.ascontiguousarray(
        np.asarray(inputs["L1_ffn_w3_s"]).reshape(8, 128).T).astype(np.float32)
    return a


DRAM_SPECS = None


def dram_specs(arrays):
    return {k: (list(v.shape),
                BF16 if v.dtype == BFD else F32)
            for k, v in arrays.items()}



SUBLAYERS = ["l0a", "l0f1", "l0b", "l0f2", "l1a", "l1f1", "l1b", "l1f2"]


def build_program(subs, arrays, dumps=()):
    nc = bass.Bass("TRN2", target_bir_lowering=False)
    dram = {}
    for k, (shape, dt) in dram_specs(arrays).items():
        dram[k] = nc.dram_tensor(k, shape, dt, kind="ExternalInput").ap()
    x_in = nc.dram_tensor("x_in", [S, D], F32, kind="ExternalInput").ap()
    x_out = nc.dram_tensor("x_out", [S, D], F32, kind="ExternalOutput").ap()
    scr = {}
    if any(s in subs for s in ("l0f1", "l0f2")):
        scr["fcq"] = nc.dram_tensor("scr_fcq", [1024, 4096], BF16,
                                    kind="Internal").ap()
        scr["prq"] = nc.dram_tensor("scr_prq", [4096, 1024], BF16,
                                    kind="Internal").ap()
        scr["l0f_vec"] = nc.dram_tensor("scr_l0f_vec", [1, 1024], BF16,
                                        kind="Internal").ap()
    if any(s in subs for s in ("l1f1", "l1f2")):
        scr["w1q"] = nc.dram_tensor("scr_w1q", [1024, 4096], BF16,
                                    kind="Internal").ap()
        scr["w2q"] = nc.dram_tensor("scr_w2q", [1024, 4096], BF16,
                                    kind="Internal").ap()
        scr["h1q"] = nc.dram_tensor("scr_h1q", [4096, 4096], BF16,
                                    kind="Internal").ap()
        scr["h2q"] = nc.dram_tensor("scr_h2q", [4096, 4096], BF16,
                                    kind="Internal").ap()
        scr["w3q"] = nc.dram_tensor("scr_w3q", [4096, 1024], BF16,
                                    kind="Internal").ap()
    dump_drams = {}
    with ExitStack() as stack:
        b = Bufs(nc, stack, len(MASK_TILES))
        p = Prog(nc)
        load_consts(p, b, dram, len(MASK_TILES))
        load_x(p, b, x_in)
        did_l0f_prep = did_l1f_prep = False
        for sub in subs:
            if sub == "l0a":
                emit_ln_xhatT(p, b)
                emit_attention(p, b, CFG_L0A, dram, MASK_IDS)
            elif sub == "l0b":
                emit_ln_xhatT(p, b)
                emit_attention(p, b, CFG_L0B, dram, MASK_IDS)
            elif sub == "l1a":
                emit_ln_xhatT(p, b)
                emit_attention(p, b, CFG_L1A, dram, MASK_IDS)
            elif sub in ("l0f1", "l0f2"):
                if not did_l0f_prep:
                    emit_l0f_prep(p, b, dram, scr)
                    did_l0f_prep = True
                emit_ln_xhatT(p, b)
                emit_l0_ffn(p, b, scr)
            elif sub in ("l1f1", "l1f2"):
                if not did_l1f_prep:
                    emit_l1f_prep(p, b, dram, scr)
                    did_l1f_prep = True
                emit_ln_xhatT(p, b)
                emit_l1_ffn(p, b, dram, scr)
            elif sub == "l1b":
                emit_l1b(p, b, dram, MASK_IDS)
        for name in dumps:
            t = getattr(b, name)
            shp = [t.shape[i] for i in range(len(t.shape))]
            dd = nc.dram_tensor("dump_" + name, shp,
                                t.dtype, kind="ExternalOutput").ap()
            dump_drams[name] = dd
            p.emit("sync", lambda s, nc, t=t, dd=dd: s.dma_start(
                out=dd[:], in_=t[:]), reads=[name],
                writes=["dump_" + name], dma=True)
        store_x(p, b, x_out)
        p.finalize()
    return nc


def run_subs(x_full, inputs, subs, dumps=(), cores=8):
    arrays = host_arrays(inputs)
    nc = build_program(subs, arrays, dumps)
    in_maps = []
    for c in range(cores):
        m = dict(arrays)
        m["x_in"] = np.ascontiguousarray(x_full[c], dtype=np.float32)
        in_maps.append(m)
    res = run_bass_kernel_spmd(nc, in_maps, core_ids=list(range(cores)))
    out = np.stack([r["x_out"] for r in res.results], axis=0)
    dump_vals = [{k: r["dump_" + k] for k in dumps} for r in res.results]
    return out, dump_vals


def kernel(**inputs):
    x = np.asarray(inputs["L0_attn_a_x"], dtype=np.float32)
    out, _ = run_subs(x, inputs, SUBLAYERS)
    return out


# version 09
# speedup vs baseline: 1.3731x; 1.2642x over previous
import numpy as np
import ml_dtypes
from contextlib import ExitStack

import concourse.bass as bass
import concourse.mybir as mybir
from concourse.bass_utils import run_bass_kernel_spmd

F32 = mybir.dt.float32
BF16 = mybir.dt.bfloat16
AF = mybir.ActivationFunctionType
ALU = mybir.AluOpType
AX = mybir.AxisListType

D = 1024
S = 2048
NT = S // 128
NC = D // 128
NEG = -1.0e30
BFD = ml_dtypes.bfloat16

ENGINES = ("tensor", "vector", "scalar", "gpsimd", "sync")
ENG_SEM = {"tensor": "PE", "vector": "DVE", "scalar": "ACT",
           "gpsimd": "POOL", "sync": "SP"}
DMA_SEM = {"gpsimd": "QP", "sync": "QS", "scalar": "QA", "vector": "QV"}


class Prog:

    def __init__(self, nc):
        self.nc = nc
        self.items = {e: [] for e in ENGINES}
        self.count = {}
        for sname in list(ENG_SEM.values()) + list(DMA_SEM.values()):
            self.count[sname] = 0
        self.waited = {e: {} for e in ENGINES}
        self.last_w = {}
        self.readers = {}

    def _wait(self, eng, tok):
        if tok is None:
            return
        semname, val = tok
        if eng == "tensor" and semname == "PE":
            return
        if self.waited[eng].get(semname, 0) >= val:
            return
        self.waited[eng][semname] = val
        self.items[eng].append(("wait", semname, val))

    def emit(self, eng, fn, reads=(), writes=(), dma=False):
        for k in reads:
            self._wait(eng, self.last_w.get(k))
        for k in writes:
            self._wait(eng, self.last_w.get(k))
            for t in self.readers.get(k, ()):
                self._wait(eng, t)
        if dma:
            assert writes, "dma emit needs a destination key"
            semname = "q__" + writes[0]
            self.count.setdefault(semname, 0)
        else:
            semname = ENG_SEM[eng]
        inc = 16 if dma else 1
        self.count[semname] += inc
        tok = (semname, self.count[semname])
        self.items[eng].append(("inst", fn, semname, inc))
        for k in writes:
            self.last_w[k] = tok
            self.readers[k] = []
        for k in reads:
            self.readers.setdefault(k, []).append(tok)
        return tok

    def finalize(self):
        nc = self.nc
        final_waits = [(s, v) for s, v in self.count.items() if v > 0]
        with ExitStack() as stack:
            sems = {s: stack.enter_context(nc.semaphore(s))
                    for s in self.count}
            block = stack.enter_context(nc.Block())

            def make_body(eng):
                def body(seq):
                    for it in self.items[eng]:
                        if it[0] == "wait":
                            seq.wait_ge(sems[it[1]], it[2])
                        else:
                            _, fn, semname, inc = it
                            fn(seq, nc).then_inc(sems[semname], inc)
                    if eng == "sync":
                        for sname, v in final_waits:
                            seq.wait_ge(sems[sname], v)
                return body

            for eng in ENGINES:
                getattr(block, eng)(make_body(eng))


def psb(lo, hi):
    return [f"ps:{i}" for i in range(lo // 512, (hi + 511) // 512)]


class Bufs:
    def __init__(self, nc, stack, n_masks):
        def sb(name, shape, dt):
            return stack.enter_context(nc.sbuf_tensor(name, shape, dt))
        self.x_res = sb("x_res", [128, NT, D], F32)
        self.bigA = sb("bigA", [128, 16384], BF16)
        self.bigB = sb("bigB", [128, 16384], BF16)
        self.bigC = sb("bigC", [128, 16384], BF16)
        self.wres = sb("wres", [128, 8, 1024], BF16)
        self.wt0 = sb("wt0", [128, 2048], BF16)
        self.wt1 = sb("wt1", [128, 2048], BF16)
        self.scr32 = sb("scr32", [128, 512], F32)
        self.scrb = sb("scrb", [128, 1024], BF16)
        self.qcol = sb("qcol", [128, 8, 512], BF16)
        self.biasrow = sb("biasrow", [128, 1024], BF16)
        self.lmask = sb("lmask", [128, 64], F32)
        self.gates = sb("gates", [128, NT, 32], F32)
        self.gwb = sb("gwb", [128, 32], BF16)
        self.pt = sb("pt", [128, 2, 512], BF16)
        self.nmpad = sb("nmpad", [128, 8], BF16)
        self.rowbuf = sb("rowbuf", [128, 1024], BF16)
        self.ident = sb("ident", [128, 128], BF16)
        self.ones_bf = sb("ones_bf", [128, 128], BF16)
        self.ones32 = sb("ones32", [128, 128], F32)
        self.stat = sb("stat", [128, 64], F32)
        self.small = sb("small", [128, 40], F32)
        self.bias_fm = sb("bias_fm", [128, 16], F32)
        self.masks = sb("masks", [128, max(n_masks, 1), 128], BF16)
        self.ps = stack.enter_context(nc.psum_tensor("ps", [128, 4096], F32))

    def wslot(self, i):
        t = self.wt0 if i % 2 == 0 else self.wt1
        lo = (i // 2 % 2) * 1024
        return t[:, lo:lo + 1024], f"wts{i % 4}"


def bigA_fm(b, c, t0, t1):
    return b.bigA[:, c * 2048 + t0: c * 2048 + t1]


def load_consts(p, b, dram, n_masks):
    e = p.emit
    e("sync", lambda s, nc: s.dma_start(out=b.ident[:], in_=dram["c_ident"][:]),
      writes=["ident"], dma=True)
    if n_masks:
        e("gpsimd", lambda s, nc: s.dma_start(
            out=b.masks[:, 0:n_masks, :],
            in_=dram["c_masks"][:]), writes=["masks"], dma=True)
    e("sync", lambda s, nc: s.dma_start(out=b.lmask[:], in_=dram["c_lmask"][:]),
      writes=["lmask"], dma=True)
    e("vector", lambda s, nc: s.memset(b.ones_bf[:], 1.0), writes=["ones_bf"])
    e("vector", lambda s, nc: s.memset(b.ones32[:], 1.0), writes=["ones32"])


def load_x(p, b, x_dram):
    for t in range(NT):
        p.emit("sync", lambda s, nc, t=t: s.dma_start(
            out=b.x_res[:, t, :], in_=x_dram[t * 128:(t + 1) * 128, :]),
            writes=["x_res"], dma=True)


def store_x(p, b, out_dram):
    for t in range(NT):
        p.emit("sync", lambda s, nc, t=t: s.dma_start(
            out=out_dram[t * 128:(t + 1) * 128, :], in_=b.x_res[:, t, :]),
            reads=["x_res"], writes=["x_out"], dma=True)


def emit_ln_xhatT_old(p, b):
    e = p.emit
    st = b.stat
    for t in range(NT):
        e("vector", lambda s, nc, t=t: s.bn_stats(st[:, 0:6],
                                                  b.x_res[:, t, 0:512]),
          reads=["x_res"], writes=["stat"])
        e("vector", lambda s, nc, t=t: s.bn_stats(st[:, 6:12],
                                                  b.x_res[:, t, 512:1024]),
          reads=["x_res"], writes=["stat"])
        e("vector", lambda s, nc: s.bn_aggr(
            st[:, 12:14], st[:, 0:12].rearrange("p (a b) -> p a b", a=2)),
          reads=["stat"], writes=["stat"])
        e("vector", lambda s, nc: s.tensor_scalar(
            st[:, 16:17], st[:, 13:14], 1e-5, None, op0=ALU.add),
          reads=["stat"], writes=["stat"])
        e("scalar", lambda s, nc: s.activation(st[:, 14:15], st[:, 16:17],
                                               AF.Sqrt),
          reads=["stat"], writes=["stat"])
        e("vector", lambda s, nc: s.reciprocal(st[:, 15:16], st[:, 14:15]),
          reads=["stat"], writes=["stat"])
        e("vector", lambda s, nc, t=t: s.tensor_scalar(
            b.scrb[:, 0:1024], b.x_res[:, t, :], st[:, 12:13], st[:, 15:16],
            op0=ALU.subtract, op1=ALU.mult),
          reads=["x_res", "stat"], writes=["scrb"])
        for c in range(NC):
            lo = 2048 + (c % 2) * 512
            e("tensor", lambda s, nc, c=c, lo=lo: s.matmul(
                b.ps[:, lo:lo + 128], b.scrb[:, c * 128:(c + 1) * 128],
                b.ident[:], start=True, stop=True),
              reads=["scrb", "ident"], writes=psb(lo, lo + 128))
            e("scalar", lambda s, nc, c=c, t=t, lo=lo: s.copy(
                bigA_fm(b, c, t * 128, (t + 1) * 128), b.ps[:, lo:lo + 128]),
              reads=psb(lo, lo + 128), writes=["bigA"])


def emit_ln_xhatT(p, b):
    import os as _os
    if _os.environ.get("K_OLDLN"):
        return emit_ln_xhatT_old(p, b)
    e = p.emit
    st = b.stat
    for t in range(NT):
        par = t % 2
        sb0 = par * 20
        kst = f"stat{par}"
        slot, kslot = b.wslot(par)
        e("vector", lambda s, nc, t=t, sb0=sb0: s.bn_stats(
            st[:, sb0:sb0 + 6], b.x_res[:, t, 0:512]),
          reads=["x_res"], writes=[kst])
        e("vector", lambda s, nc, t=t, sb0=sb0: s.bn_stats(
            st[:, sb0 + 6:sb0 + 12], b.x_res[:, t, 512:1024]),
          reads=["x_res"], writes=[kst])
        e("vector", lambda s, nc, sb0=sb0: s.bn_aggr(
            st[:, sb0 + 12:sb0 + 14],
            st[:, sb0:sb0 + 12].rearrange("p (a b) -> p a b", a=2)),
          reads=[kst], writes=[kst])
        e("vector", lambda s, nc, sb0=sb0: s.tensor_scalar(
            st[:, sb0 + 16:sb0 + 17], st[:, sb0 + 13:sb0 + 14], 1e-5, None,
            op0=ALU.add),
          reads=[kst], writes=[kst])
        e("scalar", lambda s, nc, sb0=sb0: s.activation(
            st[:, sb0 + 14:sb0 + 15], st[:, sb0 + 16:sb0 + 17], AF.Sqrt),
          reads=[kst], writes=[kst])
        e("vector", lambda s, nc, sb0=sb0: s.reciprocal(
            st[:, sb0 + 15:sb0 + 16], st[:, sb0 + 14:sb0 + 15]),
          reads=[kst], writes=[kst])
        e("vector", lambda s, nc, t=t, sb0=sb0, slot=slot: s.tensor_scalar(
            slot[:, 0:1024], b.x_res[:, t, :], st[:, sb0 + 12:sb0 + 13],
            st[:, sb0 + 15:sb0 + 16],
            op0=ALU.subtract, op1=ALU.mult),
          reads=["x_res", kst], writes=[kslot])
        for c in range(NC):
            lo = 2048 + par * 1024 + (c // 4) * 512 + (c % 4) * 128
            e("tensor", lambda s, nc, c=c, lo=lo, slot=slot: s.matmul(
                b.ps[:, lo:lo + 128], slot[:, c * 128:(c + 1) * 128],
                b.ident[:], start=True, stop=True),
              reads=[kslot, "ident"], writes=psb(lo, lo + 128))
        for half in range(2):
            lo = 2048 + par * 1024 + half * 512
            e("vector", lambda s, nc, t=t, half=half, lo=lo: s.tensor_copy(
                b.bigA[:, 0:16384].rearrange("p (c x) -> p c x", c=8)
                [:, half * 4:(half + 1) * 4, t * 128:(t + 1) * 128],
                b.ps[:, lo:lo + 512].rearrange("p (c x) -> p c x", c=4)),
              reads=psb(lo, lo + 512), writes=["bigA"])


def load_wres(p, b, w_dram, ncol=8, width=1024):
    for dc in range(ncol):
        p.emit("gpsimd", lambda s, nc, dc=dc: s.dma_start(
            out=b.wres[:, dc, 0:width],
            in_=w_dram[dc * 128:(dc + 1) * 128, 0:width]),
            writes=["wres"], dma=True)



class AttnCfg:
    def __init__(self, name, dh, wq, wk, wv, wo, allowed, scale,
                 bq_fm=None, bk_fm=None, bv_row=None, bo_row=None,
                 use_gates=False):
        self.name = name
        self.dh = dh
        self.n_heads = D // dh
        self.wq, self.wk, self.wv, self.wo = wq, wk, wv, wo
        self.allowed = allowed
        self.scale = scale
        self.bq_fm, self.bk_fm = bq_fm, bk_fm
        self.bv_row, self.bo_row = bv_row, bo_row
        self.use_gates = use_gates


def band_blocks(cfg, qt):
    ii = qt * 128 + np.arange(128)[:, None]
    out = []
    for kt in range(NT):
        jj = kt * 128 + np.arange(128)[None, :]
        a = cfg.allowed(ii, jj)
        if not a.any():
            continue
        if a.all():
            out.append((kt, None))
        else:
            out.append((kt, np.where(a, 0.0, NEG).astype(np.float32)))
    return out


def collect_masks(cfgs):
    tiles = []
    index = {}

    def get_id(arr):
        key = arr.tobytes()
        if key not in index:
            index[key] = len(tiles)
            tiles.append(arr)
        return index[key]

    ids = {}
    for cfg in cfgs:
        for qt in range(NT):
            for kt, m in band_blocks(cfg, qt):
                if m is not None:
                    ids[(cfg.name, qt, kt, "tm")] = get_id(m)
                    ids[(cfg.name, qt, kt, "st")] = get_id(
                        np.ascontiguousarray(m.T))
    return tiles, ids


def emit_proj_fm(p, b, out_buf_key, out_view, bias_sl=None):
    e = p.emit
    for oc in range(8):
        for tg in range(4):
            lo = (tg % 2) * 512
            ks = psb(lo, lo + 512)
            for dc in range(NC):
                e("tensor", lambda s, nc, oc=oc, tg=tg, dc=dc, lo=lo: s.matmul(
                    b.ps[:, lo:lo + 512],
                    b.wres[:, dc, oc * 128:(oc + 1) * 128],
                    bigA_fm(b, dc, tg * 512, (tg + 1) * 512),
                    start=(dc == 0), stop=(dc == NC - 1)),
                  reads=["wres", "bigA"], writes=ks)
            if bias_sl is not None:
                e("vector", lambda s, nc, oc=oc, tg=tg, lo=lo: s.tensor_scalar(
                    out_view(oc, tg), b.ps[:, lo:lo + 512],
                    b.bias_fm[:, bias_sl + oc: bias_sl + oc + 1], None,
                    op0=ALU.add),
                  reads=ks + ["bias_fm"], writes=[out_buf_key])
            else:
                e("vector", lambda s, nc, oc=oc, tg=tg, lo=lo: s.tensor_copy(
                    out_view(oc, tg), b.ps[:, lo:lo + 512]),
                  reads=ks, writes=[out_buf_key])


def emit_proj_tm_v(p, b, cfg, dram):
    e = p.emit
    load_wres(p, b, dram[cfg.wv])
    if cfg.bv_row:
        e("gpsimd", lambda s, nc: s.dma_start(
            out=b.biasrow[0:1, 0:D], in_=dram[cfg.bv_row][:]),
          writes=["biasrow"], dma=True)
    for t in range(NT):
        for og in range(2):
            lo = (og % 2) * 512
            ks = psb(lo, lo + 512)
            first = True
            for dc in range(NC):
                for ob in range(4):
                    oc = og * 4 + ob
                    e("tensor", lambda s, nc, t=t, dc=dc, ob=ob, oc=oc, lo=lo,
                        first=first: s.matmul(
                        b.ps[:, lo + ob * 128: lo + (ob + 1) * 128],
                        bigA_fm(b, dc, t * 128, (t + 1) * 128),
                        b.wres[:, dc, oc * 128:(oc + 1) * 128],
                        start=first, stop=False),
                      reads=["bigA", "wres"], writes=ks)
                    first = False
            if cfg.bv_row:
                for ob in range(4):
                    oc = og * 4 + ob
                    e("tensor", lambda s, nc, ob=ob, oc=oc, lo=lo: s.matmul(
                        b.ps[:, lo + ob * 128: lo + (ob + 1) * 128],
                        b.ones_bf[0:1, 0:128],
                        b.biasrow[0:1, oc * 128:(oc + 1) * 128],
                        start=False, stop=(ob == 3)),
                      reads=["ones_bf", "biasrow"], writes=ks)
            e("vector", lambda s, nc, t=t, og=og, lo=lo: s.tensor_copy(
                b.bigC[:, t * 1024 + og * 512: t * 1024 + (og + 1) * 512],
                b.ps[:, lo:lo + 512]),
              reads=ks, writes=["bigC"])


def emit_qcol(p, b, t0, tw, bias_sl=None, scale=1.0):
    e = p.emit
    for oc in range(NC):
        lo = 2048 + (oc % 2) * 512
        ks = psb(lo, lo + tw)
        for dc in range(NC):
            e("tensor", lambda s, nc, oc=oc, dc=dc, lo=lo: s.matmul(
                b.ps[:, lo:lo + tw],
                b.wres[:, dc, oc * 128:(oc + 1) * 128],
                bigA_fm(b, dc, t0, t0 + tw),
                start=(dc == 0), stop=(dc == NC - 1)),
              reads=["wres", "bigA"], writes=ks)
        e("vector", lambda s, nc, oc=oc, lo=lo, bias_sl=bias_sl: s.tensor_scalar(
            b.qcol[:, oc, 0:tw], b.ps[:, lo:lo + tw],
            (b.bias_fm[:, bias_sl + oc:bias_sl + oc + 1]
             if bias_sl is not None else 0.0),
            float(scale), op0=ALU.add, op1=ALU.mult),
          reads=ks + (["bias_fm"] if bias_sl is not None else []),
          writes=["qcol"])


def emit_attention_g4(p, b, cfg, dram, mask_ids):
    e = p.emit
    G = 4

    load_wres(p, b, dram[cfg.wk])
    if cfg.bk_fm:
        e("sync", lambda s, nc: s.dma_start(out=b.bias_fm[:, 8:16],
                                            in_=dram[cfg.bk_fm][:]),
          writes=["bias_fm"], dma=True)
    emit_proj_fm(p, b, "bigB",
                 lambda oc, tg: b.bigB[:, oc * 2048 + tg * 512:
                                       oc * 2048 + (tg + 1) * 512],
                 bias_sl=(8 if cfg.bk_fm else None))
    emit_proj_tm_v(p, b, cfg, dram)
    load_wres(p, b, dram[cfg.wq])
    if cfg.bq_fm:
        e("sync", lambda s, nc: s.dma_start(out=b.bias_fm[:, 0:8],
                                            in_=dram[cfg.bq_fm][:]),
          writes=["bias_fm"], dma=True)

    for g in range(NT // G):
        emit_qcol(p, b, g * 512, 512, bias_sl=(0 if cfg.bq_fm else None),
                  scale=cfg.scale)
        bands = [dict(band_blocks(cfg, g * G + qi)) for qi in range(G)]
        ublocks = sorted(set().union(*[set(d.keys()) for d in bands]))
        for h in range(cfg.n_heads):
            par = h % 2
            hb, hp = 64 * par, h // 2
            sb0 = 16 + par * 24
            kst = f"stat{par}"
            knm = f"nmpad{par}"
            krb = f"rowbuf{par}"
            kpd = "scrb"
            for qi in range(G):
                qt = g * G + qi
                blocks = sorted(bands[qi].keys())
                nbl = len(blocks)
                W = nbl * 128
                base = (qi % 2) * 1024 if W <= 1024 else 0
                i = 0
                while i < nbl:
                    j = i
                    while (j + 1 < nbl and blocks[j + 1] == blocks[j] + 1
                           and (j + 1) // 4 == i // 4):
                        j += 1
                    kt0, width, pos = blocks[i], (j - i + 1) * 128, \
                        base + i * 128
                    e("tensor", lambda s, nc, hb=hb, hp=hp, qi=qi, kt0=kt0,
                        width=width, pos=pos: s.matmul(
                        b.ps[:, pos:pos + width],
                        b.qcol[hb:hb + 64, hp, qi * 128:(qi + 1) * 128],
                        b.bigB[hb:hb + 64, hp * 2048 + kt0 * 128:
                               hp * 2048 + kt0 * 128 + width],
                        start=True, stop=True),
                      reads=["qcol", "bigB"], writes=psb(pos, pos + width))
                    i = j + 1
                e("vector", lambda s, nc, W=W, base=base, col=par * 4 + qi:
                    s.tensor_reduce(
                    b.nmpad[:, col:col + 1], b.ps[:, base:base + W], axis=AX.X,
                    op=ALU.max, negate=True),
                  reads=psb(base, base + W), writes=[knm])
            pvlo = (6 + par) * 512
            for qi in range(G):
                e("tensor", lambda s, nc, qi=qi, col=par * 4 + qi, pvlo=pvlo:
                    s.matmul(b.ps[0:1, pvlo + qi * 128:pvlo + (qi + 1) * 128],
                             b.nmpad[:, col:col + 1],
                             b.ident[:], start=True, stop=True),
                  reads=[knm, "ident"], writes=psb(pvlo, pvlo + 512))
            e("vector", lambda s, nc, pvlo=pvlo, par=par: s.tensor_copy(
                b.rowbuf[0:1, par * 512:(par + 1) * 512],
                b.ps[0:1, pvlo:pvlo + 512]),
              reads=psb(pvlo, pvlo + 512), writes=[krb])
            pv_list = [(kt, qi) for kt in ublocks for qi in range(G)
                       if kt in bands[qi]]
            n_pv = 2 * len(pv_list)
            pvk = psb(pvlo, pvlo + 512)
            nbl_u = len(ublocks)
            ipv = 0

            def emit_st(bi, kt):
                lo_st = 2048 + (bi % 2) * 512
                ksS = psb(lo_st, lo_st + 512)
                e("tensor", lambda s, nc, hb=hb, hp=hp, kt=kt, lo_st=lo_st:
                    s.matmul(b.ps[:, lo_st:lo_st + 512],
                             b.bigB[hb:hb + 64, hp * 2048 + kt * 128:
                                    hp * 2048 + (kt + 1) * 128],
                             b.qcol[hb:hb + 64, hp, 0:512],
                             start=True, stop=False),
                  reads=["bigB", "qcol"], writes=ksS)
                e("tensor", lambda s, nc, lo_st=lo_st, par=par: s.matmul(
                    b.ps[:, lo_st:lo_st + 512], b.ones_bf[0:1, 0:128],
                    b.rowbuf[0:1, par * 512:(par + 1) * 512],
                    start=False, stop=True),
                  reads=["ones_bf", krb], writes=ksS)
                qi = 0
                while qi < G:
                    if kt in bands[qi]:
                        m = bands[qi][kt]
                        if m is not None:
                            mid = mask_ids[(cfg.name, g * G + qi, kt, "st")]
                            e("vector", lambda s, nc, mid=mid, qi=qi,
                                lo_st=lo_st: s.tensor_tensor(
                                b.ps[:, lo_st + qi * 128:lo_st + (qi + 1) * 128],
                                b.ps[:, lo_st + qi * 128:lo_st + (qi + 1) * 128],
                                b.masks[:, mid, :], op=ALU.add),
                              reads=ksS + ["masks"], writes=ksS)
                        qi += 1
                    else:
                        q1 = qi
                        while q1 < G and kt not in bands[q1]:
                            q1 += 1
                        e("vector", lambda s, nc, qi=qi, q1=q1, lo_st=lo_st:
                            s.memset(b.ps[:, lo_st + qi * 128:
                                          lo_st + q1 * 128], NEG),
                          reads=ksS, writes=ksS)
                        qi = q1
                e("scalar", lambda s, nc, bi=bi, lo_st=lo_st: s.activation(
                    b.pt[:, bi % 2, :], b.ps[:, lo_st:lo_st + 512], AF.Exp),
                  reads=ksS, writes=[f"pt{bi % 2}"])

            def emit_pv(bi, kt):
                nonlocal ipv
                for qi in range(G):
                    if kt not in bands[qi]:
                        continue
                    e("tensor", lambda s, nc, bi=bi, kt=kt, qi=qi, h=h,
                        pvlo=pvlo, st=(ipv == 0), sp=False:
                        s.matmul(
                        b.ps[:, pvlo + qi * 64:pvlo + (qi + 1) * 64],
                        b.pt[:, bi % 2, qi * 128:(qi + 1) * 128],
                        b.bigC[:, kt * 1024 + h * 64:kt * 1024 + (h + 1) * 64],
                        start=st, stop=sp, skip_group_check=True),
                      reads=[f"pt{bi % 2}", "bigC"], writes=pvk)
                    ipv += 1
                    e("tensor", lambda s, nc, bi=bi, qi=qi, pvlo=pvlo,
                        sp=(ipv == n_pv - 1): s.matmul(
                        b.ps[:, pvlo + 256 + qi:pvlo + 257 + qi],
                        b.pt[:, bi % 2, qi * 128:(qi + 1) * 128],
                        b.ones_bf[:, 0:1],
                        start=False, stop=sp, skip_group_check=True),
                      reads=[f"pt{bi % 2}", "ones_bf"], writes=pvk)
                    ipv += 1

            import os as _os
            if _os.environ.get("K_NOPIPE"):
                for bi in range(nbl_u):
                    emit_st(bi, ublocks[bi])
                    emit_pv(bi, ublocks[bi])
            else:
                for bi in range(nbl_u + 1):
                    if bi < nbl_u:
                        emit_st(bi, ublocks[bi])
                    if bi >= 1:
                        emit_pv(bi - 1, ublocks[bi - 1])
            for qi in range(G):
                qt = g * G + qi
                strip = (2 * qi + h // 8) * 2048 + g * 512 + (h % 8) * 64
                e("vector", lambda s, nc, qi=qi, pvlo=pvlo, sb0=sb0:
                    s.reciprocal(b.stat[:, sb0 + 20 + qi:sb0 + 21 + qi],
                                 b.ps[:, pvlo + 256 + qi:pvlo + 257 + qi]),
                  reads=pvk, writes=[kst])
                if cfg.use_gates:
                    e("vector", lambda s, nc, qi=qi, qt=qt, strip=strip,
                        pvlo=pvlo, sb0=sb0, h=h: s.tensor_scalar(
                        b.bigA[:, strip:strip + 64],
                        b.ps[:, pvlo + qi * 64:pvlo + (qi + 1) * 64],
                        b.stat[:, sb0 + 20 + qi:sb0 + 21 + qi],
                        b.gates[:, qt, 2 * h:2 * h + 1],
                        op0=ALU.mult, op1=ALU.mult),
                      reads=pvk + [kst, "gates"], writes=["bigA"])
                else:
                    e("vector", lambda s, nc, qi=qi, strip=strip,
                        pvlo=pvlo, sb0=sb0: s.tensor_scalar(
                        b.bigA[:, strip:strip + 64],
                        b.ps[:, pvlo + qi * 64:pvlo + (qi + 1) * 64],
                        b.stat[:, sb0 + 20 + qi:sb0 + 21 + qi], None,
                        op0=ALU.mult),
                      reads=pvk + [kst], writes=["bigA"])
        for qi in range(G):
            for c in range(NC):
                src = (2 * qi + c // 4) * 2048 + g * 512 + (c % 4) * 128
                bank = qi * 2 + c // 4
                lo = bank * 512 + (c % 4) * 128
                e("tensor", lambda s, nc, src=src, lo=lo: s.matmul(
                    b.ps[:, lo:lo + 128], b.bigA[:, src:src + 128],
                    b.ident[:], start=True, stop=True),
                  reads=["bigA", "ident"], writes=psb(lo, lo + 128))
        for bank in range(8):
            qi, chalf = bank // 2, bank % 2
            e("vector", lambda s, nc, qi=qi, chalf=chalf, bank=bank, g=g:
                s.tensor_copy(
                b.bigA[:, 0:16384].rearrange("p (c t) -> p c t", c=8)
                [:, chalf * 4:(chalf + 1) * 4,
                 (g * G + qi) * 128:(g * G + qi + 1) * 128],
                b.ps[:, bank * 512:(bank + 1) * 512]
                .rearrange("p (c t) -> p c t", c=4)),
              reads=psb(bank * 512, (bank + 1) * 512), writes=["bigA"])

    load_wres(p, b, dram[cfg.wo])
    if cfg.bo_row:
        e("gpsimd", lambda s, nc: s.dma_start(
            out=b.biasrow[0:1, 0:D], in_=dram[cfg.bo_row][:]),
          writes=["biasrow"], dma=True)
    for qt in range(NT):
        for og in range(2):
            lo = 3072 + og * 512
            ks = psb(lo, lo + 512)
            for dc in range(NC):
                e("tensor", lambda s, nc, qt=qt, og=og, dc=dc, lo=lo: s.matmul(
                    b.ps[:, lo:lo + 512],
                    bigA_fm(b, dc, qt * 128, (qt + 1) * 128),
                    b.wres[:, dc, og * 512:(og + 1) * 512],
                    start=(dc == 0),
                    stop=(dc == NC - 1 and not cfg.bo_row)),
                  reads=["bigA", "wres"], writes=ks)
            if cfg.bo_row:
                e("tensor", lambda s, nc, og=og, lo=lo: s.matmul(
                    b.ps[:, lo:lo + 512], b.ones_bf[0:1, 0:128],
                    b.biasrow[0:1, og * 512:(og + 1) * 512],
                    start=False, stop=True),
                  reads=["ones_bf", "biasrow"], writes=ks)
            e("vector", lambda s, nc, qt=qt, og=og, lo=lo: s.tensor_tensor(
                b.x_res[:, qt, og * 512:(og + 1) * 512], b.ps[:, lo:lo + 512],
                b.x_res[:, qt, og * 512:(og + 1) * 512], op=ALU.add),
              reads=ks + ["x_res"], writes=["x_res"])


def st_runs(blocks):
    runs = []
    i = 0
    nbl = len(blocks)
    while i < nbl:
        j = i
        while (j + 1 < nbl and blocks[j + 1][0] == blocks[j][0] + 1
               and (j + 1) // 4 == i // 4):
            j += 1
        runs.append((i, j))
        i = j + 1
    return runs


def emit_attention(p, b, cfg, dram, mask_ids):
    if cfg.dh == 64:
        emit_attention_g4(p, b, cfg, dram, mask_ids)
    else:
        emit_attention_g1(p, b, cfg, dram, mask_ids)


def emit_attention_g1(p, b, cfg, dram, mask_ids):
    e = p.emit
    dh = cfg.dh

    load_wres(p, b, dram[cfg.wk])
    if cfg.bk_fm:
        e("sync", lambda s, nc: s.dma_start(out=b.bias_fm[:, 8:16],
                                            in_=dram[cfg.bk_fm][:]),
          writes=["bias_fm"], dma=True)
    emit_proj_fm(p, b, "bigB",
                 lambda oc, tg: b.bigB[:, oc * 2048 + tg * 512:
                                       oc * 2048 + (tg + 1) * 512],
                 bias_sl=(8 if cfg.bk_fm else None))
    emit_proj_tm_v(p, b, cfg, dram)

    load_wres(p, b, dram[cfg.wq])
    if cfg.bq_fm:
        e("sync", lambda s, nc: s.dma_start(out=b.bias_fm[:, 0:8],
                                            in_=dram[cfg.bq_fm][:]),
          writes=["bias_fm"], dma=True)

    for qt in range(NT):
        emit_qcol(p, b, qt * 128, 128, bias_sl=(0 if cfg.bq_fm else None),
                  scale=cfg.scale)
        blocks = band_blocks(cfg, qt)
        nbl = len(blocks)
        W = nbl * 128
        for h in range(cfg.n_heads):
            hb = 64 * (h % 2) if dh == 64 else 0
            hp = h // 2
            for (i0, i1) in st_runs(blocks):
                kt0 = blocks[i0][0]
                width = (i1 - i0 + 1) * 128
                pos = i0 * 128
                ks = psb(pos, pos + width)
                if dh == 64:
                    e("tensor", lambda s, nc, hb=hb, hp=hp, kt0=kt0,
                        width=width, pos=pos: s.matmul(
                        b.ps[:, pos:pos + width],
                        b.qcol[hb:hb + 64, hp, 0:128],
                        b.bigB[hb:hb + 64, hp * 2048 + kt0 * 128:
                               hp * 2048 + kt0 * 128 + width],
                        start=True, stop=True),
                      reads=["qcol", "bigB"], writes=ks)
                else:
                    for dc in range(NC):
                        e("tensor", lambda s, nc, dc=dc, kt0=kt0,
                            width=width, pos=pos: s.matmul(
                            b.ps[:, pos:pos + width],
                            b.qcol[:, dc, 0:128],
                            b.bigB[:, dc * 2048 + kt0 * 128:
                                   dc * 2048 + kt0 * 128 + width],
                            start=(dc == 0), stop=(dc == NC - 1)),
                          reads=["qcol", "bigB"], writes=ks)
            for bi, (kt, m) in enumerate(blocks):
                if m is None:
                    continue
                mid = mask_ids[(cfg.name, qt, kt, "tm")]
                pos = bi * 128
                e("vector", lambda s, nc, mid=mid, pos=pos: s.tensor_tensor(
                    b.ps[:, pos:pos + 128], b.ps[:, pos:pos + 128],
                    b.masks[:, mid, :], op=ALU.add),
                  reads=psb(pos, pos + 128) + ["masks"],
                  writes=psb(pos, pos + 128))
            e("vector", lambda s, nc, W=W, hb=hb: s.tensor_reduce(
                b.nmpad[:, hb:hb + 1], b.ps[:, 0:W], axis=AX.X,
                op=ALU.max, negate=True),
              reads=psb(0, W), writes=["nmpad"])
            e("vector", lambda s, nc, hb=hb: s.tensor_copy(
                b.stat[:, 20:21], b.nmpad[:, hb:hb + 1]),
              reads=["nmpad"], writes=["stat"])
            nch = (W + 1023) // 1024
            for c in range(nch):
                w0, w1 = c * 1024, min(W, (c + 1) * 1024)
                e("scalar", lambda s, nc, w0=w0, w1=w1, c=c: s.activation(
                    b.scrb[:, 0:w1 - w0], b.ps[:, w0:w1], AF.Exp,
                    bias=b.stat[:, 20:21], scale=1.0,
                    accum_out=b.stat[:, 24 + c:25 + c]),
                  reads=psb(w0, w1) + ["stat"], writes=["scrb", "stat"])
            if nch > 1:
                e("vector", lambda s, nc, nch=nch: s.tensor_reduce(
                    b.stat[:, 28:29], b.stat[:, 24:24 + nch], axis=AX.X,
                    op=ALU.add), reads=["stat"], writes=["stat"])
            den_sl = 28 if nch > 1 else 24
            e("vector", lambda s, nc, den_sl=den_sl: s.reciprocal(
                b.stat[:, 29:30], b.stat[:, den_sl:den_sl + 1]),
              reads=["stat"], writes=["stat"])
            lo_nm = 2560
            e("tensor", lambda s, nc, lo_nm=lo_nm: s.matmul(
                b.ps[0:8, lo_nm:lo_nm + 128], b.nmpad[:], b.ident[:],
                start=True, stop=True),
              reads=["nmpad", "ident"], writes=psb(lo_nm, lo_nm + 128))
            e("vector", lambda s, nc, hb=hb, lo_nm=lo_nm: s.tensor_copy(
                b.rowbuf[hb:hb + 1, 0:128], b.ps[hb:hb + 1, lo_nm:lo_nm + 128]),
              reads=psb(lo_nm, lo_nm + 128), writes=["rowbuf"])
            for bi, (kt, m) in enumerate(blocks):
                lo_st = 2048 + (bi % 2) * 512
                ksS = psb(lo_st, lo_st + 128)
                if dh == 64:
                    e("tensor", lambda s, nc, hb=hb, hp=hp, kt=kt, lo_st=lo_st:
                        s.matmul(b.ps[:, lo_st:lo_st + 128],
                                 b.bigB[hb:hb + 64,
                                        hp * 2048 + kt * 128:
                                        hp * 2048 + (kt + 1) * 128],
                                 b.qcol[hb:hb + 64, hp, 0:128],
                                 start=True, stop=False),
                      reads=["bigB", "qcol"], writes=ksS)
                else:
                    for dc in range(NC):
                        e("tensor", lambda s, nc, dc=dc, kt=kt, lo_st=lo_st:
                            s.matmul(b.ps[:, lo_st:lo_st + 128],
                                     b.bigB[:, dc * 2048 + kt * 128:
                                            dc * 2048 + (kt + 1) * 128],
                                     b.qcol[:, dc, 0:128],
                                     start=(dc == 0), stop=False),
                          reads=["bigB", "qcol"], writes=ksS)
                e("tensor", lambda s, nc, hb=hb, lo_st=lo_st: s.matmul(
                    b.ps[:, lo_st:lo_st + 128],
                    b.ones_bf[hb:hb + 1, 0:128],
                    b.rowbuf[hb:hb + 1, 0:128],
                    start=False, stop=True),
                  reads=["ones_bf", "rowbuf"], writes=ksS)
                if m is not None:
                    mid = mask_ids[(cfg.name, qt, kt, "st")]
                    e("vector", lambda s, nc, mid=mid, lo_st=lo_st:
                        s.tensor_tensor(b.ps[:, lo_st:lo_st + 128],
                                        b.ps[:, lo_st:lo_st + 128],
                                        b.masks[:, mid, :], op=ALU.add),
                      reads=ksS + ["masks"], writes=ksS)
                e("scalar", lambda s, nc, bi=bi, lo_st=lo_st: s.activation(
                    b.pt[:, bi % 2, 0:128], b.ps[:, lo_st:lo_st + 128], AF.Exp),
                  reads=ksS, writes=[f"pt{bi % 2}"])
                if dh == 64:
                    e("tensor", lambda s, nc, bi=bi, kt=kt, h=h, nbl=nbl:
                        s.matmul(b.ps[:, 3072:3136],
                                 b.pt[:, bi % 2, 0:128],
                                 b.bigC[:, kt * 1024 + h * 64:
                                        kt * 1024 + (h + 1) * 64],
                                 start=(bi == 0), stop=(bi == nbl - 1)),
                      reads=[f"pt{bi % 2}", "bigC"], writes=psb(3072, 3136))
                else:
                    for og in range(2):
                        e("tensor", lambda s, nc, bi=bi, kt=kt, og=og, nbl=nbl:
                            s.matmul(b.ps[:, 3072 + og * 512:
                                          3072 + og * 512 + 512],
                                     b.pt[:, bi % 2, 0:128],
                                     b.bigC[:, kt * 1024 + og * 512:
                                            kt * 1024 + (og + 1) * 512],
                                     start=(bi == 0), stop=(bi == nbl - 1)),
                          reads=[f"pt{bi % 2}", "bigC"],
                          writes=psb(3072 + og * 512, 3072 + og * 512 + 512))
            if dh == 64:
                if cfg.use_gates:
                    e("vector", lambda s, nc, h=h, qt=qt: s.tensor_scalar(
                        b.scrb[:, h * 64:(h + 1) * 64], b.ps[:, 3072:3136],
                        b.stat[:, 29:30], b.gates[:, qt, 2 * h:2 * h + 1],
                        op0=ALU.mult, op1=ALU.mult),
                      reads=psb(3072, 3136) + ["stat", "gates"],
                      writes=["scrb"])
                else:
                    e("vector", lambda s, nc, h=h: s.tensor_scalar(
                        b.scrb[:, h * 64:(h + 1) * 64], b.ps[:, 3072:3136],
                        b.stat[:, 29:30], None, op0=ALU.mult),
                      reads=psb(3072, 3136) + ["stat"], writes=["scrb"])
            else:
                for og in range(2):
                    e("vector", lambda s, nc, og=og: s.tensor_scalar(
                        b.scrb[:, og * 512:(og + 1) * 512],
                        b.ps[:, 3072 + og * 512:3072 + og * 512 + 512],
                        b.stat[:, 29:30], None, op0=ALU.mult),
                      reads=psb(3072 + og * 512, 3072 + og * 512 + 512)
                      + ["stat"], writes=["scrb"])
        for c in range(NC):
            lo = 2048 + (c % 2) * 512
            e("tensor", lambda s, nc, c=c, lo=lo: s.matmul(
                b.ps[:, lo:lo + 128], b.scrb[:, c * 128:(c + 1) * 128],
                b.ident[:], start=True, stop=True),
              reads=["scrb", "ident"], writes=psb(lo, lo + 128))
            e("scalar", lambda s, nc, c=c, qt=qt, lo=lo: s.copy(
                bigA_fm(b, c, qt * 128, (qt + 1) * 128), b.ps[:, lo:lo + 128]),
              reads=psb(lo, lo + 128), writes=["bigA"])

    load_wres(p, b, dram[cfg.wo])
    if cfg.bo_row:
        e("gpsimd", lambda s, nc: s.dma_start(
            out=b.biasrow[0:1, 0:D], in_=dram[cfg.bo_row][:]),
          writes=["biasrow"], dma=True)
    for qt in range(NT):
        for og in range(2):
            lo = 3072 + og * 512
            ks = psb(lo, lo + 512)
            for dc in range(NC):
                e("tensor", lambda s, nc, qt=qt, og=og, dc=dc, lo=lo: s.matmul(
                    b.ps[:, lo:lo + 512],
                    bigA_fm(b, dc, qt * 128, (qt + 1) * 128),
                    b.wres[:, dc, og * 512:(og + 1) * 512],
                    start=(dc == 0),
                    stop=(dc == NC - 1 and not cfg.bo_row)),
                  reads=["bigA", "wres"], writes=ks)
            if cfg.bo_row:
                e("tensor", lambda s, nc, og=og, lo=lo: s.matmul(
                    b.ps[:, lo:lo + 512], b.ones_bf[0:1, 0:128],
                    b.biasrow[0:1, og * 512:(og + 1) * 512],
                    start=False, stop=True),
                  reads=["ones_bf", "biasrow"], writes=ks)
            e("vector", lambda s, nc, qt=qt, og=og, lo=lo: s.tensor_tensor(
                b.x_res[:, qt, og * 512:(og + 1) * 512], b.ps[:, lo:lo + 512],
                b.x_res[:, qt, og * 512:(og + 1) * 512], op=ALU.add),
              reads=ks + ["x_res"], writes=["x_res"])



def emit_gamma(p, b, w_dram, nrc, ncc, thr_sl):
    e = p.emit
    assert nrc * ncc == 64
    gview = b.gates[:, 0:16, 0:32].rearrange("p a c -> p (a c)")
    nch = 0
    for rc in range(nrc):
        for cc in range(ncc):
            st32, k32 = (b.scr32, "scr32") if nch % 2 == 0 else (gview,
                                                                 "gates")
            e("sync", lambda s, nc, rc=rc, cc=cc, st32=st32: s.dma_start(
                out=st32[:, 0:512], in_=w_dram[rc * 128:(rc + 1) * 128,
                                               cc * 512:(cc + 1) * 512]),
              writes=[k32], dma=True)
            e("vector", lambda s, nc, nch=nch, st32=st32: s.tensor_reduce(
                b.stat[:, nch:nch + 1], st32[:, 0:512], axis=AX.X,
                op=ALU.add, apply_absolute_value=True),
              reads=[k32], writes=["stat"])
            nch += 1
    e("vector", lambda s, nc: s.tensor_reduce(
        b.small[:, 34:35], b.stat[:, 0:64], axis=AX.X, op=ALU.add),
      reads=["stat"], writes=["small"])
    e("tensor", lambda s, nc: s.matmul(b.ps[0:1, 2048:2049], b.small[:, 34:35],
                                       b.ones32[0:128, 0:1],
                                       start=True, stop=True),
      reads=["small", "ones32"], writes=psb(2048, 2049))
    n_el = nrc * 128 * ncc * 512
    e("scalar", lambda s, nc, n_el=n_el: s.activation(
        b.small[0:1, 35:36], b.ps[0:1, 2048:2049], AF.Copy,
        bias=0.5e-6, scale=float(0.5 / n_el)),
      reads=psb(2048, 2049), writes=["small"])
    e("tensor", lambda s, nc: s.matmul(b.ps[:, 2048:2049],
                                       b.ones32[0:1, 0:128],
                                       b.small[0:1, 35:36],
                                       start=True, stop=True),
      reads=["ones32", "small"], writes=psb(2048, 2049))
    e("vector", lambda s, nc, thr_sl=thr_sl: s.tensor_copy(
        b.small[:, thr_sl:thr_sl + 1], b.ps[:, 2048:2049]),
      reads=psb(2048, 2176), writes=["small"])
    e("vector", lambda s, nc, thr_sl=thr_sl: s.tensor_scalar(
        b.small[:, thr_sl + 1:thr_sl + 2], b.small[:, thr_sl:thr_sl + 1],
        -1.0, None, op0=ALU.mult),
      reads=["small"], writes=["small"])


def emit_l0f_prep(p, b, dram, scr):
    e = p.emit
    emit_gamma(p, b, dram["L0f_wfcT"], 8, 8, 36)
    emit_gamma(p, b, dram["L0f_wprT"], 32, 2, 38)
    e("sync", lambda s, nc: s.dma_start(out=b.small[:, 0:32],
                                        in_=dram["L0f_lnw"][:]),
      writes=["small"], dma=True)
    e("gpsimd", lambda s, nc: s.dma_start(out=b.gwb[:, 0:32],
                                          in_=dram["L0f_lnb_fm"][:]),
      writes=["gwb"], dma=True)
    gview = b.gates[:, 0:16, 0:32].rearrange("p a c -> p (a c)")
    for rc in range(8):
        for cc in range(8):
            par = (rc * 8 + cc) % 2
            st32, k32 = (b.scr32, "scr32") if par == 0 else (gview, "gates")
            sa, ka = b.wslot(2 * par)
            sb_, kb = b.wslot(2 * par + 1)
            e("sync", lambda s, nc, rc=rc, cc=cc, st32=st32: s.dma_start(
                out=st32[:, 0:512],
                in_=dram["L0f_wfcT"][rc * 128:(rc + 1) * 128,
                                     cc * 512:(cc + 1) * 512]),
              writes=[k32], dma=True)
            e("vector", lambda s, nc, st32=st32, sa=sa: s.tensor_scalar(
                sa[:, 0:512], st32[:, 0:512], b.small[:, 36:37], None,
                op0=ALU.is_gt), reads=[k32, "small"], writes=[ka])
            e("vector", lambda s, nc, st32=st32, sa=sa: s.tensor_scalar(
                sa[:, 512:1024], st32[:, 0:512], b.small[:, 37:38], None,
                op0=ALU.is_lt), reads=[k32, "small"], writes=[ka])
            e("vector", lambda s, nc, sa=sa, sb_=sb_: s.tensor_tensor(
                sb_[:, 0:512], sa[:, 0:512], sa[:, 512:1024],
                op=ALU.subtract), reads=[ka], writes=[kb])
            e("sync", lambda s, nc, rc=rc, cc=cc, sb_=sb_: s.dma_start(
                out=scr["fcq"][rc * 128:(rc + 1) * 128,
                               cc * 512:(cc + 1) * 512],
                in_=sb_[:, 0:512]),
              reads=[kb], writes=["fcq"], dma=True)
    for rc in range(32):
        par = rc % 2
        sa, ka = b.wslot(2 * par)
        sb_, kb = b.wslot(2 * par + 1)
        for hf in range(2):
            st32, k32 = (b.scr32, "scr32") if hf == 0 else (gview, "gates")
            e("sync", lambda s, nc, rc=rc, hf=hf, st32=st32: s.dma_start(
                out=st32[:, 0:512],
                in_=dram["L0f_wprT"][rc * 128:(rc + 1) * 128,
                                     hf * 512:(hf + 1) * 512]),
              writes=[k32], dma=True)
            e("vector", lambda s, nc, st32=st32, sa=sa: s.tensor_scalar(
                sa[:, 0:512], st32[:, 0:512], b.small[:, 38:39], None,
                op0=ALU.is_gt), reads=[k32, "small"], writes=[ka])
            e("vector", lambda s, nc, st32=st32, sa=sa: s.tensor_scalar(
                sa[:, 512:1024], st32[:, 0:512], b.small[:, 39:40], None,
                op0=ALU.is_lt), reads=[k32, "small"], writes=[ka])
            e("vector", lambda s, nc, hf=hf, sa=sa, sb_=sb_: s.tensor_tensor(
                sb_[:, hf * 512:(hf + 1) * 512], sa[:, 0:512],
                sa[:, 512:1024],
                op=ALU.subtract), reads=[ka], writes=[kb])
        for half in range(2):
            e("tensor", lambda s, nc, rc=rc, half=half, sb_=sb_: s.matmul(
                b.ps[0:1, 3072 + half * 512: 3072 + (half + 1) * 512],
                b.gwb[:, rc:rc + 1],
                sb_[:, half * 512:(half + 1) * 512],
                start=(rc == 0), stop=(rc == 31)),
              reads=["gwb", kb],
              writes=psb(3072 + half * 512, 3072 + (half + 1) * 512))
        e("vector", lambda s, nc, rc=rc, sa=sa, sb_=sb_: s.tensor_scalar(
            sa[:, 0:1024], sb_[:, 0:1024], b.small[:, rc:rc + 1],
            None, op0=ALU.mult),
          reads=[kb, "small"], writes=[ka])
        e("sync", lambda s, nc, rc=rc, sa=sa: s.dma_start(
            out=scr["prq"][rc * 128:(rc + 1) * 128, :], in_=sa[:, 0:1024]),
          reads=[ka], writes=["prq"], dma=True)
    e("vector", lambda s, nc: s.tensor_copy(b.scrb[0:1, 0:512],
                                            b.ps[0:1, 3072:3584]),
      reads=psb(3072, 3584), writes=["scrb"])
    e("vector", lambda s, nc: s.tensor_copy(b.scrb[0:1, 512:1024],
                                            b.ps[0:1, 3584:4096]),
      reads=psb(3584, 4096), writes=["scrb"])
    e("sync", lambda s, nc: s.dma_start(out=scr["l0f_vec"][0:1, :],
                                        in_=b.scrb[0:1, 0:1024]),
      reads=["scrb"], writes=["l0f_vec"], dma=True)


def emit_l0_ffn(p, b, scr):
    e = p.emit
    st = b.stat
    e("sync", lambda s, nc: s.dma_start(out=b.biasrow[0:1, 0:1024],
                                        in_=scr["l0f_vec"][0:1, :]),
      reads=["l0f_vec"], writes=["biasrow"], dma=True)
    for tg in range(4):
        for hg in range(8):
            for dc in range(NC):
                e("sync", lambda s, nc, dc=dc, hg=hg: s.dma_start(
                    out=b.wres[:, dc, 0:512],
                    in_=scr["fcq"][dc * 128:(dc + 1) * 128,
                                   hg * 512:(hg + 1) * 512]),
                  reads=["fcq"], writes=["wres"], dma=True)
            for ti in range(4):
                t = tg * 4 + ti
                lo = (ti % 2) * 512
                ks = psb(lo, lo + 512)
                for dc in range(NC):
                    e("tensor", lambda s, nc, dc=dc, t=t, lo=lo: s.matmul(
                        b.ps[:, lo:lo + 512],
                        bigA_fm(b, dc, t * 128, (t + 1) * 128),
                        b.wres[:, dc, 0:512],
                        start=(dc == 0), stop=(dc == NC - 1)),
                      reads=["bigA", "wres"], writes=ks)
                e("scalar", lambda s, nc, ti=ti, hg=hg, lo=lo: s.activation(
                    b.bigB[:, ti * 4096 + hg * 512: ti * 4096 + (hg + 1) * 512],
                    b.ps[:, lo:lo + 512], AF.Gelu),
                  reads=ks, writes=["bigB"])
        for ti in range(4):
            for i8 in range(8):
                e("vector", lambda s, nc, ti=ti, i8=i8: s.bn_stats(
                    st[:, 6 * i8:6 * i8 + 6],
                    b.bigB[:, ti * 4096 + i8 * 512: ti * 4096 + (i8 + 1) * 512]),
                  reads=["bigB"], writes=["stat"])
            e("vector", lambda s, nc: s.bn_aggr(
                st[:, 48:50], st[:, 0:48].rearrange("p (a b) -> p a b", a=8)),
              reads=["stat"], writes=["stat"])
            e("vector", lambda s, nc: s.tensor_scalar(
                st[:, 50:51], st[:, 49:50], 1e-5, None, op0=ALU.add),
              reads=["stat"], writes=["stat"])
            e("scalar", lambda s, nc: s.activation(st[:, 51:52], st[:, 50:51],
                                                   AF.Sqrt),
              reads=["stat"], writes=["stat"])
            e("vector", lambda s, nc: s.reciprocal(st[:, 52:53], st[:, 51:52]),
              reads=["stat"], writes=["stat"])
            for cc in range(4):
                e("vector", lambda s, nc, ti=ti, cc=cc: s.tensor_scalar(
                    b.scrb[:, 0:1024],
                    b.bigB[:, ti * 4096 + cc * 1024: ti * 4096 + (cc + 1) * 1024],
                    st[:, 48:49], st[:, 52:53],
                    op0=ALU.subtract, op1=ALU.mult),
                  reads=["bigB", "stat"], writes=["scrb"])
                for ci in range(8):
                    hc = cc * 8 + ci
                    lo = 2048 + (ci % 2) * 512
                    e("tensor", lambda s, nc, ci=ci, lo=lo: s.matmul(
                        b.ps[:, lo:lo + 128],
                        b.scrb[:, ci * 128:(ci + 1) * 128], b.ident[:],
                        start=True, stop=True),
                      reads=["scrb", "ident"], writes=psb(lo, lo + 128))
                    e("scalar", lambda s, nc, hc=hc, ti=ti, lo=lo: s.copy(
                        b.bigC[:, hc * 512 + ti * 128: hc * 512 + (ti + 1) * 128],
                        b.ps[:, lo:lo + 128]),
                      reads=psb(lo, lo + 128), writes=["bigC"])
        for dg in range(2):
            for hc in range(32):
                wslot, wkey = b.wslot(dg * 32 + hc)
                e("sync", lambda s, nc, hc=hc, dg=dg, wslot=wslot: s.dma_start(
                    out=wslot[:, 0:512],
                    in_=scr["prq"][hc * 128:(hc + 1) * 128,
                                   dg * 512:(dg + 1) * 512]),
                  reads=["prq"], writes=[wkey], dma=True)
                for ti in range(4):
                    bank = dg * 4 + ti
                    e("tensor", lambda s, nc, hc=hc, ti=ti, bank=bank,
                        wslot=wslot, wkey=wkey: s.matmul(
                        b.ps[:, bank * 512:(bank + 1) * 512],
                        b.bigC[:, hc * 512 + ti * 128:
                               hc * 512 + (ti + 1) * 128],
                        wslot[:, 0:512],
                        start=(hc == 0), stop=False),
                      reads=["bigC", wkey],
                      writes=psb(bank * 512, (bank + 1) * 512))
            for ti in range(4):
                bank = dg * 4 + ti
                e("tensor", lambda s, nc, dg=dg, bank=bank: s.matmul(
                    b.ps[:, bank * 512:(bank + 1) * 512],
                    b.ones_bf[0:1, 0:128],
                    b.biasrow[0:1, dg * 512:(dg + 1) * 512],
                    start=False, stop=True),
                  reads=["ones_bf", "biasrow"],
                  writes=psb(bank * 512, (bank + 1) * 512))
                t = tg * 4 + ti
                e("vector", lambda s, nc, t=t, dg=dg, bank=bank: s.tensor_tensor(
                    b.x_res[:, t, dg * 512:(dg + 1) * 512],
                    b.ps[:, bank * 512:(bank + 1) * 512],
                    b.x_res[:, t, dg * 512:(dg + 1) * 512], op=ALU.add),
                  reads=psb(bank * 512, (bank + 1) * 512) + ["x_res"],
                  writes=["x_res"])



def emit_l1f_prep(p, b, dram, scr):
    e = p.emit
    jobs = [("L1f_w1T", "w1q", 8, 4096), ("L1f_w2T", "w2q", 8, 4096),
            ("L1f_h1T", "h1q", 32, 4096), ("L1f_h2T", "h2q", 32, 4096),
            ("L1f_w3T", "w3q", 32, 1024)]
    n = 0
    for src, dst, nrc, width in jobs:
        for rc in range(nrc):
            for cc in range(width // 1024):
                slot, key = b.wslot(n)
                n += 1
                e("gpsimd", lambda s, nc, src=src, rc=rc, cc=cc,
                    slot=slot: s.dma_start(
                    out=slot[:, 0:1024],
                    in_=dram[src][rc * 128:(rc + 1) * 128,
                                  cc * 1024:cc * 1024 + 1024]),
                  writes=[key], dma=True)
                e("sync", lambda s, nc, dst=dst, rc=rc, cc=cc,
                    slot=slot: s.dma_start(
                    out=scr[dst][rc * 128:(rc + 1) * 128,
                                 cc * 1024:cc * 1024 + 1024],
                    in_=slot[:, 0:1024]),
                  reads=[key], writes=[dst], dma=True)


def emit_l1_ffn(p, b, dram, scr):
    e = p.emit
    for i, k in enumerate(["L1f_s1", "L1f_s2", "L1f_sh1", "L1f_sh2"]):
        e("sync", lambda s, nc, i=i, k=k: s.dma_start(
            out=b.scr32[:, 32 * i:32 * (i + 1)], in_=dram[k][:]),
          writes=["scr32"], dma=True)
    e("sync", lambda s, nc: s.dma_start(out=b.small[:, 0:8],
                                        in_=dram["L1f_s3fm"][:]),
      writes=["small"], dma=True)

    def mm_block(scrname, in_key, in_view, out_cb, s_off, n_ic, tg,
                 nog=4, width_o=4096):
        for ocg in range(nog):
            for ic in range(n_ic):
                slot, key = b.wslot(ocg * n_ic + ic)
                e("sync", lambda s, nc, scrname=scrname, ic=ic, ocg=ocg,
                    slot=slot: s.dma_start(
                    out=slot[:, 0:1024],
                    in_=scr[scrname][ic * 128:(ic + 1) * 128,
                                     ocg * 1024:(ocg + 1) * 1024]),
                  reads=[scrname], writes=[key], dma=True)
                for ob in range(8):
                    e("tensor", lambda s, nc, ic=ic, ob=ob, slot=slot,
                        key=key: s.matmul(
                        b.ps[:, ob * 512:(ob + 1) * 512],
                        slot[:, ob * 128:(ob + 1) * 128],
                        in_view(ic),
                        start=(ic == 0), stop=(ic == n_ic - 1)),
                      reads=[key, in_key],
                      writes=psb(ob * 512, (ob + 1) * 512))
            for ob in range(8):
                out_cb(ocg * 8 + ob, ob)

    for tg in range(4):
        def evac_silu(oc, bank):
            e("scalar", lambda s, nc, oc=oc, bank=bank: s.activation(
                b.bigB[:, oc * 512:(oc + 1) * 512],
                b.ps[:, bank * 512:(bank + 1) * 512], AF.Silu,
                scale=b.scr32[:, oc:oc + 1]),
              reads=psb(bank * 512, (bank + 1) * 512) + ["scr32"],
              writes=["bigB"])
        mm_block("w1q", "bigA",
                 lambda ic, tg=tg: bigA_fm(b, ic, tg * 512, (tg + 1) * 512),
                 evac_silu, 0, NC, tg)
        def evac_h(oc, bank):
            e("vector", lambda s, nc, oc=oc, bank=bank: s.scalar_tensor_tensor(
                b.bigC[:, oc * 512:(oc + 1) * 512],
                b.ps[:, bank * 512:(bank + 1) * 512],
                b.scr32[:, 32 + oc:33 + oc],
                b.bigB[:, oc * 512:(oc + 1) * 512],
                op0=ALU.mult, op1=ALU.mult),
              reads=psb(bank * 512, (bank + 1) * 512) + ["scr32", "bigB"],
              writes=["bigC"])
        mm_block("w2q", "bigA",
                 lambda ic, tg=tg: bigA_fm(b, ic, tg * 512, (tg + 1) * 512),
                 evac_h, 32, NC, tg)
        def evac_g1(oc, bank):
            e("scalar", lambda s, nc, oc=oc, bank=bank: s.activation(
                b.bigB[:, oc * 512:(oc + 1) * 512],
                b.ps[:, bank * 512:(bank + 1) * 512], AF.Gelu,
                scale=b.scr32[:, 64 + oc:65 + oc]),
              reads=psb(bank * 512, (bank + 1) * 512) + ["scr32"],
              writes=["bigB"])
        mm_block("h1q", "bigC",
                 lambda ic: b.bigC[:, ic * 512:(ic + 1) * 512],
                 evac_g1, 64, 32, tg)
        def evac_g2(oc, bank):
            e("scalar", lambda s, nc, oc=oc, bank=bank: s.activation(
                b.bigC[:, oc * 512:(oc + 1) * 512],
                b.ps[:, bank * 512:(bank + 1) * 512], AF.Gelu,
                scale=b.scr32[:, 96 + oc:97 + oc]),
              reads=psb(bank * 512, (bank + 1) * 512) + ["scr32"],
              writes=["bigC"])
        mm_block("h2q", "bigB",
                 lambda ic: b.bigB[:, ic * 512:(ic + 1) * 512],
                 evac_g2, 96, 32, tg)
        def evac_w3(dcb, bank):
            e("vector", lambda s, nc, dcb=dcb, bank=bank: s.tensor_scalar(
                b.scrb[:, 0:512], b.ps[:, bank * 512:(bank + 1) * 512],
                b.small[:, dcb:dcb + 1], None, op0=ALU.mult),
              reads=psb(bank * 512, (bank + 1) * 512) + ["small"],
              writes=["scrb"])
            for ti in range(4):
                t = tg * 4 + ti
                lo = bank * 512
                e("tensor", lambda s, nc, ti=ti, lo=lo: s.matmul(
                    b.ps[:, lo:lo + 128],
                    b.scrb[:, ti * 128:(ti + 1) * 128], b.ident[:],
                    start=True, stop=True),
                  reads=["scrb", "ident"], writes=psb(lo, lo + 128))
                e("vector", lambda s, nc, t=t, dcb=dcb, lo=lo: s.tensor_tensor(
                    b.x_res[:, t, dcb * 128:(dcb + 1) * 128],
                    b.ps[:, lo:lo + 128],
                    b.x_res[:, t, dcb * 128:(dcb + 1) * 128], op=ALU.add),
                  reads=psb(lo, lo + 128) + ["x_res"], writes=["x_res"])
        mm_block("w3q", "bigC",
                 lambda ic: b.bigC[:, ic * 512:(ic + 1) * 512],
                 evac_w3, 0, 32, tg, nog=1, width_o=1024)



def emit_l1b(p, b, dram, mask_ids):
    e = p.emit
    emit_ln_xhatT(p, b)
    load_wres(p, b, dram["L1b_wq"])
    e("gpsimd", lambda s, nc: s.dma_start(out=b.gwb[:, 0:32],
                                          in_=dram["L1b_gw"][:]),
      writes=["gwb"], dma=True)
    for qt in range(NT):
        emit_qcol(p, b, qt * 128, 128, bias_sl=None, scale=1.0)
        for h in range(16):
            hb, hp = 64 * (h % 2), h // 2
            for gi in range(2):
                e("tensor", lambda s, nc, h=h, hb=hb, hp=hp, gi=gi: s.matmul(
                    b.ps[:, 2048 + h * 2 + gi: 2048 + h * 2 + gi + 1],
                    b.qcol[hb:hb + 64, hp, 0:128],
                    b.gwb[hb:hb + 64, 16 * gi + h:16 * gi + h + 1],
                    start=True, stop=True),
                  reads=["qcol", "gwb"], writes=psb(2048, 2560))
        e("scalar", lambda s, nc, qt=qt: s.activation(
            b.gates[:, qt, :], b.ps[:, 2048:2080], AF.Sigmoid),
          reads=psb(2048, 2560), writes=["gates"])
    load_wres(p, b, dram["L1b_wgq"], width=512)
    for oc in range(4):
        for tg in range(4):
            lo = (tg % 2) * 512
            ks = psb(lo, lo + 512)
            for dc in range(NC):
                e("tensor", lambda s, nc, oc=oc, tg=tg, dc=dc, lo=lo: s.matmul(
                    b.ps[:, lo:lo + 512],
                    b.wres[:, dc, oc * 128:(oc + 1) * 128],
                    bigA_fm(b, dc, tg * 512, (tg + 1) * 512),
                    start=(dc == 0), stop=(dc == NC - 1)),
                  reads=["wres", "bigA"], writes=ks)
            e("vector", lambda s, nc, oc=oc, tg=tg, lo=lo: s.tensor_scalar(
                b.bigB[:, oc * 2048 + tg * 512: oc * 2048 + (tg + 1) * 512],
                b.ps[:, lo:lo + 512], 0.125, None, op0=ALU.mult),
              reads=psb(lo, lo + 512), writes=["bigB"])
    for which, wname, dst0 in (("gk", "L1b_wgk", 0), ("gv", "L1b_wgv", 256)):
        load_wres(p, b, dram[wname], width=512)
        for oc in range(4):
            for tg in range(4):
                lo = (tg % 2) * 512
                ks = psb(lo, lo + 512)
                for dc in range(NC):
                    e("tensor", lambda s, nc, oc=oc, tg=tg, dc=dc, lo=lo:
                        s.matmul(b.ps[:, lo:lo + 512],
                                 b.wres[:, dc, oc * 128:(oc + 1) * 128],
                                 bigA_fm(b, dc, tg * 512, (tg + 1) * 512),
                                 start=(dc == 0), stop=(dc == NC - 1)),
                      reads=["wres", "bigA"], writes=ks)
                e("vector", lambda s, nc, oc=oc, tg=tg, lo=lo: s.tensor_copy(
                    b.bigC[:, oc * 2048 + tg * 512:
                           oc * 2048 + (tg + 1) * 512],
                    b.ps[:, lo:lo + 512]),
                  reads=psb(lo, lo + 512), writes=["bigC"])
        for oc in range(4):
            e("vector", lambda s, nc, oc=oc, dst0=dst0: s.tensor_reduce(
                b.scr32[:, dst0 + oc * 64: dst0 + (oc + 1) * 64],
                b.bigC[:, oc * 2048:(oc + 1) * 2048]
                .rearrange("p (n w) -> p n w", w=32),
                axis=AX.X, op=ALU.add),
              reads=["bigC"], writes=["scr32"])
        e("vector", lambda s, nc, dst0=dst0: s.tensor_scalar(
            b.scrb[:, dst0:dst0 + 256], b.scr32[:, dst0:dst0 + 256],
            1.0 / 32.0, None, op0=ALU.mult),
          reads=["scr32"], writes=["scrb"])
    for lh in range(8):
        lhb, lhp = 64 * (lh % 2), lh // 2
        e("tensor", lambda s, nc, lhb=lhb, lhp=lhp: s.matmul(
            b.ps[0:64, 2560:2624],
            b.scrb[lhb:lhb + 64, 256 + lhp * 64: 256 + (lhp + 1) * 64],
            b.ident[lhb:lhb + 64, lhb:lhb + 64],
            start=True, stop=True),
          reads=["scrb", "ident"], writes=psb(2560, 2688))
        e("vector", lambda s, nc: s.tensor_copy(b.pt[0:64, 1, 0:64],
                                                b.ps[0:64, 2560:2624]),
          reads=psb(2560, 2688), writes=["pt1"])
        for stile in range(NT):
            e("tensor", lambda s, nc, lhb=lhb, lhp=lhp, stile=stile: s.matmul(
                b.ps[:, 2048:2112],
                b.bigB[lhb:lhb + 64, lhp * 2048 + stile * 128:
                       lhp * 2048 + (stile + 1) * 128],
                b.scrb[lhb:lhb + 64, lhp * 64:(lhp + 1) * 64],
                start=True, stop=True),
              reads=["bigB", "scrb"], writes=psb(2048, 2112))
            if 4 * stile < 64:
                e("vector", lambda s, nc, stile=stile: s.tensor_tensor(
                    b.ps[:, 2048 + 4 * stile:2112],
                    b.ps[:, 2048 + 4 * stile:2112],
                    b.lmask[:, 0:64 - 4 * stile], op=ALU.add),
                  reads=psb(2048, 2112) + ["lmask"], writes=psb(2048, 2112))
            e("vector", lambda s, nc: s.tensor_reduce(
                b.stat[:, 20:21], b.ps[:, 2048:2112], axis=AX.X,
                op=ALU.max, negate=True),
              reads=psb(2048, 2112), writes=["stat"])
            e("scalar", lambda s, nc: s.activation(
                b.scrb[:, 512:576], b.ps[:, 2048:2112], AF.Exp,
                bias=b.stat[:, 20:21], scale=1.0,
                accum_out=b.stat[:, 24:25]),
              reads=psb(2048, 2112) + ["stat"], writes=["scrb", "stat"])
            e("vector", lambda s, nc: s.reciprocal(b.stat[:, 29:30],
                                                   b.stat[:, 24:25]),
              reads=["stat"], writes=["stat"])
            e("tensor", lambda s, nc: s.matmul(
                b.ps[0:64, 2560:2688], b.scrb[:, 512:576], b.ident[:],
                start=True, stop=True),
              reads=["scrb", "ident"], writes=psb(2560, 2688))
            e("vector", lambda s, nc: s.tensor_copy(b.pt[0:64, 0, 0:128],
                                                    b.ps[0:64, 2560:2688]),
              reads=psb(2560, 2688), writes=["pt0"])
            e("tensor", lambda s, nc: s.matmul(
                b.ps[:, 3072:3136], b.pt[0:64, 0, 0:128], b.pt[0:64, 1, 0:64],
                start=True, stop=True),
              reads=["pt0", "pt1"], writes=psb(3072, 3136))
            e("vector", lambda s, nc, lh=lh, stile=stile: s.tensor_scalar(
                b.bigB[:, 8192 + stile * 512 + lh * 64:
                       8192 + stile * 512 + (lh + 1) * 64],
                b.ps[:, 3072:3136], b.stat[:, 29:30], None, op0=ALU.mult),
              reads=psb(3072, 3136) + ["stat"], writes=["bigB"])
    load_wres(p, b, dram["L1b_rout"], ncol=4)
    for stile in range(NT):
        for lc in range(4):
            lo = 2048 + (lc % 2) * 512
            e("tensor", lambda s, nc, stile=stile, lc=lc, lo=lo: s.matmul(
                b.ps[:, lo:lo + 128],
                b.bigB[:, 8192 + stile * 512 + lc * 128:
                       8192 + stile * 512 + (lc + 1) * 128],
                b.ident[:], start=True, stop=True),
              reads=["bigB", "ident"], writes=psb(lo, lo + 128))
            e("scalar", lambda s, nc, lc=lc, lo=lo: s.copy(
                b.qcol[:, lc, 0:128], b.ps[:, lo:lo + 128]),
              reads=psb(lo, lo + 128), writes=["qcol"])
        for og in range(2):
            lo = 3072 + og * 512
            ks = psb(lo, lo + 512)
            for lc in range(4):
                e("tensor", lambda s, nc, lc=lc, og=og, lo=lo: s.matmul(
                    b.ps[:, lo:lo + 512], b.qcol[:, lc, 0:128],
                    b.wres[:, lc, og * 512:(og + 1) * 512],
                    start=(lc == 0), stop=(lc == 3)),
                  reads=["qcol", "wres"], writes=ks)
            for hh in range(8):
                h = og * 8 + hh
                e("vector", lambda s, nc, stile=stile, h=h, hh=hh, lo=lo:
                    s.tensor_scalar(
                    b.bigC[:, stile * 1024 + h * 64: stile * 1024 + (h + 1) * 64],
                    b.ps[:, lo + hh * 64: lo + (hh + 1) * 64],
                    b.gates[:, stile, 2 * h + 1:2 * h + 2], None, op0=ALU.mult),
                  reads=ks + ["gates"], writes=["bigC"])
    load_wres(p, b, dram["L1b_wout"])
    for qt in range(NT):
        for c in range(NC):
            lo = 2048 + (c % 2) * 512
            e("tensor", lambda s, nc, qt=qt, c=c, lo=lo: s.matmul(
                b.ps[:, lo:lo + 128],
                b.bigC[:, qt * 1024 + c * 128: qt * 1024 + (c + 1) * 128],
                b.ident[:], start=True, stop=True),
              reads=["bigC", "ident"], writes=psb(lo, lo + 128))
            e("scalar", lambda s, nc, c=c, lo=lo: s.copy(
                b.qcol[:, c, 0:128], b.ps[:, lo:lo + 128]),
              reads=psb(lo, lo + 128), writes=["qcol"])
        for og in range(2):
            lo = 3072 + og * 512
            ks = psb(lo, lo + 512)
            for dc in range(NC):
                e("tensor", lambda s, nc, dc=dc, og=og, lo=lo: s.matmul(
                    b.ps[:, lo:lo + 512], b.qcol[:, dc, 0:128],
                    b.wres[:, dc, og * 512:(og + 1) * 512],
                    start=(dc == 0), stop=(dc == NC - 1)),
                  reads=["qcol", "wres"], writes=ks)
            e("vector", lambda s, nc, qt=qt, og=og, lo=lo: s.tensor_tensor(
                b.x_res[:, qt, og * 512:(og + 1) * 512], b.ps[:, lo:lo + 512],
                b.x_res[:, qt, og * 512:(og + 1) * 512], op=ALU.add),
              reads=ks + ["x_res"], writes=["x_res"])
    emit_attention(p, b, CFG_L1B, dram, mask_ids)



def _caus(ii, jj):
    return jj <= ii


def _longf(ii, jj):
    return (np.abs(ii - jj) <= 256) | (ii == 0) | (jj == 0)


def _local(ii, jj):
    d = ii - jj
    return (d >= 0) & (d < 256)


CFG_L0A = AttnCfg("l0a", 64, "L0a_wqT", "L0a_wkT", "L0a_wvT", "L0a_woT",
                  _caus, 0.125, bq_fm="L0a_bq", bk_fm="L0a_bk",
                  bv_row="L0a_bv", bo_row="L0a_bo")
CFG_L0B = AttnCfg("l0b", 1024, "L0b_wq", "L0b_wk", "L0b_wv", "L0b_wp",
                  _caus, 1.0 / 32.0, bq_fm="L0b_bq", bk_fm="L0b_bk",
                  bv_row="L0b_bv", bo_row="L0b_bp")
CFG_L1A = AttnCfg("l1a", 64, "L1a_wq", "L1a_wk", "L1a_wv", "L1a_wo",
                  _longf, 0.125, bq_fm="L1a_bq", bk_fm="L1a_bk",
                  bv_row="L1a_bv", bo_row="L1a_bo")
CFG_L1B = AttnCfg("l1b", 64, "L1b_wq", "L1b_wk", "L1b_wv", "L1b_wout",
                  _local, 0.125, use_gates=True)

ALL_CFGS = [CFG_L0A, CFG_L0B, CFG_L1A, CFG_L1B]
MASK_TILES, MASK_IDS = collect_masks(ALL_CFGS)


def fm_pack(v):
    return np.ascontiguousarray(v.reshape(8, 128).T).astype(np.float32)


def host_arrays(inputs):
    a = {}
    t = lambda w: np.ascontiguousarray(np.asarray(w).T)
    asf = lambda w: np.ascontiguousarray(np.asarray(w), dtype=np.float32)
    a["c_ident"] = np.eye(128, dtype=BFD)
    a["c_masks"] = (np.stack(MASK_TILES, axis=1).reshape(128, -1)
                    if MASK_TILES else np.zeros((128, 128), np.float32))
    ss = np.arange(128)[:, None]
    mm = np.arange(64)[None, :]
    a["c_lmask"] = np.where(mm <= ss // 32, 0.0, NEG).astype(np.float32)
    a["L0a_wqT"] = t(inputs["L0_attn_a_Wq"])
    a["L0a_wkT"] = t(inputs["L0_attn_a_Wk"])
    a["L0a_wvT"] = t(inputs["L0_attn_a_Wv"])
    a["L0a_woT"] = t(inputs["L0_attn_a_Wo"])
    a["L0a_bq"] = fm_pack(np.asarray(inputs["L0_attn_a_bq"]))
    a["L0a_bk"] = fm_pack(np.asarray(inputs["L0_attn_a_bk"]))
    a["L0a_bv"] = asf(inputs["L0_attn_a_bv"]).reshape(1, D)
    a["L0a_bo"] = asf(inputs["L0_attn_a_bo"]).reshape(1, D)
    wat = np.asarray(inputs["L0_attn_b_w_attn"])
    bat = np.asarray(inputs["L0_attn_b_b_attn"])
    a["L0b_wq"] = np.ascontiguousarray(wat[:, 0:1024])
    a["L0b_wk"] = np.ascontiguousarray(wat[:, 1024:2048])
    a["L0b_wv"] = np.ascontiguousarray(wat[:, 2048:3072])
    a["L0b_bq"] = fm_pack(bat[0:1024])
    a["L0b_bk"] = fm_pack(bat[1024:2048])
    a["L0b_bv"] = asf(bat[2048:3072]).reshape(1, D)
    a["L0b_wp"] = asf(inputs["L0_attn_b_w_proj"])
    a["L0b_bp"] = asf(inputs["L0_attn_b_b_proj"]).reshape(1, D)
    a["L1a_wq"] = asf(inputs["L1_attn_a_Wq"])
    a["L1a_wk"] = asf(inputs["L1_attn_a_Wk"])
    a["L1a_wv"] = asf(inputs["L1_attn_a_Wv"])
    a["L1a_wo"] = asf(inputs["L1_attn_a_Wo"])
    a["L1a_bq"] = fm_pack(np.asarray(inputs["L1_attn_a_bq"]))
    a["L1a_bk"] = fm_pack(np.asarray(inputs["L1_attn_a_bk"]))
    a["L1a_bv"] = asf(inputs["L1_attn_a_bv"]).reshape(1, D)
    a["L1a_bo"] = asf(inputs["L1_attn_a_bo"]).reshape(1, D)
    a["L1b_wq"] = asf(inputs["L1_attn_b_wq"])
    a["L1b_wk"] = asf(inputs["L1_attn_b_wk"])
    a["L1b_wv"] = asf(inputs["L1_attn_b_wv"])
    a["L1b_wgq"] = asf(inputs["L1_attn_b_wgq"])
    a["L1b_wgk"] = asf(inputs["L1_attn_b_wgk"])
    a["L1b_wgv"] = asf(inputs["L1_attn_b_wgv"])
    a["L1b_rout"] = asf(inputs["L1_attn_b_r_out_w"])
    a["L1b_wout"] = asf(inputs["L1_attn_b_w_out"])
    gl = np.asarray(inputs["L1_attn_b_gate_local_w"])
    gr = np.asarray(inputs["L1_attn_b_gate_remote_w"])
    gw = np.zeros((128, 32), np.float32)
    for h in range(16):
        hb = 64 * (h % 2)
        gw[hb:hb + 64, h] = gl[h]
        gw[hb:hb + 64, 16 + h] = gr[h]
    a["L1b_gw"] = gw
    a["L0f_wfcT"] = t(inputs["L0_ffn_W_fc"])
    a["L0f_wprT"] = t(inputs["L0_ffn_W_proj"])
    a["L0f_lnw"] = np.ascontiguousarray(
        np.asarray(inputs["L0_ffn_ln_w"]).reshape(32, 128).T).astype(np.float32)
    a["L0f_lnb_fm"] = np.ascontiguousarray(
        np.asarray(inputs["L0_ffn_ln_b"]).reshape(32, 128).T).astype(np.float32)
    a["L1f_w1T"] = t(inputs["L1_ffn_w1_w"])
    a["L1f_w2T"] = t(inputs["L1_ffn_w2_w"])
    a["L1f_h1T"] = t(inputs["L1_ffn_h1_w"])
    a["L1f_h2T"] = t(inputs["L1_ffn_h2_w"])
    a["L1f_w3T"] = t(inputs["L1_ffn_w3_w"])
    a["L1f_s1"] = np.ascontiguousarray(
        np.asarray(inputs["L1_ffn_w1_s"]).reshape(32, 128).T).astype(np.float32)
    a["L1f_s2"] = np.ascontiguousarray(
        np.asarray(inputs["L1_ffn_w2_s"]).reshape(32, 128).T).astype(np.float32)
    a["L1f_sh1"] = np.ascontiguousarray(
        np.asarray(inputs["L1_ffn_h1_s"]).reshape(32, 128).T).astype(np.float32)
    a["L1f_sh2"] = np.ascontiguousarray(
        np.asarray(inputs["L1_ffn_h2_s"]).reshape(32, 128).T).astype(np.float32)
    a["L1f_s3fm"] = np.ascontiguousarray(
        np.asarray(inputs["L1_ffn_w3_s"]).reshape(8, 128).T).astype(np.float32)
    return a


DRAM_SPECS = None


def dram_specs(arrays):
    return {k: (list(v.shape),
                BF16 if v.dtype == BFD else F32)
            for k, v in arrays.items()}



SUBLAYERS = ["l0a", "l0f1", "l0b", "l0f2", "l1a", "l1f1", "l1b", "l1f2"]


def build_program(subs, arrays, dumps=()):
    nc = bass.Bass("TRN2", target_bir_lowering=False)
    dram = {}
    for k, (shape, dt) in dram_specs(arrays).items():
        dram[k] = nc.dram_tensor(k, shape, dt, kind="ExternalInput").ap()
    x_in = nc.dram_tensor("x_in", [S, D], F32, kind="ExternalInput").ap()
    x_out = nc.dram_tensor("x_out", [S, D], F32, kind="ExternalOutput").ap()
    scr = {}
    if any(s in subs for s in ("l0f1", "l0f2")):
        scr["fcq"] = nc.dram_tensor("scr_fcq", [1024, 4096], BF16,
                                    kind="Internal").ap()
        scr["prq"] = nc.dram_tensor("scr_prq", [4096, 1024], BF16,
                                    kind="Internal").ap()
        scr["l0f_vec"] = nc.dram_tensor("scr_l0f_vec", [1, 1024], BF16,
                                        kind="Internal").ap()
    if any(s in subs for s in ("l1f1", "l1f2")):
        scr["w1q"] = nc.dram_tensor("scr_w1q", [1024, 4096], BF16,
                                    kind="Internal").ap()
        scr["w2q"] = nc.dram_tensor("scr_w2q", [1024, 4096], BF16,
                                    kind="Internal").ap()
        scr["h1q"] = nc.dram_tensor("scr_h1q", [4096, 4096], BF16,
                                    kind="Internal").ap()
        scr["h2q"] = nc.dram_tensor("scr_h2q", [4096, 4096], BF16,
                                    kind="Internal").ap()
        scr["w3q"] = nc.dram_tensor("scr_w3q", [4096, 1024], BF16,
                                    kind="Internal").ap()
    dump_drams = {}
    with ExitStack() as stack:
        b = Bufs(nc, stack, len(MASK_TILES))
        p = Prog(nc)
        load_consts(p, b, dram, len(MASK_TILES))
        load_x(p, b, x_in)
        did_l0f_prep = did_l1f_prep = False
        for sub in subs:
            if sub == "l0a":
                emit_ln_xhatT(p, b)
                emit_attention(p, b, CFG_L0A, dram, MASK_IDS)
            elif sub == "l0b":
                emit_ln_xhatT(p, b)
                emit_attention(p, b, CFG_L0B, dram, MASK_IDS)
            elif sub == "l1a":
                emit_ln_xhatT(p, b)
                emit_attention(p, b, CFG_L1A, dram, MASK_IDS)
            elif sub in ("l0f1", "l0f2"):
                if not did_l0f_prep:
                    emit_l0f_prep(p, b, dram, scr)
                    did_l0f_prep = True
                emit_ln_xhatT(p, b)
                emit_l0_ffn(p, b, scr)
            elif sub in ("l1f1", "l1f2"):
                if not did_l1f_prep:
                    emit_l1f_prep(p, b, dram, scr)
                    did_l1f_prep = True
                emit_ln_xhatT(p, b)
                emit_l1_ffn(p, b, dram, scr)
            elif sub == "l1b":
                emit_l1b(p, b, dram, MASK_IDS)
        for name in dumps:
            t = getattr(b, name)
            shp = [t.shape[i] for i in range(len(t.shape))]
            dd = nc.dram_tensor("dump_" + name, shp,
                                t.dtype, kind="ExternalOutput").ap()
            dump_drams[name] = dd
            p.emit("sync", lambda s, nc, t=t, dd=dd: s.dma_start(
                out=dd[:], in_=t[:]), reads=[name],
                writes=["dump_" + name], dma=True)
        store_x(p, b, x_out)
        p.finalize()
    return nc


def run_subs(x_full, inputs, subs, dumps=(), cores=8):
    arrays = host_arrays(inputs)
    nc = build_program(subs, arrays, dumps)
    in_maps = []
    for c in range(cores):
        m = dict(arrays)
        m["x_in"] = np.ascontiguousarray(x_full[c], dtype=np.float32)
        in_maps.append(m)
    res = run_bass_kernel_spmd(nc, in_maps, core_ids=list(range(cores)))
    out = np.stack([r["x_out"] for r in res.results], axis=0)
    dump_vals = [{k: r["dump_" + k] for k in dumps} for r in res.results]
    return out, dump_vals


def kernel(**inputs):
    x = np.asarray(inputs["L0_attn_a_x"], dtype=np.float32)
    out, _ = run_subs(x, inputs, SUBLAYERS)
    return out
